# version 36
# baseline (speedup 1.0000x reference)
import numpy as np
import ml_dtypes
from contextlib import ExitStack

import concourse.bass as bass
import concourse.mybir as mybir
from concourse.bass_utils import run_bass_kernel_spmd

F32 = mybir.dt.float32
BF16 = mybir.dt.bfloat16
I32 = mybir.dt.int32
AF = mybir.ActivationFunctionType
ALU = mybir.AluOpType
AX = mybir.AxisListType

T = 2048
D = 1024
NT = T // 128
ND = D // 128
NEG = -1.0e30

BF = ml_dtypes.bfloat16


def cdiv(a, b):
    return (a + b - 1) // b


def host_constants():
    ident = np.eye(128)
    mask = np.where(np.arange(128)[None, :] > np.arange(128)[:, None], NEG, 0.0)
    return {
        "c_ident_bf": ident.astype(BF),
        "c_ident_f32": ident.astype(np.float32),
        "c_mask": mask.astype(np.float32),
        "c_ones_bf": np.ones((1, 512), dtype=BF),
    }


CONST_SHAPES = {"c_ident_bf": ([128, 128], BF16), "c_ident_f32": ([128, 128], F32),
                "c_mask": ([128, 128], F32), "c_ones_bf": ([1, 512], BF16)}


class C:

    def __init__(self, nc, stack):
        self.nc = nc
        self.stack = stack
        self.val = {}
        self.sems = {}
        for name in ["dma", "dma2", "dxa", "dxb", "doa", "dob",
                     "self_DVE", "self_ACT", "self_POOL",
                     "s0", "s1", "s2", "s3", "s4", "s5", "s6", "s7",
                     "s8", "s9", "s10", "s11", "s12", "s13",
                     "dxc", "dxd"]:
            self.sems[name] = stack.enter_context(nc.semaphore(name))
            self.val[name] = 0
        self.k = {}
        self.last_ins = {}
        self.inc_used = set()
        self._uid = 0

    def uid(self):
        self._uid += 1
        return self._uid

    def sb(self, scope, name, shape, dtype):
        return scope.enter_context(
            self.nc.sbuf_tensor(f"{name}_{self.uid()}", shape, dtype))

    def ps(self, scope, name, shape, dtype=F32):
        return scope.enter_context(
            self.nc.psum_tensor(f"{name}_{self.uid()}", shape, dtype))

    def sem(self, name):
        return self.sems[name]

    def dma_push(self, eng, dst, src, sem="dma"):
        self.val[sem] += 16
        eng.dma_start(dst, src).then_inc(self.sems[sem], 16)
        return self.val[sem]

    def base(self, *names):
        return [self.val[n] for n in names]

    def fin(self, **upd):
        for k, v in upd.items():
            self.val[k] = v

    def block(self):
        from contextlib import contextmanager

        @contextmanager
        def _mk():
            with self.nc.Block() as b:
                yield BlockProxy(self, b)
        return _mk()



SELF_SYNC = True
EMIT_OPS = {
    "tensor_copy", "tensor_tensor", "tensor_add", "tensor_sub", "tensor_mul",
    "tensor_max", "tensor_scalar", "tensor_single_scalar", "tensor_scalar_add",
    "tensor_scalar_sub", "tensor_scalar_mul", "tensor_scalar_max",
    "tensor_scalar_min", "tensor_relu", "tensor_reduce", "reduce_max",
    "reduce_sum", "scalar_tensor_tensor", "tensor_tensor_scan", "select",
    "reciprocal", "memset", "memzero", "activation", "copy", "square", "sqrt",
    "mul", "add", "sign", "iota", "partition_broadcast", "affine_select",
}


class EngWrap:
    def __init__(self, c, eng, key):
        self._c, self._e, self._k = c, eng, key

    def sig(self, sem_name, n=1):
        if SELF_SYNC:
            cnt = self._c.val[self._k]
            if cnt > 0:
                self._e.wait_ge(self._c.sems[self._k], cnt)
            self._e.sem_inc(self._c.sems[sem_name], n)
        else:
            ins = self._c.last_ins.get(self._k)
            if ins is not None and id(ins) not in self._c.inc_used:
                self._c.inc_used.add(id(ins))
                ins.then_inc(self._c.sems[sem_name], n)
            else:
                self._e.drain(fusable=True)
                self._e.sem_inc(self._c.sems[sem_name], n)

    def __getattr__(self, attr):
        f = getattr(self._e, attr)
        if attr in EMIT_OPS:
            c, k = self._c, self._k
            def wrap(*a, **kw):
                ins = f(*a, **kw)
                if SELF_SYNC:
                    n = c.val[k]
                    if n > 0:
                        ins._wait_ge(c.sems[k], n)
                    c.val[k] = n + 1
                    ins.then_inc(c.sems[k], 1)
                else:
                    c.last_ins[k] = ins
                return ins
            return wrap
        return f


class BlockProxy:
    def __init__(self, c, b):
        self._c, self._b = c, b

    def vector(self, f):
        return self._b.vector(lambda e: f(EngWrap(self._c, e, "self_DVE")))

    def scalar(self, f):
        return self._b.scalar(lambda e: f(EngWrap(self._c, e, "self_ACT")))

    def gpsimd(self, f):
        return self._b.gpsimd(lambda e: f(EngWrap(self._c, e, "self_POOL")))

    def tensor(self, f):
        return self._b.tensor(f)

    def sync(self, f):
        return self._b.sync(f)


DEFERRED_CONSTS = {"c_Abc"}


def declare_consts(c, extra=()):
    nc = c.nc
    items = [(n,) + CONST_SHAPES[n] for n in CONST_SHAPES] + list(extra)
    drams = {}
    c.kd = {}
    for n, sh, dt in items:
        drams[n] = nc.dram_tensor(n, sh, dt, kind="ExternalInput").ap()
        if n in DEFERRED_CONSTS:
            c.kd[n] = drams[n]
            continue
        c.k[n] = c.stack.enter_context(nc.sbuf_tensor(n + "_s", sh, dt))
    with c.block() as block:
        def body(sync):
            for n, sh, dt in items:
                if n in DEFERRED_CONSTS:
                    continue
                c.dma_push(sync, c.k[n][:], drams[n])
            sync.wait_ge(c.sem("dma"), c.val["dma"])
        block.sync(body)


def emit_load_natural(c, dram_ap, R, Cc, dst_bf, stage_f32):
    nc = c.nc
    nr = R // 128
    src = dram_ap.rearrange("(r p) c -> p r c", p=128)
    with c.block() as block:
        def sy(sync):
            c.dma_push(sync, stage_f32[:, 0:nr * Cc].rearrange(
                "p (r c) -> p r c", r=nr), src)
        block.sync(sy)
        dtot = c.val["dma"]

        def sc(scalar):
            scalar.wait_ge(c.sem("dma"), dtot)
            scalar.copy(dst_bf[:, 0:nr * Cc], stage_f32[:, 0:nr * Cc])
        block.scalar(sc)


def emit_load_natural_chunked(c, dram_ap, R, Cc, dst_bf, name="wnc"):
    nc = c.nc
    nr = R // 128
    src = dram_ap.rearrange("(r p) c -> p r c", p=128)
    with ExitStack() as S:
        st = [c.sb(S, f"{name}_st{i}", [128, Cc], F32) for i in range(2)]
        bA = c.val["s10"]
        dsems = ["dxa", "dxb"]
        bdx = [c.val["dxa"], c.val["dxb"]]
        dvals = [bdx[r % 2] + 16 * (r // 2 + 1) for r in range(nr)]
        with c.block() as block:
            def sy(sync):
                for r in range(nr):
                    if r >= 2:
                        sync.wait_ge(c.sem("s10"), bA + r - 1)
                    c.dma_push(sync, st[r % 2][:, :], src[:, r, :],
                               sem=dsems[r % 2])
            block.sync(sy)

            def sc(scalar):
                for r in range(nr):
                    scalar.wait_ge(c.sem(dsems[r % 2]), dvals[r])
                    scalar.copy(dst_bf[:, r * Cc:(r + 1) * Cc], st[r % 2][:, :])
                    scalar.sig("s10")
            block.scalar(sc)
        c.fin(s10=bA + nr)


def emit_transpose_natural_to_T(c, src_bf, R, Cc, dstT_bf, ps_tr):
    nc = c.nc
    nr, ncc = R // 128, Cc // 128
    bT, bV = c.base("s6", "s7")
    pairs = [(r, cc) for r in range(nr) for cc in range(ncc)]
    with c.block() as block:
        def tb(tensor):
            for k, (r, cc) in enumerate(pairs):
                if k > 0:
                    tensor.wait_ge(c.sem("s7"), bV + k)
                tensor.transpose(ps_tr[:, 0:128],
                                 src_bf[:, r * Cc + cc * 128: r * Cc + cc * 128 + 128],
                                 c.k["c_ident_bf"][:, :]).then_inc(c.sem("s6"), 1)
        block.tensor(tb)

        def vb(vector):
            for k, (r, cc) in enumerate(pairs):
                vector.wait_ge(c.sem("s6"), bT + k + 1)
                vector.tensor_copy(
                    dstT_bf[:, cc * R + r * 128: cc * R + r * 128 + 128],
                    ps_tr[:, 0:128])
                vector.sig("s7")
        block.vector(vb)
    c.fin(s6=bT + len(pairs), s7=bV + len(pairs))


def emit_row_load(c, dram_ap, n_elem, stage_row_f32, dst_row_bf=None):
    nc = c.nc
    with c.block() as block:
        def sy(sync):
            c.dma_push(sync, stage_row_f32[0:1, 0:n_elem],
                       dram_ap.rearrange("(o n) -> o n", o=1))
        block.sync(sy)
        dtot = c.val["dma"]
        if dst_row_bf is not None:
            def sc(scalar):
                scalar.wait_ge(c.sem("dma"), dtot)
                scalar.copy(dst_row_bf[0:1, 0:n_elem], stage_row_f32[0:1, 0:n_elem])
            block.scalar(sc)
        else:
            def sy2(sync):
                sync.wait_ge(c.sem("dma"), dtot)
            block.sync(sy2)


def emit_row_load_cast(c, dram_ap, n_elem, dst_bf):
    nc = c.nc
    with c.block() as block:
        def gy(gpsimd):
            c.dma_push(gpsimd, dst_bf[0:1, 0:n_elem],
                       dram_ap.rearrange("(o n) -> o n", o=1), sem="dma2")
            gpsimd.wait_ge(c.sem("dma2"), c.val["dma2"])
        block.gpsimd(gy)


def emit_col_from_row(c, row_f32, n, dst_col_f32, scale=1.0):
    nc = c.nc
    with ExitStack() as P:
        ps_col = c.ps(P, "pscol", [n, 1], F32)
        with c.block() as block:
            def tb(tensor):
                tensor.transpose(ps_col[0:n, 0:1], row_f32[0:1, 0:n],
                                 c.k["c_ident_f32"][0:1, 0:1])
            block.tensor(tb)
        with c.block() as block:
            def vb(vector):
                if scale == 1.0:
                    vector.tensor_copy(dst_col_f32[0:n, 0:1], ps_col[0:n, 0:1])
                else:
                    vector.tensor_scalar_mul(dst_col_f32[0:n, 0:1],
                                             ps_col[0:n, 0:1], scale)
            block.vector(vb)


def emit_load_T_chunked(c, dram_ap, R, Cc, dstT_bf, name="wtc", src_dtype=F32, CH=512):
    nc = c.nc
    nr, ncc = R // 128, Cc // 128
    nch = Cc // CH
    ntile = CH // 128
    src = dram_ap.rearrange("(r p) c -> p r c", p=128)
    jobs = [(r, ch) for r in range(nr) for ch in range(nch)]
    nj = len(jobs)
    with ExitStack() as S:
        st = [c.sb(S, f"{name}_st{i}", [128, CH], src_dtype) for i in range(4)]
        bf = [c.sb(S, f"{name}_bf{i}", [128, CH], BF16) for i in range(2)]
        ps_tr = [c.ps(S, f"{name}_ps{i}", [128, CH], BF16) for i in range(2)]
        bA, bT, bV = c.base("s0", "s1", "s2")
        dsems = ["dxa", "dxb", "dxc", "dxd"]
        bdx = [c.val[s] for s in dsems]
        dvals = [bdx[j % 4] + 16 * (j // 4 + 1) for j in range(nj)]
        with c.block() as block:
            def sy(sync):
                for j in range(nj):
                    r, ch = jobs[j]
                    if j >= 4:
                        sync.wait_ge(c.sem("s0"), bA + j - 3)
                    c.dma_push(sync, st[j % 4][:, :],
                               src[:, r, ch * CH:(ch + 1) * CH], sem=dsems[j % 4])
            block.sync(sy)

            def sc(scalar):
                for j in range(nj):
                    scalar.wait_ge(c.sem(dsems[j % 4]), dvals[j])
                    if j >= 2:
                        scalar.wait_ge(c.sem("s1"), bT + j - 1)
                    scalar.copy(bf[j % 2][:, :], st[j % 4][:, :])
                    scalar.sig("s0")
            block.scalar(sc)

            def tb(tensor):
                for j in range(nj):
                    tensor.wait_ge(c.sem("s0"), bA + j + 1)
                    if j >= 2:
                        tensor.wait_ge(c.sem("s2"), bV + j - 1)
                    for cc in range(ntile):
                        ins = tensor.transpose(
                            ps_tr[j % 2][:, cc * 128:cc * 128 + 128],
                            bf[j % 2][:, cc * 128:cc * 128 + 128],
                            c.k["c_ident_bf"][:, :])
                    ins.then_inc(c.sem("s1"), 1)
            block.tensor(tb)

            def vb(vector):
                for j in range(nj):
                    r, ch = jobs[j]
                    vector.wait_ge(c.sem("s1"), bT + j + 1)
                    dv = dstT_bf[:].rearrange("p (cc q) -> p cc q", cc=ncc)
                    pv = ps_tr[j % 2][:].rearrange("p (cc q) -> p cc q", cc=ntile)
                    vector.tensor_copy(
                        dv[:, ch * ntile:(ch + 1) * ntile, r * 128:r * 128 + 128],
                        pv)
                    vector.sig("s2")
            block.vector(vb)
        c.fin(s0=bA + nj, s1=bT + nj, s2=bV + nj)


def emit_bcast_row(c, row_bf, n_elem, dst_f32):
    nc = c.nc
    with ExitStack() as S:
        ps = c.ps(S, "bcrow", [128, n_elem], F32)
        with c.block() as block:
            def tb(tensor):
                for n0 in range(0, n_elem, 512):
                    nn = min(512, n_elem - n0)
                    tensor.matmul(ps[:, n0:n0 + nn],
                                  c.k["c_ones_bf"][0:1, 0:128],
                                  row_bf[0:1, n0:n0 + nn],
                                  start=True, stop=True)
            block.tensor(tb)
        with c.block() as block:
            def vb(vector):
                vector.tensor_copy(dst_f32[:, 0:n_elem], ps[:, 0:n_elem])
            block.vector(vb)


def emit_ln(c, x_dram, xnT, stats_out=None, eps=1e-5):
    nc = c.nc
    with ExitStack() as S:
        xstage = c.sb(S, "ln_xstage", [128, 2 * D], F32)
        sq = c.sb(S, "ln_sq", [128, D], F32)
        stats = c.sb(S, "ln_stats", [128, 8], F32)
        xn_bf = c.sb(S, "ln_xnbf", [128, D], BF16)
        ps_tr = c.ps(S, "ln_pstr", [128, D], BF16)
        xsrc = x_dram.rearrange("(t p) d -> p t d", p=128)

        a1, v1, a2, v2, a3, t1, v3 = c.base("s0", "s1", "s2", "s3", "s4", "s5", "s6")
        bdx = [c.val["dxa"], c.val["dxb"]]
        dsem = ["dxa", "dxb"]
        dval = [bdx[t % 2] + 16 * (t // 2 + 1) for t in range(NT)]
        with c.block() as block:
            def sy(sync):
                for t in range(NT):
                    if t >= 2:
                        sync.wait_ge(c.sem("s4"), a3 + t - 1)
                    c.dma_push(sync, xstage[:, (t % 2) * D:(t % 2) * D + D],
                               xsrc[:, t, :], sem=dsem[t % 2])
                sync.wait_ge(c.sem("dxa"), dval[NT - 2])
                sync.wait_ge(c.sem("dxb"), dval[NT - 1])
            block.sync(sy)

            def sc(scalar):
                for t in range(NT):
                    xt = xstage[:, (t % 2) * D:(t % 2) * D + D]
                    scalar.wait_ge(c.sem(dsem[t % 2]), dval[t])
                    scalar.square(sq[:], xt)
                    scalar.sig("s0")
                    scalar.wait_ge(c.sem("s1"), v1 + t + 1)
                    scalar.activation(stats[:, 2:3], stats[:, 1:2], AF.Sqrt)
                    scalar.sig("s2")
                    scalar.wait_ge(c.sem("s3"), v2 + t + 1)
                    if t > 0:
                        scalar.wait_ge(c.sem("s5"), t1 + t)
                    scalar.activation(xn_bf[:], xt, AF.Identity,
                                      bias=stats[:, 4:5], scale=stats[:, 3:4])
                    scalar.sig("s4")
            block.scalar(sc)

            def vb(vector):
                for t in range(NT):
                    xt = xstage[:, (t % 2) * D:(t % 2) * D + D]
                    vector.wait_ge(c.sem("s0"), a1 + t + 1)
                    vector.reduce_sum(stats[:, 0:1], xt, axis=AX.X)
                    vector.reduce_sum(stats[:, 5:6], sq[:], axis=AX.X)
                    vector.tensor_scalar_mul(stats[:, 6:7], stats[:, 0:1], -1.0 / D)
                    vector.tensor_mul(stats[:, 7:8], stats[:, 6:7], stats[:, 6:7])
                    vector.tensor_scalar(stats[:, 1:2], stats[:, 5:6], 1.0 / D,
                                         eps, ALU.mult, ALU.add)
                    vector.tensor_sub(stats[:, 1:2], stats[:, 1:2],
                                      stats[:, 7:8])
                    vector.sig("s1")
                    vector.wait_ge(c.sem("s2"), a2 + t + 1)
                    vector.reciprocal(stats[:, 3:4], stats[:, 2:3])
                    vector.tensor_mul(stats[:, 4:5], stats[:, 6:7],
                                      stats[:, 3:4])
                    if stats_out is not None:
                        vector.tensor_copy(stats_out[:, 2 * t:2 * t + 1],
                                           stats[:, 3:4])
                        vector.tensor_copy(stats_out[:, 2 * t + 1:2 * t + 2],
                                           stats[:, 4:5])
                    vector.sig("s3")
                    vector.wait_ge(c.sem("s5"), t1 + t + 1)
                    dst = xnT[:].rearrange("p (cc n) -> p cc n", cc=ND)
                    vector.tensor_copy(
                        dst[:, :, t * 128:t * 128 + 128],
                        ps_tr[:].rearrange("p (cc n) -> p cc n", cc=ND))
                    vector.sig("s6")
            block.vector(vb)

            def tb(tensor):
                for t in range(NT):
                    tensor.wait_ge(c.sem("s4"), a3 + t + 1)
                    if t > 0:
                        tensor.wait_ge(c.sem("s6"), v3 + t)
                    for cc in range(ND):
                        ins = tensor.transpose(ps_tr[:, cc * 128:cc * 128 + 128],
                                               xn_bf[:, cc * 128:cc * 128 + 128],
                                               c.k["c_ident_bf"][:, :])
                    ins.then_inc(c.sem("s5"), 1)
            block.tensor(tb)

        c.fin(s0=a1 + NT, s1=v1 + NT, s2=a2 + NT, s3=v2 + NT,
              s4=a3 + NT, s5=t1 + NT, s6=v3 + NT)


def emit_quant_tile(c, vector, src_f32_ap, dst_bf_ap, q):
    nblk = 32
    src3 = src_f32_ap.rearrange("p (b e) -> p b e", e=32)
    vector.tensor_reduce(q["scl"][:, 0:nblk], src3, axis=AX.X, op=ALU.max,
                         apply_absolute_value=True)
    vector.tensor_scalar(q["scl"][:, 0:nblk], q["scl"][:, 0:nblk], 1.0 / 7.5, 1e-30,
                         ALU.mult, ALU.max)
    vector.reciprocal(q["iscl"][:, 0:nblk], q["scl"][:, 0:nblk])
    iscl_bc = bass.AP(q["iscl"][:].tensor, 0, [[32, 128], [1, 32], [0, 32]])
    vector.tensor_tensor(q["qy"][:].rearrange("p (b e) -> p b e", e=32),
                         src3, iscl_bc, ALU.mult)
    vector.scalar_tensor_tensor(q["qa"][:], q["qy"][:], -1.0, q["qy"][:],
                                ALU.mult, ALU.max)
    vector.tensor_single_scalar(q["qb"][:], q["qa"][:], 2.0, ALU.is_ge)
    vector.tensor_single_scalar(q["qc"][:], q["qa"][:], 4.0, ALU.is_ge)
    vector.tensor_scalar(q["qa"][:], q["qb"][:], -4.0, 8.0, ALU.mult, ALU.add)
    vector.scalar_tensor_tensor(q["qa"][:], q["qc"][:], -2.0, q["qa"][:],
                                ALU.mult, ALU.add)
    vector.tensor_scalar(q["qb"][:], q["qb"][:], 0.125, 0.125, ALU.mult, ALU.add)
    vector.scalar_tensor_tensor(q["qb"][:], q["qc"][:], 0.25, q["qb"][:],
                                ALU.mult, ALU.add)
    vector.tensor_mul(q["qa"][:], q["qy"][:], q["qa"][:])
    vector.tensor_copy(q["qi"][:], q["qa"][:])
    vector.tensor_copy(q["qa"][:], q["qi"][:])
    vector.tensor_mul(q["qb"][:], q["qa"][:], q["qb"][:])
    scl_bc = bass.AP(q["scl"][:].tensor, 0, [[32, 128], [1, 32], [0, 32]])
    ins = vector.tensor_tensor(dst_bf_ap.rearrange("p (b e) -> p b e", e=32),
                               q["qb"][:].rearrange("p (b e) -> p b e", e=32),
                               scl_bc, ALU.mult)
    return ins


def make_quant_scratch(c, scope):
    q = {}
    for n in ["qa", "qb", "qc", "qy"]:
        q[n] = c.sb(scope, "q" + n, [128, D], F32)
    q["qi"] = c.sb(scope, "qqi", [128, D], I32)
    q["scl"] = c.sb(scope, "qscl", [128, 32], F32)
    q["iscl"] = c.sb(scope, "qiscl", [128, 32], F32)
    return q


def emit_attn_core(c, *, s_chunks, klen_fn, v_lhsT_fn, dv, sink_fn,
                   P, PT, ps_S, ps_PT, ps_PV, exp_scale=1.0, causal=True,
                   n_qt=NT):
    nc = c.nc
    b0, b1, b2, b3, b5, b6, b7 = c.base("s0", "s1", "s2", "s3",
                                        "s5", "s6", "s7")
    stat = c.attn_stat
    nbs = [cdiv(klen_fn(qt), 512) for qt in range(n_qt)]
    cnb = [0]
    cnb1 = [0]
    for qt in range(n_qt):
        cnb.append(cnb[-1] + nbs[qt])
        cnb1.append(cnb1[-1] + nbs[qt] + 1)

    with c.block() as block:
        def tb(tensor):
            for qt in range(n_qt):
                kl = klen_fn(qt)
                nb = nbs[qt]
                nkc = cdiv(kl, 128)
                if qt > 0:
                    tensor.wait_ge(c.sem("s2"), b2 + qt)
                for b in range(nb):
                    n0 = b * 512
                    nn = min(512, kl - n0)
                    for ci, (lhsT_fn, rhs_fn) in enumerate(s_chunks):
                        ins = tensor.matmul(ps_S[:, n0:n0 + nn], lhsT_fn(qt),
                                            rhs_fn(n0, nn),
                                            start=ci == 0,
                                            stop=ci == len(s_chunks) - 1)
                    ins.then_inc(c.sem("s0"), 1)
                tensor.wait_ge(c.sem("s3"), b3 + qt + 1)
                if qt > 0:
                    tensor.wait_ge(c.sem("s6"), b6 + qt)
                for kc in range(nkc):
                    ins = tensor.transpose(ps_PT[:, kc * 128:kc * 128 + 128],
                                           P[:, kc * 128:kc * 128 + 128],
                                           c.k["c_ident_bf"][:, :])
                ins.then_inc(c.sem("s5"), 1)
                tensor.wait_ge(c.sem("s6"), b6 + qt + 1)
                if qt > 0:
                    tensor.wait_ge(c.sem("s7"), b7 + qt)
                for kc in range(nkc):
                    ins = tensor.matmul(ps_PV[0:dv, 0:128], v_lhsT_fn(kc),
                                        PT[:, kc * 128:kc * 128 + 128],
                                        start=kc == 0, stop=kc == nkc - 1)
                ins.then_inc(c.sem("s0"), 1)
        block.tensor(tb)

        def vb(vector):
            for qt in range(n_qt):
                kl = klen_fn(qt)
                nb = nbs[qt]
                for b in range(nb):
                    vector.wait_ge(c.sem("s0"), b0 + cnb1[qt] + b + 1)
                    n0 = b * 512
                    nn = min(512, kl - n0)
                    if causal and b == nb - 1:
                        vector.tensor_tensor(ps_S[:, kl - 128:kl],
                                             ps_S[:, kl - 128:kl],
                                             c.k["c_mask"][:, :], ALU.add)
                    vector.reduce_max(stat[:, 4 + b:5 + b], ps_S[:, n0:n0 + nn],
                                      axis=AX.X)
                vector.reduce_max(stat[:, 0:1], stat[:, 4:4 + nb], axis=AX.X,
                                  negate=True)
                if exp_scale != 1.0:
                    vector.tensor_scalar_mul(stat[:, 3:4], stat[:, 0:1],
                                             exp_scale)
                vector.sig("s1")
                vector.wait_ge(c.sem("s2"), b2 + qt + 1)
                vector.reciprocal(stat[:, 2:3], stat[:, 1:2])
                vector.tensor_scalar_mul(P[:, 0:kl], P[:, 0:kl], stat[:, 2:3])
                vector.sig("s3")
        block.vector(vb)

        def ab(scalar):
            for qt in range(n_qt):
                kl = klen_fn(qt)
                nkc = cdiv(kl, 128)
                scalar.wait_ge(c.sem("s1"), b1 + qt + 1)
                if qt > 0:
                    scalar.wait_ge(c.sem("s5"), b5 + qt)
                bias = stat[:, 3:4] if exp_scale != 1.0 else stat[:, 0:1]
                scalar.activation(P[:, 0:kl], ps_S[:, 0:kl], AF.Exp,
                                  bias=bias, scale=exp_scale,
                                  accum_out=stat[:, 1:2])
                scalar.sig("s2")
                scalar.wait_ge(c.sem("s5"), b5 + qt + 1)
                scalar.copy(PT[:, 0:nkc * 128], ps_PT[:, 0:nkc * 128])
                scalar.sig("s6")
                scalar.wait_ge(c.sem("s0"), b0 + cnb1[qt + 1])
                sink_fn(scalar, qt, ps_PV)
                scalar.sig("s7")
        block.scalar(ab)

    c.fin(s0=b0 + cnb1[n_qt], s1=b1 + n_qt, s2=b2 + n_qt, s3=b3 + n_qt,
          s5=b5 + n_qt, s6=b6 + n_qt, s7=b7 + n_qt)




def attn_core_v2_streams(c, *, jobs, s_chunks, klen_fn, v_lhsT_fn, dv,
                         sink_fn, P2, S_sb2, ps_Sb, ps_PT, ps_PV, PT,
                         exp_scale=1.0, causal=True, wave_cols=2048,
                         pre_s_hook=None, t_hook=None, a_hook=None):
    nc = c.nc
    stat = c.attn_stat
    nj = len(jobs)
    nvp = len(ps_PV)
    nss = len(S_sb2)
    pd = len(P2)
    wkc = wave_cols // 128
    nbs = [cdiv(klen_fn(qt), 512) for (h, qt) in jobs]
    nwv = [cdiv(cdiv(klen_fn(qt), 128), wkc) for (h, qt) in jobs]
    cnb = [0]
    cwv = [0]
    for j in range(nj):
        cnb.append(cnb[-1] + nbs[j])
        cwv.append(cwv[-1] + nwv[j])
    bS, bE, bM, bX, bN, bPTT, bPTe, bPV, bPVe, bMask = c.base(
        "s0", "s1", "s2", "s3", "s4", "s5", "s6", "s7", "s8", "s9")
    ncaus = [0]
    for j in range(nj):
        ncaus.append(ncaus[-1] + (1 if causal else 0))

    def tb(tensor):
        def pt_pv(k):
            h, qt = jobs[k]
            kl = klen_fn(qt)
            nkc = cdiv(kl, 128)
            tensor.wait_ge(c.sem("s4"), bN + k + 1)
            for wv in range(nwv[k]):
                if bPTe + cwv[k] + wv > 0:
                    tensor.wait_ge(c.sem("s6"), bPTe + cwv[k] + wv)
                for kc in range(wv * wkc, min(nkc, (wv + 1) * wkc)):
                    ins = tensor.transpose(
                        ps_PT[:, (kc - wv * wkc) * 128:
                              (kc - wv * wkc) * 128 + 128],
                        P2[k % pd][:, kc * 128:kc * 128 + 128],
                        c.k["c_ident_bf"][:, :])
                ins.then_inc(c.sem("s5"), 1)
            tensor.wait_ge(c.sem("s6"), bPTe + cwv[k + 1])
            if bPVe + k - nvp + 1 > 0:
                tensor.wait_ge(c.sem("s8"), bPVe + k - nvp + 1)
            for kc in range(nkc):
                ins = tensor.matmul(ps_PV[k % nvp][0:dv, 0:128],
                                    v_lhsT_fn(h, kc),
                                    PT[:, kc * 128:kc * 128 + 128],
                                    start=kc == 0, stop=kc == nkc - 1)
            ins.then_inc(c.sem("s7"), 1)

        for j, (h, qt) in enumerate(jobs):
            kl = klen_fn(qt)
            chunks = s_chunks(h, qt)
            if pre_s_hook is not None:
                pre_s_hook(tensor, j)
            for b in range(nbs[j]):
                n0 = b * 512
                nn = min(512, kl - n0)
                if b == 0:
                    if nss == 0:
                        if bX + j > 0:
                            tensor.wait_ge(c.sem("s3"), bX + j)
                    elif j > 0:
                        tensor.wait_ge(c.sem("s1"), bE + j)
                for ci, (lhsT_ap, rhs_fn) in enumerate(chunks):
                    ins = tensor.matmul(ps_Sb[:, n0:n0 + nn], lhsT_ap,
                                        rhs_fn(n0, nn),
                                        start=ci == 0,
                                        stop=ci == len(chunks) - 1)
                ins.then_inc(c.sem("s0"), 1)
            if j >= 2:
                pt_pv(j - 2)
            if t_hook is not None:
                t_hook(tensor, j)
        if nj >= 2:
            pt_pv(nj - 2)
        pt_pv(nj - 1)

    def ab(scalar):
        def evac(k):
            kl = klen_fn(jobs[k][1])
            scalar.wait_ge(c.sem("s0"), bS + cnb[k + 1])
            if causal:
                scalar.wait_ge(c.sem("s9"), bMask + ncaus[k + 1])
            scalar.copy(S_sb2[k % nss][:, 0:kl], ps_Sb[:, 0:kl])
            scalar.sig("s1")

        def sink(k):
            hp, qtp = jobs[k]
            scalar.wait_ge(c.sem("s7"), bPV + k + 1)
            sink_fn(scalar, hp, qtp, ps_PV[k % nvp])
            scalar.sig("s8")

        for j, (h, qt) in enumerate(jobs):
            kl = klen_fn(qt)
            p = j % 2
            if j == 0 and nss > 0:
                evac(0)
            if nss > 1 and j + 1 < nj:
                evac(j + 1)
            scalar.wait_ge(c.sem("s2"), bM + j + 1)
            if j >= pd + 1:
                scalar.wait_ge(c.sem("s5"), bPTT + cwv[j - pd + 1])
            bias = stat[:, 2 + p:3 + p] if exp_scale != 1.0 \
                else stat[:, 0 + p:1 + p]
            esrc = ps_Sb[:, 0:kl] if nss == 0 else S_sb2[j % nss][:, 0:kl]
            scalar.activation(P2[j % pd][:, 0:kl], esrc, AF.Exp,
                              bias=bias, scale=exp_scale,
                              accum_out=stat[:, 6 + p:7 + p])
            scalar.sig("s3")
            if j >= 2:
                sink(j - 2)
            if a_hook is not None:
                a_hook(scalar, j)
            if nss == 1 and j + 1 < nj:
                evac(j + 1)
        if nj >= 2:
            sink(nj - 2)
        sink(nj - 1)

    def vb(vector):
        def norm(k):
            klp = klen_fn(jobs[k][1])
            pp = k % 2
            vector.wait_ge(c.sem("s3"), bX + k + 1)
            vector.reciprocal(stat[:, 8 + pp:9 + pp],
                              stat[:, 6 + pp:7 + pp])
            vector.tensor_scalar_mul(P2[k % pd][:, 0:klp],
                                     P2[k % pd][:, 0:klp],
                                     stat[:, 8 + pp:9 + pp])
            vector.sig("s4")

        def ptevac(k):
            nkcp = cdiv(klen_fn(jobs[k][1]), 128)
            for wv in range(nwv[k]):
                ncc = min(nkcp, (wv + 1) * wkc) - wv * wkc
                vector.wait_ge(c.sem("s5"), bPTT + cwv[k] + wv + 1)
                vector.tensor_copy(
                    PT[:, wv * wave_cols:wv * wave_cols + ncc * 128],
                    ps_PT[:, 0:ncc * 128])
                vector.sig("s6")

        for j, (h, qt) in enumerate(jobs):
            kl = klen_fn(qt)
            p = j % 2
            if causal:
                vector.wait_ge(c.sem("s0"), bS + cnb[j + 1])
                vector.tensor_tensor(ps_Sb[:, kl - 128:kl],
                                     ps_Sb[:, kl - 128:kl],
                                     c.k["c_mask"][:, :], ALU.add)
                vector.sig("s9")
            if j >= 2:
                ptevac(j - 2)
            if j > 0:
                norm(j - 1)
            if nss == 0:
                if not causal:
                    vector.wait_ge(c.sem("s0"), bS + cnb[j + 1])
                vector.reduce_max(stat[:, 0 + p:1 + p], ps_Sb[:, 0:kl],
                                  axis=AX.X, negate=True)
            else:
                vector.wait_ge(c.sem("s1"), bE + j + 1)
                vector.reduce_max(stat[:, 0 + p:1 + p],
                                  S_sb2[j % nss][:, 0:kl],
                                  axis=AX.X, negate=True)
            if exp_scale != 1.0:
                vector.tensor_scalar_mul(stat[:, 2 + p:3 + p],
                                         stat[:, 0 + p:1 + p], exp_scale)
            vector.sig("s2")
        norm(nj - 1)
        if nj >= 2:
            ptevac(nj - 2)
        ptevac(nj - 1)

    def fin():
        c.fin(s0=bS + cnb[nj], s1=bE + (nj if nss > 0 else 0), s2=bM + nj,
              s3=bX + nj,
              s4=bN + nj, s5=bPTT + cwv[nj], s6=bPTe + cwv[nj], s7=bPV + nj,
              s9=bMask + ncaus[nj])
        c.val["s8"] = bPVe + nj

    return tb, ab, vb, fin


def emit_attn_core_v2(c, **kw):
    tb, ab, vb, fin = attn_core_v2_streams(c, **kw)
    with c.block() as block:
        block.tensor(tb)
        block.scalar(ab)
        block.vector(vb)
    fin()


def emit_combine(c, xin, xout, emit_s_psum, g_all, s_sbuf=None):
    nc = c.nc
    with ExitStack() as S:
        xst = c.sb(S, "cmb_xst", [128, 4 * D], F32)
        ost = c.sb(S, "cmb_ost", [128, 2 * D], F32)
        pbf = c.sb(S, "cmb_pbf", [128, D], BF16)
        ps_o = c.ps(S, "cmb_pso", [128, 2 * D], F32)
        xsrc = xin.rearrange("(t p) d -> p t d", p=128)
        odst = xout.rearrange("(t p) d -> p t d", p=128)
        bT, bV = c.base("s0", "s1")
        xsem = ["dxa", "dxb", "dxc", "dxd"]
        bdx = [c.val[s] for s in xsem]
        bdo = [c.val["doa"], c.val["dob"]]
        osem = ["doa", "dob"]
        dma_x = [bdx[t % 4] + 16 * (t // 4 + 1) for t in range(NT)]
        dma_o = [bdo[t % 2] + 16 * (t // 2 + 1) for t in range(NT)]
        with c.block() as block:
            def sy(sync):
                for t in range(NT):
                    if t >= 4:
                        sync.wait_ge(c.sem("s1"), bV + t - 3)
                    c.dma_push(sync, xst[:, (t % 4) * D:(t % 4) * D + D],
                               xsrc[:, t, :], sem=xsem[t % 4])
                for qi in range(4):
                    sync.wait_ge(c.sem(xsem[qi]), dma_x[NT - 4 + qi])
            block.sync(sy)

            def gy(gpsimd):
                for t in range(NT):
                    gpsimd.wait_ge(c.sem("s1"), bV + t + 1)
                    c.dma_push(gpsimd, odst[:, t, :],
                               ost[:, (t % 2) * D:(t % 2) * D + D], sem=osem[t % 2])
                gpsimd.wait_ge(c.sem("doa"), dma_o[NT - 2])
                gpsimd.wait_ge(c.sem("dob"), dma_o[NT - 1])
            block.gpsimd(gy)

            def tb(tensor):
                if emit_s_psum is None:
                    return
                for t in range(NT):
                    buf = (t % 2) * D
                    if t >= 2:
                        tensor.wait_ge(c.sem("s1"), bV + t - 1)
                    emit_s_psum(tensor, t, ps_o[:, buf:buf + D]).then_inc(
                        c.sem("s0"), 1)
            block.tensor(tb)

            def vb(vector):
                for t in range(NT):
                    buf = (t % 2) * D
                    if emit_s_psum is not None:
                        vector.wait_ge(c.sem("s0"), bT + t + 1)
                        vector.tensor_mul(pbf[:], ps_o[:, buf:buf + D],
                                          g_all[:, t * D:t * D + D])
                    else:
                        vector.tensor_mul(pbf[:], s_sbuf[:, t * D:t * D + D],
                                          g_all[:, t * D:t * D + D])
                    vector.wait_ge(c.sem(xsem[t % 4]), dma_x[t])
                    if t >= 2:
                        vector.wait_ge(c.sem(osem[t % 2]), dma_o[t - 2])
                    vector.tensor_add(ost[:, buf:buf + D],
                                      xst[:, (t % 4) * D:(t % 4) * D + D],
                                      pbf[:])
                    vector.sig("s1")
            block.vector(vb)
        c.fin(s1=bV + NT)
        if emit_s_psum is not None:
            c.fin(s0=bT + NT)


def emit_combine_ln(c, xin, xout, emit_s_psum, g_all, xnT_next,
                    stats_next=None, s_sbuf=None, eps=1e-5):
    nc = c.nc
    with ExitStack() as S:
        xst = c.sb(S, "cmb_xst", [128, 4 * D], F32)
        ost = c.sb(S, "cmb_ost", [128, 2 * D], F32)
        pbf = c.sb(S, "cmb_pbf", [128, D], BF16)
        sq = c.sb(S, "cln_sq", [128, D], F32)
        stats = c.sb(S, "cln_stats", [128, 8], F32)
        xn_bf = c.sb(S, "cln_xnbf", [128, D], BF16)
        ps_o = c.ps(S, "cmb_pso", [128, 2 * D], F32)
        ps_tr = c.ps(S, "cln_pstr", [128, D], BF16)
        xsrc = xin.rearrange("(t p) d -> p t d", p=128)
        odst = xout.rearrange("(t p) d -> p t d", p=128)
        bT, bV, bA1, bV1, bA2, bV2, bA3, bT2, bV3 = c.base(
            "s0", "s1", "s2", "s3", "s4", "s5", "s6", "s7", "s9")
        xsem = ["dxa", "dxb", "dxc", "dxd"]
        bdx = [c.val[s] for s in xsem]
        bdo = [c.val["doa"], c.val["dob"]]
        osem = ["doa", "dob"]
        dma_x = [bdx[t % 4] + 16 * (t // 4 + 1) for t in range(NT)]
        dma_o = [bdo[t % 2] + 16 * (t // 2 + 1) for t in range(NT)]
        with c.block() as block:
            def sy(sync):
                for t in range(NT):
                    if t >= 4:
                        sync.wait_ge(c.sem("s1"), bV + t - 3)
                    c.dma_push(sync, xst[:, (t % 4) * D:(t % 4) * D + D],
                               xsrc[:, t, :], sem=xsem[t % 4])
                for qi in range(4):
                    sync.wait_ge(c.sem(xsem[qi]), dma_x[NT - 4 + qi])
            block.sync(sy)

            def gy(gpsimd):
                for t in range(NT):
                    gpsimd.wait_ge(c.sem("s1"), bV + t + 1)
                    c.dma_push(gpsimd, odst[:, t, :],
                               ost[:, (t % 2) * D:(t % 2) * D + D],
                               sem=osem[t % 2])
                gpsimd.wait_ge(c.sem("doa"), dma_o[NT - 2])
                gpsimd.wait_ge(c.sem("dob"), dma_o[NT - 1])
            block.gpsimd(gy)

            def tb(tensor):
                def transp(t):
                    tensor.wait_ge(c.sem("s6"), bA3 + t + 1)
                    if t >= 1:
                        tensor.wait_ge(c.sem("s9"), bV3 + t)
                    for cc in range(ND):
                        ins = tensor.transpose(
                            ps_tr[:, cc * 128:cc * 128 + 128],
                            xn_bf[:, cc * 128:cc * 128 + 128],
                            c.k["c_ident_bf"][:, :])
                    ins.then_inc(c.sem("s7"), 1)

                for t in range(NT):
                    if emit_s_psum is not None:
                        buf = (t % 2) * D
                        if t >= 2:
                            tensor.wait_ge(c.sem("s1"), bV + t - 1)
                        emit_s_psum(tensor, t, ps_o[:, buf:buf + D]).then_inc(
                            c.sem("s0"), 1)
                    if t >= 1:
                        transp(t - 1)
                transp(NT - 1)
            block.tensor(tb)

            def ab(scalar):
                for t in range(NT):
                    xt = ost[:, (t % 2) * D:(t % 2) * D + D]
                    scalar.wait_ge(c.sem("s1"), bV + t + 1)
                    scalar.square(sq[:], xt)
                    scalar.sig("s2")
                    scalar.wait_ge(c.sem("s3"), bV1 + t + 1)
                    scalar.activation(stats[:, 2:3], stats[:, 1:2], AF.Sqrt)
                    scalar.sig("s4")
                    scalar.wait_ge(c.sem("s5"), bV2 + t + 1)
                    if t > 0:
                        scalar.wait_ge(c.sem("s7"), bT2 + t)
                    scalar.activation(xn_bf[:], xt, AF.Identity,
                                      bias=stats[:, 4:5], scale=stats[:, 3:4])
                    scalar.sig("s6")
            block.scalar(ab)

            def vb(vector):
                def evac(t):
                    vector.wait_ge(c.sem("s7"), bT2 + t + 1)
                    dst = xnT_next[:].rearrange("p (cc n) -> p cc n", cc=ND)
                    vector.tensor_copy(
                        dst[:, :, t * 128:t * 128 + 128],
                        ps_tr[:].rearrange("p (cc n) -> p cc n", cc=ND))
                    vector.sig("s9")

                for t in range(NT):
                    buf = (t % 2) * D
                    if emit_s_psum is not None:
                        vector.wait_ge(c.sem("s0"), bT + t + 1)
                        vector.tensor_mul(pbf[:], ps_o[:, buf:buf + D],
                                          g_all[:, t * D:t * D + D])
                    else:
                        vector.tensor_mul(pbf[:], s_sbuf[:, t * D:t * D + D],
                                          g_all[:, t * D:t * D + D])
                    vector.wait_ge(c.sem(xsem[t % 4]), dma_x[t])
                    if t >= 2:
                        vector.wait_ge(c.sem(osem[t % 2]), dma_o[t - 2])
                        vector.wait_ge(c.sem("s6"), bA3 + t - 1)
                    vector.tensor_add(ost[:, buf:buf + D],
                                      xst[:, (t % 4) * D:(t % 4) * D + D],
                                      pbf[:])
                    vector.sig("s1")
                    vector.wait_ge(c.sem("s2"), bA1 + t + 1)
                    xt = ost[:, buf:buf + D]
                    vector.reduce_sum(stats[:, 0:1], xt, axis=AX.X)
                    vector.reduce_sum(stats[:, 5:6], sq[:], axis=AX.X)
                    vector.tensor_scalar_mul(stats[:, 6:7], stats[:, 0:1],
                                             -1.0 / D)
                    vector.tensor_mul(stats[:, 7:8], stats[:, 6:7],
                                      stats[:, 6:7])
                    vector.tensor_scalar(stats[:, 1:2], stats[:, 5:6], 1.0 / D,
                                         eps, ALU.mult, ALU.add)
                    vector.tensor_sub(stats[:, 1:2], stats[:, 1:2],
                                      stats[:, 7:8])
                    vector.sig("s3")
                    vector.wait_ge(c.sem("s4"), bA2 + t + 1)
                    vector.reciprocal(stats[:, 3:4], stats[:, 2:3])
                    vector.tensor_mul(stats[:, 4:5], stats[:, 6:7],
                                      stats[:, 3:4])
                    if stats_next is not None:
                        vector.tensor_copy(stats_next[:, 2 * t:2 * t + 1],
                                           stats[:, 3:4])
                        vector.tensor_copy(stats_next[:, 2 * t + 1:2 * t + 2],
                                           stats[:, 4:5])
                    vector.sig("s5")
                    if t >= 1:
                        evac(t - 1)
                evac(NT - 1)
            block.vector(vb)
        c.fin(s1=bV + NT, s2=bA1 + NT, s3=bV1 + NT, s4=bA2 + NT, s5=bV2 + NT,
              s6=bA3 + NT, s7=bT2 + NT, s9=bV3 + NT)
        if emit_s_psum is not None:
            c.fin(s0=bT + NT)


def emit_renorm_pipeline(c, S, xin, stats, consumer):
    raise NotImplementedError


def emit_layer0(c, xin, xout, w, fuse_next=None):
    nc = c.nc
    with ExitStack() as L:
        xnT = c.sb(L, "l0_xnT", [128, ND * T], BF16)
        lnstats = c.sb(L, "l0_lnst", [128, 2 * NT], F32)
        g0_dram = nc.dram_tensor(f"l0_gspill_{c.uid()}", [128, NT * D], BF16,
                                 kind="Internal").ap()
        rstage = c.sb(L, "l0_rstage", [1, D], F32)

        emit_ln(c, xin, xnT, stats_out=lnstats)

        with ExitStack() as Sf:
            g_all = c.sb(Sf, "l0_g", [128, NT * D], BF16)
            xdT = c.sb(Sf, "l0_xdT", [128, ND * T], BF16)
            wdT = c.sb(Sf, "l0_wdT", [128, ND * D], BF16)
            fb_row = c.sb(Sf, "l0_fbrow", [1, D], BF16)
            with ExitStack() as Sq:
                q = make_quant_scratch(c, Sq)
                xstage = c.sb(Sq, "l0_qxst", [128, 2 * D], F32)
                wstage = c.sb(Sq, "l0_qwst", [128, 2 * D], F32)
                xnf = c.sb(Sq, "l0_qxnf", [128, D], F32)
                xd_bf = c.sb(Sq, "l0_xdbf", [128, D], BF16)
                wd_bf = c.sb(Sq, "l0_wdbf", [128, D], BF16)
                ps_tr = c.ps(Sq, "l0_pstr", [128, D], BF16)
                ps_trw = c.ps(Sq, "l0_pstrw", [128, D], BF16)
                xsrc = xin.rearrange("(t p) d -> p t d", p=128)
                fsrc = w["L0_ffn_weight"].rearrange("(r p) c -> p r c", p=128)

                bA, bV, bT, bE, bWT, bWE, bWQ = c.base(
                    "s0", "s1", "s2", "s3", "s5", "s6", "s9")
                bdx = [c.val["dxa"], c.val["dxb"]]
                dsem = ["dxa", "dxb"]
                dval = [bdx[t % 2] + 16 * (t // 2 + 1) for t in range(NT)]
                bdw = [c.val["dxc"], c.val["dxd"]]
                dsemw = ["dxc", "dxd"]
                dvalw = [bdw[r % 2] + 16 * (r // 2 + 1) for r in range(ND)]
                with c.block() as block:
                    def sy(sync):
                        for r in range(2):
                            c.dma_push(sync, wstage[:, r * D:r * D + D],
                                       fsrc[:, r, :], sem=dsemw[r % 2])
                        for t in range(NT):
                            if t >= 2:
                                sync.wait_ge(c.sem("s0"), bA + t - 1)
                            c.dma_push(sync, xstage[:, (t % 2) * D:(t % 2) * D + D],
                                       xsrc[:, t, :], sem=dsem[t % 2])
                            if t >= 2 and t % 2 == 0 and t // 2 + 1 < ND:
                                r = t // 2 + 1
                                sync.wait_ge(c.sem("s9"), bWQ + r - 1)
                                c.dma_push(sync, wstage[:, (r % 2) * D:
                                                        (r % 2) * D + D],
                                           fsrc[:, r, :], sem=dsemw[r % 2])
                        sync.wait_ge(c.sem("dxa"), dval[NT - 2])
                        sync.wait_ge(c.sem("dxb"), dval[NT - 1])
                        sync.wait_ge(c.sem("dxc"), dvalw[ND - 2])
                        sync.wait_ge(c.sem("dxd"), dvalw[ND - 1])
                    block.sync(sy)

                    def ab(scalar):
                        for t in range(NT):
                            scalar.wait_ge(c.sem(dsem[t % 2]), dval[t])
                            if t > 0:
                                scalar.wait_ge(c.sem("s1"), bV + t)
                            scalar.activation(
                                xnf[:], xstage[:, (t % 2) * D:(t % 2) * D + D],
                                AF.Identity, bias=lnstats[:, 2 * t + 1:2 * t + 2],
                                scale=lnstats[:, 2 * t:2 * t + 1])
                            scalar.sig("s0")
                    block.scalar(ab)

                    def vb(vector):
                        for t in range(NT):
                            if t % 2 == 0 and t // 2 < ND:
                                r = t // 2
                                vector.wait_ge(c.sem(dsemw[r % 2]), dvalw[r])
                                if r > 0:
                                    vector.wait_ge(c.sem("s5"), bWT + r)
                                emit_quant_tile(
                                    c, vector,
                                    wstage[:, (r % 2) * D:(r % 2) * D + D],
                                    wd_bf[:], q)
                                vector.sig("s9")
                            vector.wait_ge(c.sem("s0"), bA + t + 1)
                            if t > 0:
                                vector.wait_ge(c.sem("s2"), bT + t)
                            emit_quant_tile(c, vector, xnf[:], xd_bf[:], q)
                            vector.sig("s1")
                            vector.wait_ge(c.sem("s2"), bT + t + 1)
                            dst = xdT[:].rearrange("p (cc n) -> p cc n", cc=ND)
                            vector.tensor_copy(
                                dst[:, :, t * 128:t * 128 + 128],
                                ps_tr[:].rearrange("p (cc n) -> p cc n", cc=ND))
                            vector.sig("s3")
                            if t % 2 == 1 and t // 2 < ND:
                                r = t // 2
                                vector.wait_ge(c.sem("s5"), bWT + r + 1)
                                dstw = wdT[:].rearrange("p (cc n) -> p cc n",
                                                        cc=ND)
                                vector.tensor_copy(
                                    dstw[:, :, r * 128:r * 128 + 128],
                                    ps_trw[:].rearrange("p (cc n) -> p cc n",
                                                        cc=ND))
                                vector.sig("s6")
                    block.vector(vb)

                    def tb(tensor):
                        for t in range(NT):
                            tensor.wait_ge(c.sem("s1"), bV + t + 1)
                            if t > 0:
                                tensor.wait_ge(c.sem("s3"), bE + t)
                            for cc in range(ND):
                                ins = tensor.transpose(
                                    ps_tr[:, cc * 128:cc * 128 + 128],
                                    xd_bf[:, cc * 128:cc * 128 + 128],
                                    c.k["c_ident_bf"][:, :])
                            ins.then_inc(c.sem("s2"), 1)
                            if t % 2 == 1 and t // 2 < ND:
                                r = t // 2
                                tensor.wait_ge(c.sem("s9"), bWQ + r + 1)
                                if r > 0:
                                    tensor.wait_ge(c.sem("s6"), bWE + r)
                                for cc in range(ND):
                                    ins = tensor.transpose(
                                        ps_trw[:, cc * 128:cc * 128 + 128],
                                        wd_bf[:, cc * 128:cc * 128 + 128],
                                        c.k["c_ident_bf"][:, :])
                                ins.then_inc(c.sem("s5"), 1)
                    block.tensor(tb)
                c.fin(s0=bA + NT, s1=bV + NT, s2=bT + NT, s3=bE + NT,
                      s5=bWT + ND, s6=bWE + ND, s9=bWQ + ND)

            emit_row_load(c, w["L0_ffn_bias"], D, rstage, fb_row)

            with ExitStack() as Sp:
                ps_f = c.ps(Sp, "l0_psf", [128, 2 * D], F32)
                bT, bA = c.base("s0", "s1")
                with c.block() as block:
                    def tb(tensor):
                        for t in range(NT):
                            buf = (t % 2) * D
                            if t >= 2:
                                tensor.wait_ge(c.sem("s1"), bA + t - 1)
                            for h in range(2):
                                for cc in range(ND):
                                    tensor.matmul(
                                        ps_f[:, buf + h * 512: buf + h * 512 + 512],
                                        xdT[:, cc * T + t * 128: cc * T + t * 128 + 128],
                                        wdT[:, cc * D + h * 512: cc * D + h * 512 + 512],
                                        start=cc == 0, stop=False)
                                ins = tensor.matmul(
                                    ps_f[:, buf + h * 512: buf + h * 512 + 512],
                                    c.k["c_ones_bf"][0:1, 0:128],
                                    fb_row[0:1, h * 512: h * 512 + 512],
                                    start=False, stop=True)
                            ins.then_inc(c.sem("s0"), 1)
                    block.tensor(tb)

                    def ab(scalar):
                        for t in range(NT):
                            buf = (t % 2) * D
                            scalar.wait_ge(c.sem("s0"), bT + t + 1)
                            scalar.activation(g_all[:, t * D:t * D + D],
                                              ps_f[:, buf:buf + D], AF.Silu)
                            scalar.sig("s1")
                    block.scalar(ab)
                c.fin(s0=bT + NT, s1=bA + NT)
            with c.block() as block:
                def sy(sync):
                    c.dma_push(sync, g0_dram, g_all[:])
                    sync.wait_ge(c.sem("dma"), c.val["dma"])
                block.sync(sy)

        ctxT = c.sb(L, "l0_ctxT", [128, 8 * T], BF16)
        with ExitStack() as Sa:
            p_tok = c.sb(Sa, "l0_ptok", [128, NT * 40], F32)
            pm_tok = c.sb(Sa, "l0_pmtok", [128, NT * 40], F32)
            bp_col = c.sb(Sa, "l0_bpcol", [40, 1], F32)
            with ExitStack() as Sw:
                wp_st = c.sb(Sw, "l0_wpst", [128, ND * 40], F32)
                wp_bf = c.sb(Sw, "l0_wpbf", [128, ND * 40], BF16)
                pT = c.sb(Sw, "l0_pT", [40, T], F32)
                emit_load_natural(c, w["L0_attn_Wp"], D, 40, wp_bf, wp_st)
                emit_row_load(c, w["L0_attn_bp"], 40, rstage)
                emit_col_from_row(c, rstage, 40, bp_col)
                with ExitStack() as Sp:
                    ps_p = c.ps(Sp, "l0_psp", [40, T], F32)
                    with c.block() as block:
                        def tb(tensor):
                            for n0 in range(0, T, 512):
                                for cc in range(ND):
                                    tensor.matmul(
                                        ps_p[:, n0:n0 + 512],
                                        wp_bf[:, cc * 40: cc * 40 + 40],
                                        xnT[:, cc * T + n0: cc * T + n0 + 512],
                                        start=cc == 0, stop=cc == ND - 1)
                        block.tensor(tb)
                    with c.block() as block:
                        def ab(scalar):
                            scalar.activation(pT[:, :], ps_p[:, :], AF.Sigmoid,
                                              bias=bp_col[:, 0:1])
                        block.scalar(ab)
                with ExitStack() as Sp:
                    ps_t = c.ps(Sp, "l0_pspt", [128, NT * 64], F32)
                    with c.block() as block:
                        def tb(tensor):
                            for t in range(NT):
                                tensor.transpose(ps_t[:, t * 64:t * 64 + 40],
                                                 pT[:, t * 128:t * 128 + 128],
                                                 c.k["c_ident_f32"][0:40, 0:40])
                        block.tensor(tb)
                    with c.block() as block:
                        def vb(vector):
                            pv = ps_t[:].rearrange("p (t e) -> p t e", e=64)
                            vector.tensor_copy(
                                p_tok[:].rearrange("p (t e) -> p t e", e=40),
                                pv[:, :, 0:40])
                            vector.tensor_scalar(
                                pm_tok[:].rearrange("p (t e) -> p t e", e=40),
                                pv[:, :, 0:40], -1.0, 1.0, ALU.mult, ALU.add)
                        block.vector(vb)

            with ExitStack() as Score:
                PhiT = c.sb(Score, "l0_PhiT", [62, 8 * T], BF16)
                phi = c.sb(Score, "l0_phi", [128, 8 * 62], F32)
                phi_bf = c.sb(Score, "l0_phibf", [128, 8 * 62], BF16)
                V_all = c.sb(Score, "l0_Vall", [128, 8 * T], BF16)
                FB = [0, 2, 6, 14, 30]
                with ExitStack() as Sp:
                    ps_phi = c.ps(Sp, "l0_psphi", [62, 8 * 128], BF16)
                    bV, bT2, bA2 = c.base("s0", "s1", "s2")
                    phi_h = phi[:].tensor
                    ptok_h = p_tok[:].tensor
                    pmtok_h = pm_tok[:].tensor
                    with c.block() as block:
                        def vb(vector):
                            for t in range(NT):
                                if t > 0:
                                    vector.wait_ge(c.sem("s1"), bT2 + t)
                                for j in range(5):
                                    nb = 1 << j
                                    pc = bass.AP(ptok_h, t * 40 + j,
                                                 [[NT * 40, 128], [5, 8], [0, nb]])
                                    pm = bass.AP(pmtok_h, t * 40 + j,
                                                 [[NT * 40, 128], [5, 8], [0, nb]])
                                    if j == 0:
                                        vector.tensor_copy(
                                            bass.AP(phi_h, 0,
                                                    [[496, 128], [62, 8], [1, 1]]),
                                            bass.AP(pmtok_h, t * 40,
                                                    [[NT * 40, 128], [5, 8], [1, 1]]))
                                        vector.tensor_copy(
                                            bass.AP(phi_h, 1,
                                                    [[496, 128], [62, 8], [1, 1]]),
                                            bass.AP(ptok_h, t * 40,
                                                    [[NT * 40, 128], [5, 8], [1, 1]]))
                                    else:
                                        par = bass.AP(phi_h, FB[j - 1],
                                                      [[496, 128], [62, 8], [1, nb]])
                                        out0 = bass.AP(phi_h, FB[j],
                                                       [[496, 128], [62, 8], [1, nb]])
                                        out1 = bass.AP(phi_h, FB[j] + nb,
                                                       [[496, 128], [62, 8], [1, nb]])
                                        vector.tensor_tensor(out0, par, pm, ALU.mult)
                                        vector.tensor_tensor(out1, par, pc, ALU.mult)
                                vector.tensor_copy(phi_bf[:], phi[:])
                                vector.sig("s0")
                        block.vector(vb)

                        def tb(tensor):
                            for t in range(NT):
                                tensor.wait_ge(c.sem("s0"), bV + t + 1)
                                if t > 0:
                                    tensor.wait_ge(c.sem("s2"), bA2 + t)
                                for hh in range(8):
                                    ins = tensor.transpose(
                                        ps_phi[:, hh * 128:hh * 128 + 128],
                                        phi_bf[:, hh * 62:hh * 62 + 62],
                                        c.k["c_ident_bf"][:, :])
                                ins.then_inc(c.sem("s1"), 1)
                        block.tensor(tb)

                        def ab(scalar):
                            for t in range(NT):
                                scalar.wait_ge(c.sem("s1"), bT2 + t + 1)
                                dst = PhiT[:].rearrange("p (hh n) -> p hh n", hh=8)
                                pv = ps_phi[:].rearrange("p (hh n) -> p hh n", hh=8)
                                scalar.copy(dst[:, :, t * 128:t * 128 + 128], pv)
                                scalar.sig("s2")
                        block.scalar(ab)
                    c.fin(s0=bV + NT, s1=bT2 + NT, s2=bA2 + NT)

                with ExitStack() as Sw:
                    wv_bf = c.sb(Sw, "l0_wvbf", [128, ND * D], BF16)
                    emit_load_natural_chunked(c, w["L0_attn_Wv"], D, D, wv_bf,
                                              "l0wv")
                    with ExitStack() as Sp:
                        ps_v = [c.ps(Sp, f"l0_psv{i}", [128, 128], F32)
                                for i in range(2)]
                        bT2, bA2 = c.base("s1", "s2")
                        jobsv = [(h, t) for h in range(8) for t in range(NT)]
                        with c.block() as block:
                            def tb(tensor):
                                for jv, (h, t) in enumerate(jobsv):
                                    if jv >= 2:
                                        tensor.wait_ge(c.sem("s2"), bA2 + jv - 1)
                                    for cc in range(ND):
                                        ins = tensor.matmul(
                                            ps_v[jv % 2][:, :],
                                            xnT[:, cc * T + t * 128: cc * T + t * 128 + 128],
                                            wv_bf[:, cc * D + h * 128: cc * D + h * 128 + 128],
                                            start=cc == 0, stop=cc == ND - 1)
                                    ins.then_inc(c.sem("s1"), 1)
                            block.tensor(tb)

                            def ab(scalar):
                                for jv, (h, t) in enumerate(jobsv):
                                    scalar.wait_ge(c.sem("s1"), bT2 + jv + 1)
                                    scalar.copy(
                                        V_all[:, h * T + t * 128: h * T + t * 128 + 128],
                                        ps_v[jv % 2][:, :])
                                    scalar.sig("s2")
                            block.scalar(ab)
                        c.fin(s1=bT2 + len(jobsv), s2=bA2 + len(jobsv))

                P2 = [c.sb(Score, f"l0_P{i}", [128, T], BF16) for i in range(3)]
                S_sb2 = [c.sb(Score, f"l0_Ssb{i}", [128, T], F32) for i in range(2)]
                PTb = c.sb(Score, "l0_PTb", [128, T], BF16)
                c.attn_stat = c.sb(Score, "l0_astat", [128, 24], F32)
                with ExitStack() as Sp:
                    ps_Sb = c.ps(Sp, "l0_psS", [128, 2048], F32)
                    ps_PT = c.ps(Sp, "l0_psPT", [128, T], BF16)
                    ps_PV = [c.ps(Sp, f"l0_psPV{i}", [128, 128], F32)
                             for i in range(2)]

                    def sink(scalar, h, qt, ps):
                        return scalar.copy(
                            ctxT[:, h * T + qt * 128: h * T + qt * 128 + 128],
                            ps[0:128, 0:128])

                    emit_attn_core_v2(
                        c,
                        jobs=[(h, qt) for h in range(8) for qt in range(NT)],
                        s_chunks=lambda h, qt: [
                            (PhiT[0:62, h * T + qt * 128: h * T + qt * 128 + 128],
                             lambda k0, nn, h=h: PhiT[0:62, h * T + k0:
                                                      h * T + k0 + nn])],
                        klen_fn=lambda qt: (qt + 1) * 128,
                        v_lhsT_fn=lambda h, kc: V_all[:, h * T + kc * 128:
                                                      h * T + kc * 128 + 128],
                        dv=128, sink_fn=sink, P2=P2, S_sb2=S_sb2, PT=PTb,
                        ps_Sb=ps_Sb, ps_PT=ps_PT, ps_PV=ps_PV,
                        exp_scale=0.2, causal=True)

            wo_bf = c.sb(Sa, "l0_wobf", [128, ND * D], BF16)
            g_all = c.sb(Sa, "l0_g2", [128, NT * D], BF16)
            with c.block() as block:
                def syg(sync):
                    c.dma_push(sync, g_all[:], g0_dram)
                    sync.wait_ge(c.sem("dma"), c.val["dma"])
                block.sync(syg)
            emit_load_natural_chunked(c, w["L0_attn_Wo"], D, D, wo_bf, "l0wo")

            def s_psum(tensor, t, psum_ap):
                for nf in range(2):
                    for hh in range(8):
                        ins = tensor.matmul(
                            psum_ap[:, nf * 512: nf * 512 + 512],
                            ctxT[:, hh * T + t * 128: hh * T + t * 128 + 128],
                            wo_bf[:, hh * D + nf * 512: hh * D + nf * 512 + 512],
                            start=hh == 0, stop=hh == 7)
                return ins

            if fuse_next is None:
                emit_combine(c, xin, xout, s_psum, g_all)
            else:
                emit_combine_ln(c, xin, xout, s_psum, g_all, fuse_next[0],
                                stats_next=fuse_next[1])


def emit_layer1(c, xin, xout, w, xnT_in=None, fuse_next=None):
    nc = c.nc
    H1 = 16
    with ExitStack() as L:
        if xnT_in is None:
            xnT = c.sb(L, "l1_xnT", [128, ND * T], BF16)
        g_dram = nc.dram_tensor(f"l1_gspill_{c.uid()}", [128, NT * D], BF16,
                                kind="Internal").ap()
        if xnT_in is None:
            emit_ln(c, xin, xnT)
        else:
            xnT = xnT_in

        with ExitStack() as Sf:
            g_all = c.sb(Sf, "l1_g", [128, NT * D], BF16)
            wT = c.sb(Sf, "l1_wT", [128, ND * D], BF16)
            emit_load_T_chunked(c, w["L1_ffn_weight"], D, D, wT, "l1wt")

            gate_tok = c.sb(Sf, "l1_gate", [128, NT * 8], F32)
            gstat = c.sb(Sf, "l1_gstat", [128, 4], F32)
            gex = c.sb(Sf, "l1_gex", [128, 8], F32)
            gw_st = c.sb(Sf, "l1_gwst", [8, D], F32)
            gw_bf = c.sb(Sf, "l1_gwbf", [8, D], BF16)
            gwT = c.sb(Sf, "l1_gwT", [128, ND * 8], BF16)
            with c.block() as block:
                def sy(sync):
                    c.dma_push(sync, gw_st[:, :], w["L1_ffn_gate_w"])
                block.sync(sy)
                dtot = c.val["dma"]

                def sc(scalar):
                    scalar.wait_ge(c.sem("dma"), dtot)
                    scalar.copy(gw_bf[:, :], gw_st[:, :])
                block.scalar(sc)
            with ExitStack() as Sp:
                ps_gt = c.ps(Sp, "l1_psgt", [128, ND * 8], BF16)
                with c.block() as block:
                    def tb(tensor):
                        for cc in range(ND):
                            tensor.transpose(ps_gt[:, cc * 8:cc * 8 + 8],
                                             gw_bf[:, cc * 128:cc * 128 + 128],
                                             c.k["c_ident_bf"][0:8, 0:8])
                    block.tensor(tb)
                with c.block() as block:
                    def vb(vector):
                        vector.tensor_copy(gwT[:, :], ps_gt[:, :])
                    block.vector(vb)
            with ExitStack() as Sp:
                ps_g = [c.ps(Sp, "l1_psg0", [128, 8], F32),
                        c.ps(Sp, "l1_psg1", [128, 8], F32)]
                bT, bM, bE, bN = c.base("s0", "s1", "s2", "s3")
                with c.block() as block:
                    def tb(tensor):
                        for t in range(NT):
                            if t >= 2:
                                tensor.wait_ge(c.sem("s2"), bE + t - 1)
                            for cc in range(ND):
                                ins = tensor.matmul(
                                    ps_g[t % 2][:, 0:8],
                                    xnT[:, cc * T + t * 128: cc * T + t * 128 + 128],
                                    gwT[:, cc * 8: cc * 8 + 8],
                                    start=cc == 0, stop=cc == ND - 1)
                            ins.then_inc(c.sem("s0"), 1)
                    block.tensor(tb)

                    def vb(vector):
                        for t in range(NT):
                            vector.wait_ge(c.sem("s0"), bT + t + 1)
                            vector.reduce_max(gstat[:, 0:1], ps_g[t % 2][:, 0:8],
                                              axis=AX.X, negate=True)
                            vector.sig("s1")
                            vector.wait_ge(c.sem("s2"), bE + t + 1)
                            vector.reciprocal(gstat[:, 2:3], gstat[:, 1:2])
                            vector.tensor_scalar(gate_tok[:, t * 8:t * 8 + 8],
                                                 gex[:], gstat[:, 2:3], 1.0 / 16.0,
                                                 ALU.mult, ALU.mult)
                            vector.sig("s3")
                    block.vector(vb)

                    def ab(scalar):
                        for t in range(NT):
                            scalar.wait_ge(c.sem("s1"), bM + t + 1)
                            if t > 0:
                                scalar.wait_ge(c.sem("s3"), bN + t)
                            scalar.activation(gex[:], ps_g[t % 2][:, 0:8], AF.Exp,
                                              bias=gstat[:, 0:1],
                                              accum_out=gstat[:, 1:2])
                            scalar.sig("s2")
                    block.scalar(ab)
                c.fin(s0=bT + NT, s1=bM + NT, s2=bE + NT, s3=bN + NT)

            UallT = c.sb(Sf, "l1_UallT", [128, ND * 128], BF16)
            u_st = c.sb(Sf, "l1_ust", [128, D], F32)
            u_bf = c.sb(Sf, "l1_ubf", [128, D], BF16)
            emit_load_natural(c, w["L1_ffn_lora_U"].rearrange("e r c -> (e r) c"),
                              128, D, u_bf, u_st)
            with ExitStack() as Sp:
                ps_u = c.ps(Sp, "l1_psu", [128, D], BF16)
                with c.block() as block:
                    def tb(tensor):
                        for cc in range(ND):
                            tensor.transpose(ps_u[:, cc * 128:cc * 128 + 128],
                                             u_bf[:, cc * 128:cc * 128 + 128],
                                             c.k["c_ident_bf"][:, :])
                    block.tensor(tb)
                with c.block() as block:
                    def vb(vector):
                        vector.tensor_copy(UallT[:, :], ps_u[:, :])
                    block.vector(vb)

            teT = c.sb(Sf, "l1_teT", [16, T], BF16)
            tg_tok = c.sb(Sf, "l1_tgtok", [128, 16], BF16)
            tgT = [c.sb(Sf, f"l1_tgT{e}", [16, T], BF16) for e in range(8)]
            VeT = [c.sb(Sf, f"l1_VeT{e}", [16, D], BF16) for e in range(8)]
            v_st = c.sb(Sf, "l1_vst", [128, ND * 16], F32)
            v_bf = c.sb(Sf, "l1_vbf", [128, ND * 16], BF16)
            for e in range(8):
                emit_load_natural(c, w["L1_ffn_lora_V"][e, :, :], D, 16, v_bf, v_st)
                with ExitStack() as Sp:
                    ps_v = c.ps(Sp, "l1_psvT", [16, D], BF16)
                    with c.block() as block:
                        def tb(tensor, e=e):
                            for r in range(ND):
                                tensor.transpose(ps_v[:, r * 128:r * 128 + 128],
                                                 v_bf[:, r * 16:r * 16 + 16],
                                                 c.k["c_ident_bf"][:, :])
                        block.tensor(tb)
                    with c.block() as block:
                        def vb(vector, e=e):
                            vector.tensor_copy(VeT[e][:, :], ps_v[:, :])
                        block.vector(vb)
                with ExitStack() as Sp:
                    ps_t = c.ps(Sp, "l1_pste", [16, T], F32)
                    with c.block() as block:
                        def tb(tensor, e=e):
                            for n0 in range(0, T, 512):
                                for cc in range(ND):
                                    tensor.matmul(
                                        ps_t[:, n0:n0 + 512],
                                        UallT[:, cc * 128 + e * 16:
                                              cc * 128 + e * 16 + 16],
                                        xnT[:, cc * T + n0: cc * T + n0 + 512],
                                        start=cc == 0, stop=cc == ND - 1)
                        block.tensor(tb)
                    with c.block() as block:
                        def ab(scalar, e=e):
                            scalar.copy(teT[:, :], ps_t[:, :])
                        block.scalar(ab)
                with ExitStack() as Sp:
                    ps_tt = c.ps(Sp, "l1_pstt", [128, 16], BF16)
                    ps_tb = c.ps(Sp, "l1_pstb", [16, 128], BF16)
                    bT2, bV2, bT3, bV3 = c.base("s0", "s1", "s2", "s3")
                    with c.block() as block:
                        def tb(tensor, e=e):
                            for t in range(NT):
                                if t > 0:
                                    tensor.wait_ge(c.sem("s1"), bV2 + t)
                                tensor.transpose(
                                    ps_tt[:, 0:16], teT[:, t * 128:t * 128 + 128],
                                    c.k["c_ident_bf"][0:16, 0:16]
                                ).then_inc(c.sem("s0"), 1)
                                tensor.wait_ge(c.sem("s1"), bV2 + t + 1)
                                if t > 0:
                                    tensor.wait_ge(c.sem("s3"), bV3 + t)
                                tensor.transpose(
                                    ps_tb[:, 0:128], tg_tok[:, 0:16],
                                    c.k["c_ident_bf"][:, :]
                                ).then_inc(c.sem("s2"), 1)
                        block.tensor(tb)

                        def vb(vector, e=e):
                            for t in range(NT):
                                vector.wait_ge(c.sem("s0"), bT2 + t + 1)
                                vector.tensor_scalar_mul(
                                    tg_tok[:, 0:16], ps_tt[:, 0:16],
                                    gate_tok[:, t * 8 + e: t * 8 + e + 1])
                                vector.sig("s1")
                                vector.wait_ge(c.sem("s2"), bT3 + t + 1)
                                vector.tensor_copy(
                                    tgT[e][:, t * 128:t * 128 + 128],
                                    ps_tb[:, 0:128])
                                vector.sig("s3")
                        block.vector(vb)
                    c.fin(s0=bT2 + NT, s1=bV2 + NT, s2=bT3 + NT, s3=bV3 + NT)

            with ExitStack() as Sp:
                ps_f = c.ps(Sp, "l1_psf", [128, 2 * D], F32)
                bT, bA = c.base("s0", "s1")
                with c.block() as block:
                    def tb(tensor):
                        for t in range(NT):
                            buf = (t % 2) * D
                            if t >= 2:
                                tensor.wait_ge(c.sem("s1"), bA + t - 1)
                            for hf in range(2):
                                for cc in range(ND):
                                    tensor.matmul(
                                        ps_f[:, buf + hf * 512: buf + hf * 512 + 512],
                                        xnT[:, cc * T + t * 128: cc * T + t * 128 + 128],
                                        wT[:, cc * D + hf * 512: cc * D + hf * 512 + 512],
                                        start=cc == 0, stop=False)
                                for e in range(8):
                                    ins = tensor.matmul(
                                        ps_f[:, buf + hf * 512: buf + hf * 512 + 512],
                                        tgT[e][:, t * 128:t * 128 + 128],
                                        VeT[e][:, hf * 512: hf * 512 + 512],
                                        start=False, stop=e == 7)
                            ins.then_inc(c.sem("s0"), 1)
                    block.tensor(tb)

                    def ab(scalar):
                        for t in range(NT):
                            buf = (t % 2) * D
                            scalar.wait_ge(c.sem("s0"), bT + t + 1)
                            scalar.activation(g_all[:, t * D:t * D + D],
                                              ps_f[:, buf:buf + D], AF.Silu)
                            scalar.sig("s1")
                    block.scalar(ab)
                c.fin(s0=bT + NT, s1=bA + NT)
            with c.block() as block:
                def sy(sync):
                    c.dma_push(sync, g_dram, g_all[:])
                    sync.wait_ge(c.sem("dma"), c.val["dma"])
                block.sync(sy)

        saT = [c.sb(L, f"l1_saT{rt}", [128, T], BF16) for rt in range(8)]
        with ExitStack() as Sa:
            KT2 = [c.sb(Sa, f"l1_KT{i}", [128, ND * T], BF16) for i in range(2)]
            QTr = [c.sb(Sa, f"l1_QTr{i}", [128, ND * 512], BF16)
                   for i in range(2)]
            Qst = c.sb(Sa, "l1_Qst", [128, ND * D], BF16)
            V_all2 = [c.sb(Sa, f"l1_V{i}", [128, NT * 64], BF16)
                      for i in range(2)]
            wst = [c.sb(Sa, f"l1_wst{i}", [128, 512], F32) for i in range(4)]
            wkbf = [c.sb(Sa, f"l1_wkbf{i}", [128, D], BF16) for i in range(2)]
            vbf = c.sb(Sa, "l1_vbf", [128, 512], BF16)
            bkrow = [c.sb(Sa, f"l1_bkrow{i}", [1, 2112], BF16)
                     for i in range(2)]
            bcol_bf = [c.sb(Sa, f"l1_bcolb{i}", [128, 16], BF16)
                       for i in range(2)]
            bcolf = [c.sb(Sa, f"l1_bcolf{i}", [128, 16], F32)
                     for i in range(2)]
            P2 = [c.sb(Sa, f"l1_P{i}", [128, T], BF16) for i in range(2)]
            PTb = c.sb(Sa, "l1_PTb", [128, T], BF16)
            c.attn_stat = c.sb(Sa, "l1_astat", [128, 24], F32)
            wsrc = w["L1_attn_Wkqv"].rearrange("h (r p) c -> h p r c", p=128)
            bsrc = w["L1_attn_bkqv"]

            def chain_of(g):
                ch = []
                for et in range(3):
                    ch += [("K", et, 0), ("K", et, 1)]
                ch.append(("V", 0, 0))
                ch += [("K", 3, 0), ("K", 3, 1), ("K", 4, 0), ("K", 4, 1)]
                ch += [("Q", 0, 0), ("Q", 0, 1), ("Q", 1, 0), ("Q", 1, 1)]
                ch += [("K", 5, 0), ("K", 5, 1)]
                ch += [("Q", 2, 0), ("Q", 2, 1), ("Q", 3, 0), ("Q", 3, 1)]
                ch += [("K", 6, 0), ("K", 6, 1)]
                ch += [("Q", 4, 0), ("Q", 4, 1), ("Q", 5, 0), ("Q", 5, 1)]
                ch += [("K", 7, 0), ("K", 7, 1)]
                ch += [("Q", 6, 0), ("Q", 6, 1), ("Q", 7, 0), ("Q", 7, 1)]
                return ch
            CH_N = 33
            U_RING = 2

            cvt_idx = {}
            unit_idx = {}
            cnt = {"cvt": 0, "unit": 0}

            def reg_cvt(g, kind, et, hf):
                cvt_idx[(g, kind, et, hf)] = cnt["cvt"]
                cnt["cvt"] += 1

            def reg_unit(g, kind, et, n0):
                unit_idx[(g, kind, et, n0)] = cnt["unit"]
                cnt["unit"] += 1

            b10, b11, b12, b13 = c.base("s10", "s11", "s12", "s13")
            QUEUES = ["dxa", "dxb", "dxc", "dxd"]

            def emit_dma_chunk(sync, g, kind, et, hf, ci):
                q = QUEUES[ci % 4]
                if ci >= 4:
                    sync.wait_ge(c.sem("s10"), b10 + ci - 3)
                if kind == "K":
                    src = wsrc[g, :, hf * 4:hf * 4 + 4, et * 128:et * 128 + 128]
                elif kind == "Q":
                    src = wsrc[g, :, hf * 4:hf * 4 + 4,
                               D + et * 128:D + et * 128 + 128]
                else:
                    src = wsrc[g, :, :, 2048:2112]
                r = 4 if kind != "V" else ND
                c.dma_push(sync,
                           wst[ci % 4][:].rearrange("p (r cw) -> p r cw", r=r),
                           src, sem=q)

            def emit_cvt(scalar, g, kind, et, hf, ci, dval):
                scalar.wait_ge(c.sem(QUEUES[ci % 4]), dval)
                if kind == "K":
                    key = (g, "K", et - 2, 3) if et >= 2 else \
                        (g - 1, "K", et + 6, 3)
                    dst = wkbf[et % 2][:, hf * 512:hf * 512 + 512]
                elif kind == "Q":
                    key = (g - 1, "Q", et, 3)
                    dst = Qst[:, et * D + hf * 512:et * D + hf * 512 + 512]
                else:
                    key = (g - 1, "V", 3, 0)
                    dst = vbf[:, :]
                if key in unit_idx:
                    scalar.wait_ge(c.sem("s11"), b11 + unit_idx[key] + 1)
                scalar.copy(dst, wst[ci % 4][:, :])
                scalar.sig("s10")

            def emit_unit_mm(tensor, g, kind, et, n0):
                u = unit_idx[(g, kind, et, n0)]
                tensor.wait_ge(c.sem("s13"), b13 + g + 1)
                ckey = (g, kind, et, 1) if kind != "V" else (g, "V", 0, 0)
                tensor.wait_ge(c.sem("s10"), b10 + cvt_idx[ckey] + 1)
                if u >= U_RING:
                    tensor.wait_ge(c.sem("s12"), b12 + u - U_RING + 1)
                ps = ps_u[u % U_RING]
                if kind != "V":
                    wsb = wkbf[et % 2] if kind == "K" else \
                        Qst[:, et * D:et * D + D]
                    for dc in range(ND):
                        ins = tensor.matmul(ps[:, 0:512],
                                            wsb[:, dc * 128:dc * 128 + 128],
                                            xnT[:, dc * T + n0 * 512:
                                                dc * T + n0 * 512 + 512],
                                            start=dc == 0, stop=dc == ND - 1)
                else:
                    for tt in range(4):
                        t = et * 4 + tt
                        for dc in range(ND):
                            tensor.matmul(
                                ps[:, tt * 64:tt * 64 + 64],
                                xnT[:, dc * T + t * 128:dc * T + t * 128 + 128],
                                vbf[:, dc * 64:dc * 64 + 64],
                                start=dc == 0, stop=False)
                        ins = tensor.matmul(ps[:, tt * 64:tt * 64 + 64],
                                            c.k["c_ones_bf"][0:1, 0:128],
                                            bkrow[g % 2][0:1, 2048:2112],
                                            start=False, stop=True)
                ins.then_inc(c.sem("s11"), 1)

            def emit_unit_evac(scalar, g, kind, et, n0):
                u = unit_idx[(g, kind, et, n0)]
                scalar.wait_ge(c.sem("s11"), b11 + u + 1)
                ps = ps_u[u % U_RING]
                if kind == "K":
                    scalar.activation(KT2[g % 2][:, et * T + n0 * 512:
                                               et * T + n0 * 512 + 512],
                                      ps[:, 0:512], AF.Identity,
                                      bias=bcolf[g % 2][:, et:et + 1],
                                      scale=1.0)
                elif kind == "Q":
                    scalar.activation(QTr[n0 % 2][:, et * 512:et * 512 + 512],
                                      ps[:, 0:512], AF.Identity,
                                      bias=bcolf[g % 2][:, 8 + et:9 + et],
                                      scale=1.0 / 32.0)
                else:
                    scalar.copy(V_all2[g % 2][:, et * 256:et * 256 + 256],
                                ps[:, 0:256])
                scalar.sig("s12")

            def emit_bias_prep(gpsimd, g):
                c.dma_push(gpsimd, bkrow[g % 2][0:1, :], bsrc[g:g + 1, :],
                           sem="dma2")
                gpsimd.wait_ge(c.sem("dma2"), c.val["dma2"])
                for et in range(ND):
                    c.dma_push(gpsimd, bcol_bf[g % 2][0:128, et:et + 1],
                               bkrow[g % 2][0:1, et * 128:et * 128 + 128],
                               sem="dma2")
                    c.dma_push(gpsimd, bcol_bf[g % 2][0:128, 8 + et:9 + et],
                               bkrow[g % 2][0:1, D + et * 128:
                                            D + et * 128 + 128],
                               sem="dma2")
                gpsimd.wait_ge(c.sem("dma2"), c.val["dma2"])
                gpsimd.tensor_copy(bcolf[g % 2][:, 0:8],
                                   bcol_bf[g % 2][:, 0:8])
                gpsimd.tensor_scalar_mul(bcolf[g % 2][:, 8:16],
                                         bcol_bf[g % 2][:, 8:16], 1.0 / 32.0)
                gpsimd.sig("s13")

            def sched_units(h):
                g = h + 1
                s = {0: [(h, "Q", et, 1) for et in range(4)],
                     1: [(h, "Q", et, 1) for et in range(4, 8)],
                     3: [(h, "Q", et, 2) for et in range(4)],
                     4: [(h, "Q", et, 2) for et in range(4, 8)],
                     7: [(h, "Q", et, 3) for et in range(4)],
                     8: [(h, "Q", et, 3) for et in range(4, 8)]}
                if h < H1 - 1:
                    for i, sl in enumerate([2, 5, 6, 9, 10, 11, 12, 13]):
                        s[sl] = [(g, "K", i, n0) for n0 in range(4)]
                    s[14] = [(g, "V", tg, 0) for tg in range(4)]
                    s[15] = [(g, "Q", et, 0) for et in range(ND)]
                return s

            def sched_cvts(h):
                if h >= H1 - 1:
                    return {}
                return {0: [0, 1, 2, 3], 2: [4, 5], 4: [6], 5: [7, 8],
                        6: [9, 10], 8: [11, 12, 13, 14],
                        9: [15, 16, 17, 18, 19, 20],
                        10: [21, 22, 23, 24, 25, 26],
                        11: [27, 28, 29, 30, 31, 32]}

            ch0 = chain_of(0)
            for kind, et, hf in ch0:
                reg_cvt(0, kind, et, hf)
            _u0 = {2: [("K", 0, n0) for n0 in range(4)],
                   5: [("K", 1, n0) for n0 in range(4)],
                   6: [("K", 2, n0) for n0 in range(4)],
                   9: [("K", 3, n0) for n0 in range(4)],
                   10: [("K", 4, n0) for n0 in range(4)],
                   11: [("K", 5, n0) for n0 in range(4)],
                   12: [("K", 6, n0) for n0 in range(4)],
                   13: [("K", 7, n0) for n0 in range(4)],
                   14: [("V", tg, 0) for tg in range(4)],
                   15: [("Q", et, 0) for et in range(ND)]}
            prolog_units = [u for s in range(NT) for u in _u0.get(s, [])]
            for kind, et, n0 in prolog_units:
                reg_unit(0, kind, et, n0)

            with ExitStack() as Sp:
                ps_u = [c.ps(Sp, f"l1_psu{i}", [128, 512], F32)
                        for i in range(2)]
                dv_snap = []
                with c.block() as block:
                    def sy(sync):
                        for ci, (kind, et, hf) in enumerate(ch0):
                            emit_dma_chunk(sync, 0, kind, et, hf, ci)
                            dv_snap.append(c.val[QUEUES[ci % 4]])
                    block.sync(sy)

                    def gp(gpsimd):
                        for g in range(2):
                            emit_bias_prep(gpsimd, g)
                    block.gpsimd(gp)

                    units0 = {2: [("K", 0, n0) for n0 in range(4)],
                              5: [("K", 1, n0) for n0 in range(4)],
                              6: [("K", 2, n0) for n0 in range(4)],
                              9: [("K", 3, n0) for n0 in range(4)],
                              10: [("K", 4, n0) for n0 in range(4)],
                              11: [("K", 5, n0) for n0 in range(4)],
                              12: [("K", 6, n0) for n0 in range(4)],
                              13: [("K", 7, n0) for n0 in range(4)],
                              14: [("V", tg, 0) for tg in range(4)],
                              15: [("Q", et, 0) for et in range(ND)]}
                    cvts0 = {0: [0, 1, 2, 3], 2: [4, 5], 4: [6], 5: [7, 8],
                             6: [9, 10], 8: [11, 12, 13, 14],
                             9: [15, 16, 17, 18, 19, 20],
                             10: [21, 22, 23, 24, 25, 26],
                             11: [27, 28, 29, 30, 31, 32]}

                    def sc(scalar):
                        for s in range(NT):
                            for kind, et, n0 in units0.get(s, []):
                                emit_unit_evac(scalar, 0, kind, et, n0)
                            for ci in cvts0.get(s, []):
                                kind, et, hf = ch0[ci]
                                emit_cvt(scalar, 0, kind, et, hf, ci,
                                         dv_snap[ci])
                    block.scalar(sc)

                    def tb0(tensor):
                        for kind, et, n0 in prolog_units:
                            emit_unit_mm(tensor, 0, kind, et, n0)
                    block.tensor(tb0)
                c.val["s10"] = b10 + cnt["cvt"]
                c.val["s11"] = b11 + cnt["unit"]
                c.val["s12"] = b12 + cnt["unit"]
                c.val["s13"] = b13 + 2

            with ExitStack() as Sp:
                ps_u = [c.ps(Sp, f"l1_psu{i}b", [128, 512], F32)
                        for i in range(2)]
                ps_Sb = c.ps(Sp, "l1_psS", [128, 2048], F32)
                ps_PT = c.ps(Sp, "l1_psPT", [128, 1024], BF16)
                ps_PV = [c.ps(Sp, "l1_psPV", [64, 128], F32)]

                def sink(scalar, hh, qt, ps):
                    rt, sub = hh // 2, (hh % 2) * 64
                    return scalar.copy(
                        saT[rt][sub:sub + 64, qt * 128:qt * 128 + 128],
                        ps[0:64, 0:128])

                heads = []
                for h in range(H1):
                    g = h + 1
                    su = sched_units(h)
                    scv = sched_cvts(h)
                    chain = chain_of(g) if h < H1 - 1 else []
                    cb = cnt["cvt"]
                    for kind, et, hf in chain:
                        reg_cvt(g, kind, et, hf)
                    for s in range(NT):
                        for (gg, kind, et, n0) in su.get(s, []):
                            reg_unit(gg, kind, et, n0)
                    thr = {0: b12 + unit_idx[(h, "Q", ND - 1, 0)] + 1}
                    for n0 in range(1, 4):
                        thr[4 * n0] = b12 + unit_idx[(h, "Q", ND - 1, n0)] + 1
                    dvals_h = []

                    def pre_s(tensor, j, thr=thr):
                        if j in thr:
                            tensor.wait_ge(c.sem("s12"), thr[j])

                    def t_hook(tensor, j, su=su):
                        for (gg, kind, et, n0) in su.get(j, []):
                            emit_unit_mm(tensor, gg, kind, et, n0)

                    def a_hook(scalar, j, su=su, scv=scv, chain=chain, cb=cb,
                               g=g, dvals_h=dvals_h):
                        for (gg, kind, et, n0) in su.get(j, []):
                            emit_unit_evac(scalar, gg, kind, et, n0)
                        for pos in scv.get(j, []):
                            kind, et, hf = chain[pos]
                            emit_cvt(scalar, g, kind, et, hf, cb + pos,
                                     dvals_h[pos])

                    tb, ab, vb, fin = attn_core_v2_streams(
                        c,
                        jobs=[(h, qt) for qt in range(NT)],
                        s_chunks=lambda hh, qt: [
                            (QTr[(qt // 4) % 2][:, et * 512 + (qt % 4) * 128:
                                 et * 512 + (qt % 4) * 128 + 128],
                             lambda k0, nn, et=et, hh=hh:
                             KT2[hh % 2][:, et * T + k0:et * T + k0 + nn])
                            for et in range(ND)],
                        klen_fn=lambda qt: (qt + 1) * 128,
                        v_lhsT_fn=lambda hh, kc:
                        V_all2[hh % 2][:, kc * 64:kc * 64 + 64],
                        dv=64, sink_fn=sink, P2=P2, S_sb2=[], PT=PTb,
                        ps_Sb=ps_Sb, ps_PT=ps_PT, ps_PV=ps_PV,
                        exp_scale=1.0, causal=True, wave_cols=1024,
                        pre_s_hook=pre_s, t_hook=t_hook, a_hook=a_hook)
                    fin()
                    c.val["s10"] = b10 + cnt["cvt"]
                    c.val["s11"] = b11 + cnt["unit"]
                    c.val["s12"] = b12 + cnt["unit"]
                    if h + 2 < H1:
                        c.val["s13"] = b13 + h + 3
                    heads.append((h, g, chain, cb, tb, ab, vb, dvals_h))

                with c.block() as block:
                    def sy(sync):
                        for (h, g, chain, cb, tb, ab, vb, dvals_h) in heads:
                            for pos, (kind, et, hf) in enumerate(chain):
                                emit_dma_chunk(sync, g, kind, et, hf,
                                               cb + pos)
                                dvals_h.append(
                                    c.val[QUEUES[(cb + pos) % 4]])
                    block.sync(sy)

                    def gp(gpsimd):
                        for h in range(H1 - 2):
                            gpsimd.wait_ge(
                                c.sem("s12"),
                                b12 + unit_idx[(h, "Q", ND - 1, 3)] + 1)
                            emit_bias_prep(gpsimd, h + 2)
                    block.gpsimd(gp)

                    def tall(tensor):
                        for rec in heads:
                            rec[4](tensor)
                    block.tensor(tall)

                    def aall(scalar):
                        for rec in heads:
                            rec[5](scalar)
                    block.scalar(aall)

                    def vall(vector):
                        for rec in heads:
                            rec[6](vector)
                    block.vector(vall)

        g_all = c.sb(L, "l1_g2", [128, NT * D], BF16)
        with c.block() as block:
            def sy(sync):
                c.dma_push(sync, g_all[:], g_dram)
                sync.wait_ge(c.sem("dma"), c.val["dma"])
            block.sync(sy)
        wp_bf = c.sb(L, "l1_wpbf", [128, ND * D], BF16)
        bp_bf = c.sb(L, "l1_bpbf", [1, D], BF16)
        with ExitStack() as Sw:
            wp_st = c.sb(Sw, "l1_wpst", [128, ND * D], F32)
            rstage = c.sb(Sw, "l1_rstage", [1, D], F32)
            emit_load_natural(c, w["L1_attn_Wp"], D, D, wp_bf, wp_st)
            emit_row_load(c, w["L1_attn_bp"], D, rstage, bp_bf)

        def s_psum(tensor, t, psum_ap):
            for nf in range(2):
                for rt in range(8):
                    tensor.matmul(
                        psum_ap[:, nf * 512: nf * 512 + 512],
                        saT[rt][:, t * 128: t * 128 + 128],
                        wp_bf[:, rt * D + nf * 512: rt * D + nf * 512 + 512],
                        start=rt == 0, stop=False)
                ins = tensor.matmul(
                    psum_ap[:, nf * 512: nf * 512 + 512],
                    c.k["c_ones_bf"][0:1, 0:128],
                    bp_bf[0:1, nf * 512: nf * 512 + 512],
                    start=False, stop=True)
            return ins

        if fuse_next is None:
            emit_combine(c, xin, xout, s_psum, g_all)
        else:
            emit_combine_ln(c, xin, xout, s_psum, g_all, fuse_next[0],
                            stats_next=fuse_next[1])


def emit_ln_stats_chain(c, vector, scalar_pre, stats, eps=1e-5):
    x_ap, sq_ap = scalar_pre
    vector.reduce_sum(stats[:, 0:1], x_ap, axis=AX.X)
    vector.reduce_sum(stats[:, 5:6], sq_ap, axis=AX.X)
    vector.tensor_scalar_mul(stats[:, 6:7], stats[:, 0:1], -1.0 / D)
    vector.tensor_mul(stats[:, 7:8], stats[:, 6:7], stats[:, 6:7])
    vector.tensor_scalar(stats[:, 1:2], stats[:, 5:6], 1.0 / D, eps,
                         ALU.mult, ALU.add)
    vector.tensor_sub(stats[:, 1:2], stats[:, 1:2], stats[:, 7:8])


def emit_layer2(c, xin, xout, w, xnT_in=None, lnstats_in=None,
                fuse_next=None):
    nc = c.nc
    with ExitStack() as L:
        if xnT_in is None:
            xnT = c.sb(L, "l2_xnT", [128, ND * T], BF16)
            lnstats = c.sb(L, "l2_lnst", [128, 2 * NT], F32)
        s_all = c.sb(L, "l2_s", [128, NT * D], BF16)
        if xnT_in is None:
            emit_ln(c, xin, xnT, stats_out=lnstats)
        else:
            xnT, lnstats = xnT_in, lnstats_in

        with ExitStack() as Sa:
            rstage = c.sb(Sa, "l2_rstage", [1, D], F32)
            rbf = c.sb(Sa, "l2_rbf", [1, D], BF16)
            saT = [c.sb(Sa, f"l2_saT{rt}", [128, T], BF16) for rt in range(4)]
            QT = [c.sb(Sa, f"l2_QT{h}", [64, T], BF16) for h in range(8)]
            KTt = c.sb(Sa, "l2_KT", [64, T], BF16)
            VTt = c.sb(Sa, "l2_VT", [64, T], BF16)
            V_tok = c.sb(Sa, "l2_Vtok", [128, NT * 64], BF16)
            with ExitStack() as S1:
                ln2T = c.sb(S1, "l2_ln2T", [128, ND * T], BF16)
                nw_bc = c.sb(S1, "l2_nw", [128, D], F32)
                nb_bc = c.sb(S1, "l2_nb", [128, D], F32)
                for nm, dst in [("L2_attn_norm_w", nw_bc),
                                ("L2_attn_norm_b", nb_bc)]:
                    emit_row_load(c, w[nm], D, rstage, rbf)
                    emit_bcast_row(c, rbf, D, dst)

                with ExitStack() as S2:
                    xstage = c.sb(S2, "l2_xst", [128, 2 * D], F32)
                    xnf = c.sb(S2, "l2_xnf", [128, D], F32)
                    sq = c.sb(S2, "l2_sq", [128, D], F32)
                    t1f = c.sb(S2, "l2_t1f", [128, D], F32)
                    ln2_bf = c.sb(S2, "l2_ln2bf", [128, D], BF16)
                    stats = c.sb(S2, "l2_st2", [128, 8], F32)
                    ps_tr = c.ps(S2, "l2_pstr", [128, D], BF16)
                    xsrc = xin.rearrange("(t p) d -> p t d", p=128)
                    bA, bV1, bA2, bV2, bA3, bV3, bT1, bV4 = c.base(
                        "s0", "s1", "s2", "s3", "s4", "s5", "s6", "s7")
                    bdx = [c.val["dxa"], c.val["dxb"]]
                    dsem = ["dxa", "dxb"]
                    dval = [bdx[t % 2] + 16 * (t // 2 + 1) for t in range(NT)]
                    with c.block() as block:
                        def sy(sync):
                            for t in range(NT):
                                if t >= 2:
                                    sync.wait_ge(c.sem("s0"), bA + t - 1)
                                c.dma_push(sync,
                                           xstage[:, (t % 2) * D:(t % 2) * D + D],
                                           xsrc[:, t, :], sem=dsem[t % 2])
                            sync.wait_ge(c.sem("dxa"), dval[NT - 2])
                            sync.wait_ge(c.sem("dxb"), dval[NT - 1])
                        block.sync(sy)

                        def ab(scalar):
                            for t in range(NT):
                                xt = xstage[:, (t % 2) * D:(t % 2) * D + D]
                                scalar.wait_ge(c.sem(dsem[t % 2]), dval[t])
                                if t > 0:
                                    scalar.wait_ge(c.sem("s1"), bV1 + t)
                                scalar.activation(
                                    xnf[:], xt, AF.Identity,
                                    bias=lnstats[:, 2 * t + 1:2 * t + 2],
                                    scale=lnstats[:, 2 * t:2 * t + 1])
                                scalar.square(sq[:], xnf[:])
                                scalar.sig("s0")
                                scalar.wait_ge(c.sem("s1"), bV1 + t + 1)
                                scalar.activation(stats[:, 2:3], stats[:, 1:2],
                                                  AF.Sqrt)
                                scalar.sig("s2")
                                scalar.wait_ge(c.sem("s3"), bV2 + t + 1)
                                scalar.activation(t1f[:], xnf[:], AF.Identity,
                                                  bias=stats[:, 4:5],
                                                  scale=stats[:, 3:4])
                                scalar.sig("s4")
                        block.scalar(ab)

                        def vb(vector):
                            for t in range(NT):
                                vector.wait_ge(c.sem("s0"), bA + t + 1)
                                emit_ln_stats_chain(c, vector, (xnf[:], sq[:]),
                                                    stats)
                                vector.sig("s1")
                                vector.wait_ge(c.sem("s2"), bA2 + t + 1)
                                vector.reciprocal(stats[:, 3:4], stats[:, 2:3])
                                vector.tensor_mul(stats[:, 4:5], stats[:, 6:7],
                                                  stats[:, 3:4])
                                vector.sig("s3")
                                vector.wait_ge(c.sem("s4"), bA3 + t + 1)
                                vector.tensor_mul(t1f[:], t1f[:], nw_bc[:])
                                if t > 0:
                                    vector.wait_ge(c.sem("s6"), bT1 + t)
                                vector.tensor_add(ln2_bf[:], t1f[:], nb_bc[:])
                                vector.sig("s5")
                                vector.wait_ge(c.sem("s6"), bT1 + t + 1)
                                dst = ln2T[:].rearrange("p (cc n) -> p cc n", cc=ND)
                                vector.tensor_copy(
                                    dst[:, :, t * 128:t * 128 + 128],
                                    ps_tr[:].rearrange("p (cc n) -> p cc n", cc=ND))
                                vector.sig("s7")
                        block.vector(vb)

                        def tb(tensor):
                            for t in range(NT):
                                tensor.wait_ge(c.sem("s5"), bV3 + t + 1)
                                if t > 0:
                                    tensor.wait_ge(c.sem("s7"), bV4 + t)
                                for cc in range(ND):
                                    ins = tensor.transpose(
                                        ps_tr[:, cc * 128:cc * 128 + 128],
                                        ln2_bf[:, cc * 128:cc * 128 + 128],
                                        c.k["c_ident_bf"][:, :])
                                ins.then_inc(c.sem("s6"), 1)
                        block.tensor(tb)
                    c.fin(s0=bA + NT, s1=bV1 + NT, s2=bA2 + NT, s3=bV2 + NT,
                          s4=bA3 + NT, s5=bV3 + NT, s6=bT1 + NT, s7=bV4 + NT)

                with ExitStack() as Sw:
                    wq_st = c.sb(Sw, "l2_wqst", [128, ND * 512], F32)
                    wq_bf = c.sb(Sw, "l2_wqbf", [128, ND * 512], BF16)
                    wkv_bf = c.sb(Sw, "l2_wkvbf", [128, ND * 128], BF16)
                    emit_load_natural(c, w["L2_attn_Wq"], D, 512, wq_bf, wq_st)
                    emit_load_natural(c, w["L2_attn_Wkv"], D, 128, wkv_bf, wq_st)
                    for h in range(8):
                        with ExitStack() as Sp:
                            ps_q = c.ps(Sp, "l2_psq", [64, T], F32)
                            with c.block() as block:
                                def tb(tensor, h=h):
                                    for n0 in range(0, T, 512):
                                        for cc in range(ND):
                                            tensor.matmul(
                                                ps_q[:, n0:n0 + 512],
                                                wq_bf[:, cc * 512 + h * 64:
                                                      cc * 512 + h * 64 + 64],
                                                ln2T[:, cc * T + n0:
                                                     cc * T + n0 + 512],
                                                start=cc == 0, stop=cc == ND - 1)
                                block.tensor(tb)
                            with c.block() as block:
                                def ab(scalar, h=h):
                                    scalar.activation(QT[h][:, :], ps_q[:, :],
                                                      AF.Copy, scale=0.125)
                                block.scalar(ab)
                    with ExitStack() as Sp:
                        ps_kv = c.ps(Sp, "l2_pskv", [128, T], F32)
                        with c.block() as block:
                            def tb(tensor):
                                for n0 in range(0, T, 512):
                                    for cc in range(ND):
                                        tensor.matmul(
                                            ps_kv[:, n0:n0 + 512],
                                            wkv_bf[:, cc * 128:cc * 128 + 128],
                                            ln2T[:, cc * T + n0: cc * T + n0 + 512],
                                            start=cc == 0, stop=cc == ND - 1)
                            block.tensor(tb)
                        with c.block() as block:
                            def ab(scalar):
                                scalar.copy(KTt[:, :], ps_kv[0:64, :])
                            block.scalar(ab)

                            def vb(vector):
                                vector.tensor_copy(VTt[:, :], ps_kv[64:128, :])
                            block.vector(vb)
                    with ExitStack() as Sp:
                        ps_vt = c.ps(Sp, "l2_psvt", [128, NT * 64], BF16)
                        with c.block() as block:
                            def tb(tensor):
                                for t in range(NT):
                                    tensor.transpose(
                                        ps_vt[:, t * 64:t * 64 + 64],
                                        VTt[:, t * 128:t * 128 + 128],
                                        c.k["c_ident_bf"][0:64, 0:64])
                            block.tensor(tb)
                        with c.block() as block:
                            def vb(vector):
                                vector.tensor_copy(V_tok[:, :], ps_vt[:, :])
                            block.vector(vb)

            wo_bf = c.sb(Sa, "l2_wobf", [128, 4 * D], BF16)
            onw_bc = c.sb(Sa, "l2_onw", [128, D], F32)
            onb_bc = c.sb(Sa, "l2_onb", [128, D], F32)
            for nm, dst in [("L2_attn_out_norm_w", onw_bc),
                            ("L2_attn_out_norm_b", onb_bc)]:
                emit_row_load(c, w[nm], D, rstage, rbf)
                emit_bcast_row(c, rbf, D, dst)
            with ExitStack() as Sw2:
                wo_st = c.sb(Sw2, "l2_wost", [128, 4 * D], F32)
                emit_load_natural(c, w["L2_attn_Wo"], 512, D, wo_bf, wo_st)

            P2 = [c.sb(Sa, f"l2_P{i}", [128, T], BF16) for i in range(3)]
            S_sb2 = [c.sb(Sa, f"l2_Ssb{i}", [128, T], F32) for i in range(2)]
            PTb = c.sb(Sa, "l2_PTb", [128, T], BF16)
            c.attn_stat = c.sb(Sa, "l2_astat", [128, 24], F32)
            with ExitStack() as Sp:
                ps_Sb = c.ps(Sp, "l2_psS", [128, 2048], F32)
                ps_PT = c.ps(Sp, "l2_psPT", [128, T], BF16)
                ps_PV = [c.ps(Sp, f"l2_psPV{i}", [64, 128], F32)
                         for i in range(2)]

                def sink(scalar, h, qt, ps):
                    rt, sub = h // 2, (h % 2) * 64
                    return scalar.copy(
                        saT[rt][sub:sub + 64, qt * 128:qt * 128 + 128],
                        ps[0:64, 0:128])

                emit_attn_core_v2(
                    c,
                    jobs=[(h, qt) for h in range(8) for qt in range(NT)],
                    s_chunks=lambda h, qt: [
                        (QT[h][0:64, qt * 128:qt * 128 + 128],
                         lambda k0, nn: KTt[0:64, k0:k0 + nn])],
                    klen_fn=lambda qt: T,
                    v_lhsT_fn=lambda h, kc: V_tok[:, kc * 64:kc * 64 + 64],
                    dv=64, sink_fn=sink, P2=P2, S_sb2=S_sb2, PT=PTb,
                    ps_Sb=ps_Sb, ps_PT=ps_PT, ps_PV=ps_PV,
                    exp_scale=1.0, causal=False)

            with ExitStack() as Sp:
                ps_y = c.ps(Sp, "l2_psy", [128, 2 * D], F32)
                y_tok = c.sb(Sp, "l2_ytok", [128, D], F32)
                sq = c.sb(Sp, "l2_sqy", [128, D], F32)
                t1f = c.sb(Sp, "l2_t1fy", [128, D], F32)
                stats = c.sb(Sp, "l2_sty", [128, 8], F32)
                bT, bA1, bV1, bA2, bV2, bA3, bV3 = c.base(
                    "s0", "s1", "s2", "s3", "s4", "s5", "s6")
                with c.block() as block:
                    def tb(tensor):
                        for t in range(NT):
                            buf = (t % 2) * D
                            if t >= 2:
                                tensor.wait_ge(c.sem("s1"), bA1 + t - 1)
                            for nf in range(2):
                                for ic in range(4):
                                    ins = tensor.matmul(
                                        ps_y[:, buf + nf * 512: buf + nf * 512 + 512],
                                        saT[ic][:, t * 128: t * 128 + 128],
                                        wo_bf[:, ic * D + nf * 512:
                                              ic * D + nf * 512 + 512],
                                        start=ic == 0, stop=ic == 3)
                            ins.then_inc(c.sem("s0"), 1)
                    block.tensor(tb)

                    def ab(scalar):
                        for t in range(NT):
                            buf = (t % 2) * D
                            scalar.wait_ge(c.sem("s0"), bT + t + 1)
                            if t > 0:
                                scalar.wait_ge(c.sem("s2"), bV1 + t)
                            scalar.copy(y_tok[:], ps_y[:, buf:buf + D])
                            scalar.square(sq[:], y_tok[:])
                            scalar.sig("s1")
                            scalar.wait_ge(c.sem("s2"), bV1 + t + 1)
                            scalar.activation(stats[:, 2:3], stats[:, 1:2],
                                              AF.Sqrt)
                            scalar.sig("s3")
                            scalar.wait_ge(c.sem("s4"), bV2 + t + 1)
                            scalar.activation(t1f[:], y_tok[:], AF.Identity,
                                              bias=stats[:, 4:5],
                                              scale=stats[:, 3:4])
                            scalar.sig("s5")
                    block.scalar(ab)

                    def vb(vector):
                        for t in range(NT):
                            vector.wait_ge(c.sem("s1"), bA1 + t + 1)
                            emit_ln_stats_chain(c, vector, (y_tok[:], sq[:]),
                                                stats)
                            vector.sig("s2")
                            vector.wait_ge(c.sem("s3"), bA2 + t + 1)
                            vector.reciprocal(stats[:, 3:4], stats[:, 2:3])
                            vector.tensor_mul(stats[:, 4:5], stats[:, 6:7],
                                              stats[:, 3:4])
                            vector.sig("s4")
                            vector.wait_ge(c.sem("s5"), bA3 + t + 1)
                            vector.tensor_mul(t1f[:], t1f[:], onw_bc[:])
                            vector.tensor_add(s_all[:, t * D:t * D + D], t1f[:],
                                              onb_bc[:])
                            vector.sig("s6")
                    block.vector(vb)
                c.fin(s0=bT + NT, s1=bA1 + NT, s2=bV1 + NT, s3=bA2 + NT,
                      s4=bV2 + NT, s5=bA3 + NT, s6=bV3 + NT)

        g_all = c.sb(L, "l2_g", [128, NT * D], BF16)
        with ExitStack() as Sf:
            murT = c.sb(Sf, "l2_murT", [128, ND], F32)
            mukT = c.sb(Sf, "l2_mukT", [128, ND], F32)
            with ExitStack() as Smu:
                rstage = c.sb(Smu, "l2_rstage2", [1, D], F32)
                for nm, dst in [("L2_ffn_mu_r", murT), ("L2_ffn_mu_k", mukT)]:
                    emit_row_load(c, w[nm], D, rstage)
                    with ExitStack() as Sp:
                        ps_mu = c.ps(Sp, "l2_psmu", [128, ND], F32)
                        with c.block() as block:
                            def tb(tensor, rstage=rstage):
                                for cc in range(ND):
                                    tensor.transpose(
                                        ps_mu[:, cc:cc + 1],
                                        rstage[0:1, cc * 128:cc * 128 + 128],
                                        c.k["c_ident_f32"][0:1, 0:1])
                            block.tensor(tb)
                        with c.block() as block:
                            def vb(vector, dst=dst):
                                vector.tensor_copy(dst[:, :], ps_mu[:, :])
                            block.vector(vb)

            def emit_mix(dstT, muT):
                with c.block() as block:
                    def vb(vector):
                        for cc in range(ND):
                            xc = xnT[:, cc * T: cc * T + T]
                            dc = dstT[:, cc * T: cc * T + T]
                            vector.tensor_sub(dc[:, 1:T], xc[:, 1:T],
                                              xc[:, 0:T - 1])
                            vector.scalar_tensor_tensor(
                                dc[:, 1:T], dc[:, 1:T], muT[:, cc:cc + 1],
                                xc[:, 0:T - 1], ALU.mult, ALU.add)
                            vector.tensor_scalar_mul(dc[:, 0:1], xc[:, 0:1],
                                                     muT[:, cc:cc + 1])
                    block.vector(vb)

            kaT = c.sb(Sf, "l2_kaT", [128, ND * T], BF16)
            with ExitStack() as Sk:
                xkT = c.sb(Sk, "l2_xkT", [128, ND * T], BF16)
                WkT = c.sb(Sk, "l2_WkT", [128, ND * D], BF16)
                emit_load_T_chunked(c, w["L2_ffn_W_k"], D, D, WkT, "l2wk", CH=256)
                emit_mix(xkT, mukT)
                for ot in range(ND):
                    with ExitStack() as Sp:
                        ps_k = c.ps(Sp, "l2_psk", [128, T], F32)
                        with c.block() as block:
                            def tb(tensor, ot=ot):
                                for n0 in range(0, T, 512):
                                    for cc in range(ND):
                                        tensor.matmul(
                                            ps_k[:, n0:n0 + 512],
                                            WkT[:, cc * D + ot * 128:
                                                cc * D + ot * 128 + 128],
                                            xkT[:, cc * T + n0: cc * T + n0 + 512],
                                            start=cc == 0, stop=cc == ND - 1)
                            block.tensor(tb)
                        with c.block() as block:
                            def ab(scalar, ot=ot):
                                scalar.activation(kaT[:, ot * T: ot * T + T],
                                                  ps_k[:, :], AF.Relu)
                            block.scalar(ab)
                        with c.block() as block:
                            def vb(vector, ot=ot):
                                kc = kaT[:, ot * T: ot * T + T]
                                vector.tensor_mul(kc, kc, kc)
                            block.vector(vb)

            with ExitStack() as Sr:
                xrT = c.sb(Sr, "l2_xrT", [128, ND * T], BF16)
                WrT = c.sb(Sr, "l2_WrT", [128, ND * D], BF16)
                WvT = c.sb(Sr, "l2_WvT", [128, ND * D], BF16)
                r_tile = c.sb(Sr, "l2_rtile", [128, D], BF16)
                tmp_bf = c.sb(Sr, "l2_tmpbf", [128, D], BF16)
                sgm_bf = c.sb(Sr, "l2_sgmbf", [128, D], BF16)
                emit_load_T_chunked(c, w["L2_ffn_W_r"], D, D, WrT, "l2wr", CH=256)
                emit_load_T_chunked(c, w["L2_ffn_W_v"], D, D, WvT, "l2wv", CH=256)
                emit_mix(xrT, murT)
                with ExitStack() as Sp:
                    ps_r = c.ps(Sp, "l2_psr", [128, D], F32)
                    ps_v = c.ps(Sp, "l2_psv", [128, D], F32)
                    bT, bA1, bV1, bA2 = c.base("s0", "s1", "s2", "s3")
                    with c.block() as block:
                        def tb(tensor):
                            for t in range(NT):
                                if t > 0:
                                    tensor.wait_ge(c.sem("s1"), bA1 + t)
                                for nf in range(2):
                                    for cc in range(ND):
                                        ins = tensor.matmul(
                                            ps_r[:, nf * 512: nf * 512 + 512],
                                            xrT[:, cc * T + t * 128:
                                                cc * T + t * 128 + 128],
                                            WrT[:, cc * D + nf * 512:
                                                cc * D + nf * 512 + 512],
                                            start=cc == 0, stop=cc == ND - 1)
                                ins.then_inc(c.sem("s0"), 1)
                                if t > 0:
                                    tensor.wait_ge(c.sem("s2"), bV1 + t)
                                for nf in range(2):
                                    for cc in range(ND):
                                        ins = tensor.matmul(
                                            ps_v[:, nf * 512: nf * 512 + 512],
                                            kaT[:, cc * T + t * 128:
                                                cc * T + t * 128 + 128],
                                            WvT[:, cc * D + nf * 512:
                                                cc * D + nf * 512 + 512],
                                            start=cc == 0, stop=cc == ND - 1)
                                ins.then_inc(c.sem("s0"), 1)
                        block.tensor(tb)

                        def ab(scalar):
                            for t in range(NT):
                                scalar.wait_ge(c.sem("s0"), bT + 2 * t + 1)
                                scalar.activation(r_tile[:], ps_r[:, :],
                                                  AF.Sigmoid)
                                scalar.sig("s1")
                                scalar.wait_ge(c.sem("s2"), bV1 + t + 1)
                                scalar.activation(sgm_bf[:], tmp_bf[:],
                                                  AF.Sigmoid)
                                scalar.sig("s3")
                        block.scalar(ab)

                        def vb(vector):
                            for t in range(NT):
                                vector.wait_ge(c.sem("s0"), bT + 2 * t + 2)
                                vector.wait_ge(c.sem("s1"), bA1 + t + 1)
                                if t > 0:
                                    vector.wait_ge(c.sem("s3"), bA2 + t)
                                vector.tensor_mul(tmp_bf[:], ps_v[:, :],
                                                  r_tile[:])
                                vector.sig("s2")
                                vector.wait_ge(c.sem("s3"), bA2 + t + 1)
                                vector.tensor_mul(g_all[:, t * D:t * D + D],
                                                  tmp_bf[:], sgm_bf[:])
                                vector.sig("s4")
                        block.vector(vb)
                    c.fin(s0=bT + 2 * NT, s1=bA1 + NT, s2=bV1 + NT, s3=bA2 + NT,
                          s4=c.val["s4"] + NT)

        if fuse_next is None:
            emit_combine(c, xin, xout, None, g_all, s_sbuf=s_all)
        else:
            emit_combine_ln(c, xin, xout, None, g_all, fuse_next[0],
                            stats_next=fuse_next[1], s_sbuf=s_all)


D4 = [0.4829629131445341, 0.8365163037378079, 0.2241438680420134,
      -0.1294095225512604]
N_SCALES = 11
OFFSETS = sorted({(1 << j) * tau for j in range(N_SCALES) for tau in range(4)
                  if (1 << j) * tau < T})
HCO = np.zeros((N_SCALES, len(OFFSETS)), dtype=np.float64)
for _j in range(N_SCALES):
    for _tau, _hc in enumerate(D4):
        _o = (1 << _j) * _tau
        if _o < T:
            HCO[_j, OFFSETS.index(_o)] += _hc


def l3_host_constants():
    no = len(OFFSETS)
    c_A = np.zeros((128, no * 4), dtype=np.float64)
    for oi in range(no):
        for hl in range(4):
            for s in range(N_SCALES):
                c_A[hl * 32 + s, oi * 4 + hl] = HCO[s, oi]
    c_Z = np.zeros((128, 4), dtype=np.float64)
    for hl in range(4):
        for s in range(N_SCALES):
            c_Z[hl * 32 + s, hl] = 1.0
    c_Zbc = np.zeros((4, 128), dtype=np.float64)
    for p in range(128):
        c_Zbc[p // 32, p] = 1.0
    c_sel = np.zeros((128, 2 * 4), dtype=np.float64)
    for rtl in range(2):
        for p in range(128):
            c_sel[p, rtl * 4 + 2 * rtl + p // 64] = 1.0
    c_bc = np.zeros((4, 2 * 128), dtype=np.float64)
    for rtl in range(2):
        for p in range(128):
            c_bc[2 * rtl + p // 64, rtl * 128 + p] = 1.0
    c_Abc = np.zeros((128, no * 2 * 128), dtype=np.float64)
    for oi in range(no):
        for i in range(2):
            for p in range(128):
                c_Abc[:, (oi * 2 + i) * 128 + p] = \
                    c_A[:, oi * 4 + 2 * i + p // 64]
    c_selbc = np.zeros((128, 256), dtype=np.float64)
    for i in range(2):
        for p in range(128):
            c_selbc[:, i * 128 + p] = c_sel[:, 6 * i + p // 64]
    return {"c_A": c_A.astype(BF), "c_Z": c_Z.astype(BF),
            "c_Zbc": c_Zbc.astype(BF), "c_sel": c_sel.astype(BF),
            "c_bc": c_bc.astype(BF), "c_Abc": c_Abc.astype(BF),
            "c_selbc": c_selbc.astype(BF)}


L3_CONSTS = [("c_Z", [128, 4], BF16),
             ("c_Zbc", [4, 128], BF16),
             ("c_bc", [4, 256], BF16),
             ("c_Abc", [128, len(OFFSETS) * 2 * 128], BF16),
             ("c_selbc", [128, 256], BF16)]


def emit_layer3(c, xin, xout, w, xnT_in=None):
    nc = c.nc
    NO = len(OFFSETS)
    with ExitStack() as L:
        if xnT_in is None:
            xnT = c.sb(L, "l3_xnT", [128, ND * T], BF16)
            emit_ln(c, xin, xnT)
        else:
            xnT = xnT_in
        g_all = c.sb(L, "l3_g", [128, NT * D], BF16)

        with ExitStack() as Sf:
            wdT = c.sb(Sf, "l3_wdT", [128, 24 * D], BF16)
            emit_load_T_chunked(c, w["L3_ffn_w_down"], D, 3072, wdT, "l3wd",
                                src_dtype=I32)
            hT = c.sb(Sf, "l3_hT", [128, 24 * (T // 2)], BF16)
            wg_st2 = [c.sb(Sf, f"l3_wgst{i}", [128, D], I32) for i in range(2)]
            wu_st2 = [c.sb(Sf, f"l3_wust{i}", [128, D], I32) for i in range(2)]
            wg_bf2 = [c.sb(Sf, f"l3_wgbf{i}", [128, D], BF16) for i in range(2)]
            wu_bf2 = [c.sb(Sf, f"l3_wubf{i}", [128, D], BF16) for i in range(2)]
            wgT2 = [c.sb(Sf, f"l3_wgT{i}", [128, D], BF16) for i in range(2)]
            wuT2 = [c.sb(Sf, f"l3_wuT{i}", [128, D], BF16) for i in range(2)]
            tmp_si = [c.sb(Sf, f"l3_tmpsi{i}", [128, T // 2], BF16)
                      for i in range(2)]
            NH = T // 2
            gsrc = w["L3_ffn_w_gate"].rearrange("(r p) c -> p r c", p=128)
            usrc = w["L3_ffn_w_up"].rearrange("(r p) c -> p r c", p=128)
            for nh in range(2):
                with ExitStack() as Sp:
                    ps_t = [c.ps(Sp, f"l3_pswt{i}", [128, 2 * D], BF16)
                            for i in range(2)]
                    ps_gate = c.ps(Sp, "l3_psga", [128, D], F32)
                    ps_up = c.ps(Sp, "l3_psup", [128, D], F32)
                    bA, bT1, bV1, bT2, bT3, bSi, bH = c.base(
                        "s0", "s1", "s2", "s3", "s4", "s5", "s6")
                    dsems = ["dxa", "dxb"]
                    bdx = [c.val["dxa"], c.val["dxb"]]
                    dvals = [bdx[cc % 2] + 32 * (cc // 2 + 1) for cc in range(24)]
                    with c.block() as block:
                        def sy(sync, nh=nh):
                            for cc in range(24):
                                if cc >= 2:
                                    sync.wait_ge(c.sem("s0"), bA + cc - 1)
                                c.dma_push(sync, wg_st2[cc % 2][:],
                                           gsrc[:, cc, :], sem=dsems[cc % 2])
                                c.dma_push(sync, wu_st2[cc % 2][:],
                                           usrc[:, cc, :], sem=dsems[cc % 2])
                        block.sync(sy)

                        def sc(scalar, nh=nh):
                            for cc in range(24):
                                scalar.wait_ge(c.sem(dsems[cc % 2]), dvals[cc])
                                if cc >= 2:
                                    scalar.wait_ge(c.sem("s1"), bT1 + cc - 1)
                                scalar.copy(wg_bf2[cc % 2][:], wg_st2[cc % 2][:])
                                scalar.copy(wu_bf2[cc % 2][:], wu_st2[cc % 2][:])
                                scalar.sig("s0")
                                if cc >= 1:
                                    scalar.wait_ge(c.sem("s3"), bT2 + cc)
                                    if cc >= 3:
                                        scalar.wait_ge(c.sem("s6"), bH + cc - 2)
                                    scalar.activation(tmp_si[(cc - 1) % 2][:],
                                                      ps_gate[:, :], AF.Silu,
                                                      scale=1.0 / 127.0)
                                    scalar.sig("s5")
                            scalar.wait_ge(c.sem("s3"), bT2 + 24)
                            scalar.wait_ge(c.sem("s6"), bH + 22)
                            scalar.activation(tmp_si[23 % 2][:], ps_gate[:, :],
                                              AF.Silu, scale=1.0 / 127.0)
                            scalar.sig("s5")
                        block.scalar(sc)

                        def tb(tensor, nh=nh):
                            for cc in range(24):
                                tensor.wait_ge(c.sem("s0"), bA + cc + 1)
                                if cc >= 2:
                                    tensor.wait_ge(c.sem("s2"), bV1 + cc - 1)
                                for dc in range(ND):
                                    tensor.transpose(
                                        ps_t[cc % 2][:, dc * 128:dc * 128 + 128],
                                        wg_bf2[cc % 2][:, dc * 128:dc * 128 + 128],
                                        c.k["c_ident_bf"][:, :])
                                    ins = tensor.transpose(
                                        ps_t[cc % 2][:, D + dc * 128:
                                                      D + dc * 128 + 128],
                                        wu_bf2[cc % 2][:, dc * 128:dc * 128 + 128],
                                        c.k["c_ident_bf"][:, :])
                                ins.then_inc(c.sem("s1"), 1)
                                tensor.wait_ge(c.sem("s2"), bV1 + cc + 1)
                                if cc >= 1:
                                    tensor.wait_ge(c.sem("s5"), bSi + cc)
                                for n0 in range(0, NH, 512):
                                    for dc in range(ND):
                                        ins = tensor.matmul(
                                            ps_gate[:, n0:n0 + 512],
                                            wgT2[cc % 2][:, dc * 128:dc * 128 + 128],
                                            xnT[:, dc * T + nh * NH + n0:
                                                dc * T + nh * NH + n0 + 512],
                                            start=dc == 0, stop=dc == ND - 1)
                                ins.then_inc(c.sem("s3"), 1)
                                if cc >= 1:
                                    tensor.wait_ge(c.sem("s6"), bH + cc)
                                for n0 in range(0, NH, 512):
                                    for dc in range(ND):
                                        ins = tensor.matmul(
                                            ps_up[:, n0:n0 + 512],
                                            wuT2[cc % 2][:, dc * 128:dc * 128 + 128],
                                            xnT[:, dc * T + nh * NH + n0:
                                                dc * T + nh * NH + n0 + 512],
                                            start=dc == 0, stop=dc == ND - 1)
                                ins.then_inc(c.sem("s4"), 1)
                        block.tensor(tb)

                        def vb(vector, nh=nh):
                            for cc in range(24):
                                vector.wait_ge(c.sem("s1"), bT1 + cc + 1)
                                if cc >= 2:
                                    vector.wait_ge(c.sem("s4"), bT3 + cc - 1)
                                vector.tensor_copy(wgT2[cc % 2][:],
                                                   ps_t[cc % 2][:, 0:D])
                                vector.tensor_copy(wuT2[cc % 2][:],
                                                   ps_t[cc % 2][:, D:2 * D])
                                vector.sig("s2")
                                vector.wait_ge(c.sem("s4"), bT3 + cc + 1)
                                vector.wait_ge(c.sem("s5"), bSi + cc + 1)
                                vector.scalar_tensor_tensor(
                                    hT[:, cc * NH: cc * NH + NH], ps_up[:, :],
                                    1.0 / 127.0, tmp_si[cc % 2][:],
                                    ALU.mult, ALU.mult)
                                vector.sig("s6")
                        block.vector(vb)
                    c.fin(s0=bA + 24, s1=bT1 + 24, s2=bV1 + 24, s3=bT2 + 24,
                          s4=bT3 + 24, s5=bSi + 24, s6=bH + 24)
                with ExitStack() as Sp:
                    ps_d = c.ps(Sp, "l3_psd", [128, 2 * D], F32)
                    bT, bA = c.base("s0", "s1")
                    with c.block() as block:
                        def tb(tensor, nh=nh):
                            for tt in range(8):
                                buf = (tt % 2) * D
                                if tt >= 2:
                                    tensor.wait_ge(c.sem("s1"), bA + tt - 1)
                                for nf in range(2):
                                    for cc in range(24):
                                        ins = tensor.matmul(
                                            ps_d[:, buf + nf * 512:
                                                 buf + nf * 512 + 512],
                                            hT[:, cc * NH + tt * 128:
                                               cc * NH + tt * 128 + 128],
                                            wdT[:, cc * D + nf * 512:
                                                cc * D + nf * 512 + 512],
                                            start=cc == 0, stop=cc == 23)
                                ins.then_inc(c.sem("s0"), 1)
                        block.tensor(tb)

                        def ab(scalar, nh=nh):
                            for tt in range(8):
                                buf = (tt % 2) * D
                                scalar.wait_ge(c.sem("s0"), bT + tt + 1)
                                t = nh * 8 + tt
                                scalar.activation(g_all[:, t * D:t * D + D],
                                                  ps_d[:, buf:buf + D],
                                                  AF.Silu, scale=1.0 / 127.0)
                                scalar.sig("s1")
                        block.scalar(ab)
                    c.fin(s0=bT + 8, s1=bA + 8)

        ggT = [c.sb(L, f"l3_ggT{rt}", [128, T], BF16) for rt in range(8)]
        with ExitStack() as Sa:
            sg_sb = c.sb(Sa, "l3_sg", [11, 16], F32)
            wqsT = c.sb(Sa, "l3_wqsT", [128, 11], BF16)
            sp_cols = c.sb(Sa, "l3_spc", [4, 4], F32)
            with ExitStack() as Ss:
                row_st = c.sb(Ss, "l3_rowst", [1, 3072], F32)
                row_bf = c.sb(Ss, "l3_rowbf", [1, 3072], BF16)
                with c.block() as block:
                    def sy(sync):
                        c.dma_push(sync, sg_sb[:, :], w["L3_attn_scale_gain"])
                    block.sync(sy)
                    dtot = c.val["dma"]

                    def sc(scalar):
                        scalar.wait_ge(c.sem("dma"), dtot)
                    block.scalar(sc)
                wqs_st = c.sb(Ss, "l3_wqsst", [11, 64], F32)
                wqs_bf = c.sb(Ss, "l3_wqsbf", [11, 64], BF16)
                with c.block() as block:
                    def sy(sync):
                        c.dma_push(sync, wqs_st[:, :], w["L3_attn_W_qscale"])
                    block.sync(sy)
                    dtot = c.val["dma"]

                    def sc(scalar):
                        scalar.wait_ge(c.sem("dma"), dtot)
                        scalar.copy(wqs_bf[:, :], wqs_st[:, :])
                    block.scalar(sc)
                with ExitStack() as Sp:
                    ps_w = c.ps(Sp, "l3_psqs", [64, 11], BF16)
                    with c.block() as block:
                        def tb(tensor):
                            tensor.transpose(ps_w[:, :], wqs_bf[:, :],
                                             c.k["c_ident_bf"][0:11, 0:11])
                        block.tensor(tb)
                    with c.block() as block:
                        def vb(vector):
                            vector.tensor_copy(wqsT[0:64, :], ps_w[:, :])
                        block.vector(vb)
                emit_row_load(c, w["L3_attn_identity_bypass"], 16, row_st)
                sp_row = c.sb(Ss, "l3_sprow", [1, 16], F32)
                with c.block() as block:
                    def vb(vector):
                        vector.scalar_tensor_tensor(sp_row[0:1, :],
                                                    row_st[0:1, 0:16], -1.0,
                                                    row_st[0:1, 0:16],
                                                    ALU.mult, ALU.max)
                    block.vector(vb)
                with c.block() as block:
                    def ab(scalar):
                        scalar.activation(sp_row[0:1, :], sp_row[0:1, :],
                                          AF.Exp, scale=-1.0)
                        scalar.activation(sp_row[0:1, :], sp_row[0:1, :],
                                          AF.Ln, bias=1.0)
                    block.scalar(ab)
                with c.block() as block:
                    def vb(vector):
                        vector.tensor_scalar_max(row_st[0:1, 16:32],
                                                 row_st[0:1, 0:16], 0.0)
                        vector.tensor_add(sp_row[0:1, :], sp_row[0:1, :],
                                          row_st[0:1, 16:32])
                    block.vector(vb)
                with ExitStack() as Sp:
                    ps_sp = c.ps(Sp, "l3_pssp", [4, 4], F32)
                    with c.block() as block:
                        def tb(tensor):
                            for g in range(4):
                                tensor.transpose(
                                    ps_sp[:, g:g + 1],
                                    sp_row[0:1, 4 * g:4 * g + 4],
                                    c.k["c_ident_f32"][0:1, 0:1])
                        block.tensor(tb)
                    with c.block() as block:
                        def vb(vector):
                            vector.tensor_copy(sp_cols[:, :], ps_sp[:, :])
                        block.vector(vb)

            wst4 = [c.sb(Sa, f"l3_wst{i}", [128, D], F32) for i in range(3)]
            wrow_bf2 = [c.sb(Sa, f"l3_wrbf{i}", [128, D], BF16)
                        for i in range(2)]
            wrowT2 = [c.sb(Sa, f"l3_wrT{i}", [128, D], BF16) for i in range(2)]
            bqkv_col = c.sb(Sa, "l3_bqc", [128, 24], F32)
            bg_col = c.sb(Sa, "l3_bgc", [128, 8], F32)
            sp_bc_sb = c.sb(Sa, "l3_spbc", [128, 8], F32)
            c.k["c_Abc"] = c.sb(Sa, "l3_cAbc", [128, len(OFFSETS) * 2 * 128],
                                BF16)
            with c.block() as block:
                def sy(sync):
                    c.dma_push(sync, c.k["c_Abc"][:], c.kd["c_Abc"])
                    sync.wait_ge(c.sem("dma"), c.val["dma"])
                block.sync(sy)
            wqkv4 = w["L3_attn_W_qkv"].rearrange("(o p) c -> p o c", p=128)
            wgate4 = w["L3_attn_W_gate"].rearrange("(o p) c -> p o c", p=128)

            TILES = []
            for g in range(4):
                for i in range(2):
                    rt = 2 * g + i
                    TILES += [(wqkv4, rt, "q", i), (wqkv4, 8 + rt, "k", i),
                              (wqkv4, 16 + rt, "v", i)]
                TILES += [(wgate4, 2 * g, "gate", 0),
                          (wgate4, 2 * g + 1, "gate", 1)]
            QUEUES3 = ["dxa", "dxb", "dxc", "dxd"]
            b10g, b11g, b12g, b13g, b8g = c.base("s10", "s11", "s12", "s13",
                                                 "s8")
            dsnap = {}

            def front_dma(sync, k):
                wsrc4, ot, kind, i = TILES[k]
                if k >= 3:
                    sync.wait_ge(c.sem("s10"), b10g + k - 2)
                c.dma_push(sync, wst4[k % 3][:].rearrange(
                    "p (o cw) -> p o cw", o=1), wsrc4[:, ot:ot + 1, :],
                    sem=QUEUES3[k % 4])
                dsnap[k] = c.val[QUEUES3[k % 4]]

            with c.block() as block:
                def sy(sync):
                    for k in range(3):
                        front_dma(sync, k)
                block.sync(sy)

                def gp(gpsimd):
                    for ot in range(24):
                        c.dma_push(gpsimd, bqkv_col[0:128, ot:ot + 1],
                                   w["L3_attn_b_qkv"][ot * 128:ot * 128 + 128]
                                   .rearrange("(p n) -> p n", n=1), sem="dma2")
                    for ot in range(8):
                        c.dma_push(gpsimd, bg_col[0:128, ot:ot + 1],
                                   w["L3_attn_b_gate"][ot * 128:ot * 128 + 128]
                                   .rearrange("(p n) -> p n", n=1), sem="dma2")
                    c.dma_push(gpsimd, wqsT[64:128, :], wqsT[0:64, :],
                               sem="dma2")
                    sph = sp_cols[:].tensor
                    for g4 in range(4):
                        for i in range(2):
                            src_ap = bass.AP(sph, 2 * i * 4 + g4,
                                             [[4, 2], [0, 64]])
                            c.dma_push(gpsimd,
                                       sp_bc_sb[0:128, 2 * g4 + i:
                                                2 * g4 + i + 1],
                                       src_ap, sem="dma2")
                    gpsimd.wait_ge(c.sem("dma2"), c.val["dma2"])
                block.gpsimd(gp)

            def emit_front_group(g, dsts):
                k0 = g * 8
                with ExitStack() as Sp:
                    ps_t = c.ps(Sp, "l3_pst", [128, D], BF16)
                    ps_p = [c.ps(Sp, f"l3_psp{i}", [128, 512], F32)
                            for i in range(2)]
                    with c.block() as block:
                        def sy(sync):
                            for k in range(k0 + 3, min(32, k0 + 11)):
                                front_dma(sync, k)
                        block.sync(sy)

                        def sc(scalar):
                            for k in range(k0, k0 + 8):
                                wsrc4, ot, kind, i = TILES[k]
                                scalar.wait_ge(c.sem(QUEUES3[k % 4]), dsnap[k])
                                if k >= 2:
                                    scalar.wait_ge(c.sem("s13"), b13g + k - 1)
                                scalar.copy(wrow_bf2[k % 2][:], wst4[k % 3][:])
                                scalar.sig("s10")
                                bias = bqkv_col[:, ot:ot + 1] if kind != "gate" \
                                    else bg_col[:, ot:ot + 1]
                                func = AF.Sigmoid if kind == "gate" \
                                    else AF.Identity
                                for n0 in range(4):
                                    scalar.wait_ge(c.sem("s11"),
                                                   b11g + 4 * k + n0 + 1)
                                    scalar.activation(
                                        dsts[k - k0][:, n0 * 512:n0 * 512 + 512],
                                        ps_p[(4 * k + n0) % 2][:, :], func,
                                        bias=bias)
                                    scalar.sig("s12")
                        block.scalar(sc)

                        def tb(tensor):
                            for k in range(k0, k0 + 8):
                                tensor.wait_ge(c.sem("s10"), b10g + k + 1)
                                if k > 0:
                                    tensor.wait_ge(c.sem("s8"), b8g + k)
                                for dc in range(ND):
                                    ins = tensor.transpose(
                                        ps_t[:, dc * 128:dc * 128 + 128],
                                        wrow_bf2[k % 2][:, dc * 128:dc * 128 + 128],
                                        c.k["c_ident_bf"][:, :])
                                ins.then_inc(c.sem("s13"), 1)
                                tensor.wait_ge(c.sem("s8"), b8g + k + 1)
                                for n0 in range(4):
                                    if 4 * k + n0 >= 2:
                                        tensor.wait_ge(c.sem("s12"),
                                                       b12g + 4 * k + n0 - 1)
                                    for dc in range(ND):
                                        ins = tensor.matmul(
                                            ps_p[(4 * k + n0) % 2][:, :],
                                            wrowT2[k % 2][:, dc * 128:dc * 128 + 128],
                                            xnT[:, dc * T + n0 * 512:
                                                dc * T + n0 * 512 + 512],
                                            start=dc == 0, stop=dc == ND - 1)
                                    ins.then_inc(c.sem("s11"), 1)
                        block.tensor(tb)

                        def vb(vector):
                            for k in range(k0, k0 + 8):
                                vector.wait_ge(c.sem("s13"), b13g + k + 1)
                                if k >= 2:
                                    vector.wait_ge(c.sem("s11"),
                                                   b11g + 4 * (k - 1))
                                vector.tensor_copy(wrowT2[k % 2][:], ps_t[:])
                                vector.sig("s8")
                        block.vector(vb)
                c.fin(s10=b10g + k0 + 8, s11=b11g + 4 * (k0 + 8),
                      s12=b12g + 4 * (k0 + 8), s13=b13g + k0 + 8,
                      s8=b8g + k0 + 8)


            qG = [c.sb(Sa, f"l3_q{i}", [128, T], BF16) for i in range(2)]
            kG = [c.sb(Sa, f"l3_k{i}", [128, T], BF16) for i in range(2)]
            vG = [c.sb(Sa, f"l3_v{i}", [128, T], BF16) for i in range(2)]
            gateG = [c.sb(Sa, f"l3_gate{i}", [128, T], BF16) for i in range(2)]
            eG = c.sb(Sa, "l3_eG", [128, T], BF16)
            invZ = c.sb(Sa, "l3_invZ", [4, T], BF16)
            prod = [c.sb(Sa, f"l3_prod{i}", [128, T], BF16) for i in range(2)]
            QKbf = c.sb(Sa, "l3_QKbf", [128, T], BF16)
            tmpm = [c.sb(Sa, f"l3_tmpm{i}", [128, T], BF16) for i in range(2)]
            accG = [c.sb(Sa, f"l3_acc{i}", [128, T], F32) for i in range(2)]

            for g in range(4):
                emit_front_group(g, [qG[0], kG[0], vG[0], qG[1], kG[1], vG[1],
                                     gateG[0], gateG[1]])
                with ExitStack() as Sp:
                    ps_g = c.ps(Sp, "l3_psg", [128, T], F32)
                    bQO, bEX, bLZ, bMS = c.base("s0", "s1", "s2", "s3")
                    with c.block() as block:
                        def tbg(tensor, g=g):
                            for hl in range(4):
                                i, sub = hl // 2, (hl % 2) * 64
                                if hl > 0:
                                    tensor.wait_ge(c.sem("s1"), bEX + hl)
                                for n0 in range(0, T, 512):
                                    ins = tensor.matmul(
                                        ps_g[0:11, n0:n0 + 512],
                                        wqsT[sub:sub + 64, :],
                                        qG[i][sub:sub + 64, n0:n0 + 512],
                                        start=True, stop=True)
                                ins.then_inc(c.sem("s0"), 1)
                            tensor.wait_ge(c.sem("s1"), bEX + 4)
                            for n0 in range(0, T, 512):
                                ins = tensor.matmul(ps_g[0:4, n0:n0 + 512],
                                                    c.k["c_Z"][:, :],
                                                    eG[:, n0:n0 + 512],
                                                    start=True, stop=True)
                            ins.then_inc(c.sem("s0"), 1)
                            tensor.wait_ge(c.sem("s2"), bLZ + 1)
                            for n0 in range(0, T, 512):
                                ins = tensor.matmul(ps_g[:, n0:n0 + 512],
                                                    c.k["c_Zbc"][:, :],
                                                    invZ[0:4, n0:n0 + 512],
                                                    start=True, stop=True)
                            ins.then_inc(c.sem("s0"), 1)
                        block.tensor(tbg)

                        def abg(scalar, g=g):
                            for hl in range(4):
                                h = 4 * g + hl
                                scalar.wait_ge(c.sem("s0"), bQO + hl + 1)
                                if hl == 0:
                                    scalar.wait_ge(c.sem("s3"), bMS + 1)
                                scalar.activation(
                                    eG[hl * 32: hl * 32 + 11, :],
                                    ps_g[0:11, :], AF.Exp,
                                    bias=sg_sb[:, h:h + 1])
                                scalar.sig("s1")
                            scalar.wait_ge(c.sem("s0"), bQO + 5)
                            scalar.activation(invZ[:, :], ps_g[0:4, :], AF.Ln)
                            scalar.activation(invZ[:, :], invZ[:, :], AF.Exp,
                                              scale=-1.0)
                            scalar.sig("s2")
                        block.scalar(abg)

                        def vbg(vector, g=g):
                            vector.memset(eG[:, :], 0.0)
                            vector.sig("s3")
                            vector.wait_ge(c.sem("s0"), bQO + 6)
                            vector.tensor_mul(eG[:, :], eG[:, :], ps_g[:, :])
                        block.vector(vbg)
                    c.fin(s0=bQO + 6, s1=bEX + 4, s2=bLZ + 1, s3=bMS + 1)

                with ExitStack() as Sp:
                    ps_QK = c.ps(Sp, "l3_psQK", [128, T], F32)
                    ps_A = c.ps(Sp, "l3_psA", [128, T], F32)
                    NOo = len(OFFSETS)
                    bP0, bQK, bE3, bC, bM, bP, bAm, bQE = c.base(
                        "s0", "s1", "s2", "s4", "s6", "s7", "s3", "s5")
                    with c.block() as block:
                        def emit_prods(vector, oi):
                            o = OFFSETS[oi]
                            if oi > 0:
                                vector.wait_ge(c.sem("s1"), bQK + 2 * oi)
                            for i in range(2):
                                if o > 0:
                                    vector.memset(prod[i][:, 0:o], 0.0)
                                vector.tensor_mul(prod[i][:, o:T],
                                                  qG[i][:, o:T],
                                                  kG[i][:, 0:T - o])
                            vector.sig("s0")

                        def vb(vector, g=g):
                            emit_prods(vector, 0)
                            for oi, o in enumerate(OFFSETS):
                                for i in range(2):
                                    k = 2 * oi + i
                                    if i == 1 and oi + 1 < len(OFFSETS):
                                        emit_prods(vector, oi + 1)
                                    vector.wait_ge(c.sem("s2"), bE3 + k + 1)
                                    vector.wait_ge(c.sem("s5"), bQE + k + 1)
                                    vector.tensor_mul(tmpm[i][:, :],
                                                      tmpm[i][:, :],
                                                      QKbf[:, :])
                                    vector.sig("s4")
                                    if o == 0:
                                        vector.tensor_mul(tmpm[i][:, :],
                                                          tmpm[i][:, :],
                                                          vG[i][:, :])
                                    else:
                                        vector.tensor_mul(tmpm[i][:, o:T],
                                                          tmpm[i][:, o:T],
                                                          vG[i][:, 0:T - o])
                                    vector.sig("s6")
                        block.vector(vb)

                        def ab(scalar, g=g):
                            for oi, o in enumerate(OFFSETS):
                                for i in range(2):
                                    k = 2 * oi + i
                                    scalar.wait_ge(c.sem("s1"), bQK + k + 1)
                                    if k > 0:
                                        scalar.wait_ge(c.sem("s4"), bC + k)
                                    scalar.copy(QKbf[:, :], ps_QK[:, :])
                                    scalar.sig("s5")
                                    scalar.wait_ge(c.sem("s3"), bAm + k + 1)
                                    if k >= 2:
                                        scalar.wait_ge(c.sem("s7"),
                                                       bP + k - 1)
                                    if o == 0:
                                        scalar.activation(
                                            tmpm[i][:, :], ps_A[:, :],
                                            AF.Identity,
                                            bias=sp_bc_sb[:, 2 * g + i:
                                                          2 * g + i + 1])
                                    else:
                                        scalar.copy(tmpm[i][:, :],
                                                    ps_A[:, :])
                                    scalar.sig("s2")
                        block.scalar(ab)

                        def tb(tensor, g=g):
                            for oi, o in enumerate(OFFSETS):
                                for i in range(2):
                                    k = 2 * oi + i
                                    if k > 0:
                                        tensor.wait_ge(c.sem("s2"), bE3 + k)
                                    for n0 in range(0, T, 512):
                                        ins = tensor.matmul(
                                            ps_A[:, n0:n0 + 512],
                                            c.k["c_Abc"][:, (oi * 2 + i) * 128:
                                                         (oi * 2 + i) * 128 + 128],
                                            eG[:, n0:n0 + 512],
                                            start=True, stop=True)
                                    ins.then_inc(c.sem("s3"), 1)
                                    if i == 0:
                                        tensor.wait_ge(c.sem("s0"),
                                                       bP0 + oi + 1)
                                    if k > 0:
                                        tensor.wait_ge(c.sem("s5"), bQE + k)
                                    for n0 in range(0, T, 512):
                                        ins = tensor.matmul(
                                            ps_QK[:, n0:n0 + 512],
                                            c.k["c_selbc"][:, i * 128:
                                                           i * 128 + 128],
                                            prod[i][:, n0:n0 + 512],
                                            start=True, stop=True)
                                    ins.then_inc(c.sem("s1"), 1)
                        block.tensor(tb)

                        def gy(gpsimd, g=g):
                            for oi, o in enumerate(OFFSETS):
                                for i in range(2):
                                    k = 2 * oi + i
                                    gpsimd.wait_ge(c.sem("s6"), bM + k + 1)
                                    if oi == 0:
                                        gpsimd.tensor_copy(accG[i][:, :],
                                                           tmpm[i][:, :])
                                    else:
                                        gpsimd.tensor_add(accG[i][:, o:T],
                                                          accG[i][:, o:T],
                                                          tmpm[i][:, o:T])
                                    gpsimd.sig("s7")
                        block.gpsimd(gy)
                    c.fin(s0=bP0 + NOo, s1=bQK + 2 * NOo, s2=bE3 + 2 * NOo,
                          s4=bC + 2 * NOo, s6=bM + 2 * NOo, s7=bP + 2 * NOo,
                          s3=bAm + 2 * NOo, s5=bQE + 2 * NOo)

                with c.block() as block:
                    def vb(vector, g=g):
                        for i in range(2):
                            vector.tensor_mul(ggT[2 * g + i][:, :], accG[i][:, :],
                                              gateG[i][:, :])
                    block.vector(vb)

        woutT = c.sb(L, "l3_woutT", [128, ND * D], BF16)
        bout_bf = c.sb(L, "l3_bout", [1, D], BF16)
        with ExitStack() as Sw:
            row_st = c.sb(Sw, "l3_rowst2", [1, D], F32)
            emit_row_load(c, w["L3_attn_b_out"], D, row_st, bout_bf)
            emit_load_T_chunked(c, w["L3_attn_W_out"], D, D, woutT, "l3wo")

        def s_psum(tensor, t, psum_ap):
            for nf in range(2):
                for rt in range(8):
                    tensor.matmul(
                        psum_ap[:, nf * 512: nf * 512 + 512],
                        ggT[rt][:, t * 128: t * 128 + 128],
                        woutT[:, rt * D + nf * 512: rt * D + nf * 512 + 512],
                        start=rt == 0, stop=False)
                ins = tensor.matmul(
                    psum_ap[:, nf * 512: nf * 512 + 512],
                    c.k["c_ones_bf"][0:1, 0:128],
                    bout_bf[0:1, nf * 512: nf * 512 + 512],
                    start=False, stop=True)
            return ins

        emit_combine(c, xin, xout, s_psum, g_all)


L0_WEIGHTS = {
    "L0_attn_Wp": [D, 40], "L0_attn_bp": [40], "L0_attn_Wv": [D, D],
    "L0_attn_Wo": [D, D], "L0_ffn_weight": [D, D], "L0_ffn_bias": [D],
}
L1_WEIGHTS = {
    "L1_attn_Wkqv": [16, D, 2112], "L1_attn_bkqv": [16, 2112],
    "L1_attn_Wp": [D, D], "L1_attn_bp": [D],
    "L1_ffn_weight": [D, D], "L1_ffn_gate_w": [8, D],
    "L1_ffn_lora_U": [8, 16, D], "L1_ffn_lora_V": [8, D, 16],
}


L2_WEIGHTS = {
    "L2_attn_Wq": [D, 512], "L2_attn_Wkv": [D, 128], "L2_attn_Wo": [512, D],
    "L2_attn_norm_w": [D], "L2_attn_norm_b": [D],
    "L2_attn_out_norm_w": [D], "L2_attn_out_norm_b": [D],
    "L2_ffn_W_r": [D, D], "L2_ffn_W_k": [D, D], "L2_ffn_W_v": [D, D],
    "L2_ffn_mu_r": [D], "L2_ffn_mu_k": [D],
}
L3_WEIGHTS = {
    "L3_attn_W_qkv": [3072, D], "L3_attn_b_qkv": [3072],
    "L3_attn_W_out": [D, D], "L3_attn_b_out": [D],
    "L3_attn_W_gate": [D, D], "L3_attn_b_gate": [D],
    "L3_attn_scale_gain": [11, 16], "L3_attn_W_qscale": [11, 64],
    "L3_attn_identity_bypass": [16],
    "L3_ffn_w_gate": ([3072, D], I32), "L3_ffn_w_up": ([3072, D], I32),
    "L3_ffn_w_down": ([D, 3072], I32),
}

LAYER_WEIGHTS = {0: list(L0_WEIGHTS), 1: list(L1_WEIGHTS), 2: list(L2_WEIGHTS),
                 3: list(L3_WEIGHTS)}


def extra_host_inputs(lid):
    if lid != 3:
        return {}
    allc = l3_host_constants()
    names = {n for n, _, _ in L3_CONSTS}
    return {k: v for k, v in allc.items() if k in names}


def build_layer_program(layer_ids):
    nc = bass.Bass("TRN2", target_bir_lowering=False, debug=False)
    with ExitStack() as stack:
        c = C(nc, stack)
        xin = nc.dram_tensor("x", [T, D], F32, kind="ExternalInput").ap()
        xout = nc.dram_tensor("x_out", [T, D], F32, kind="ExternalOutput").ap()
        declare_consts(c, extra=L3_CONSTS if 3 in layer_ids else ())
        emitters = {0: (emit_layer0, L0_WEIGHTS), 1: (emit_layer1, L1_WEIGHTS),
                    2: (emit_layer2, L2_WEIGHTS), 3: (emit_layer3, L3_WEIGHTS)}
        fuse_layers = len(layer_ids) > 1
        if fuse_layers:
            xnT_hand = c.sb(stack, "xnT_hand", [128, ND * T], BF16)
            lnst_hand = c.sb(stack, "lnst_hand", [128, 2 * NT], F32)
        cur_in = xin
        fused_prev = False
        for i, lid in enumerate(layer_ids):
            emit, wshapes = emitters[lid]
            w = {}
            for k, sh in wshapes.items():
                sh, dt = sh if isinstance(sh, tuple) else (sh, F32)
                w[k] = nc.dram_tensor(k, sh, dt, kind="ExternalInput").ap()
            out_ap = xout if i == len(layer_ids) - 1 else \
                nc.dram_tensor(f"x_mid{i}", [T, D], F32, kind="Internal").ap()
            nxt = layer_ids[i + 1] if i + 1 < len(layer_ids) else None
            kw = {}
            if lid in (1, 2, 3) and fused_prev:
                kw["xnT_in"] = xnT_hand
                if lid == 2:
                    kw["lnstats_in"] = lnst_hand
            if lid in (0, 1, 2) and nxt in (1, 2, 3):
                kw["fuse_next"] = (xnT_hand,
                                   lnst_hand if nxt == 2 else None)
                fused_prev = True
            else:
                fused_prev = False
            emit(c, cur_in, out_ap, w, **kw)
            cur_in = out_ap
    return nc


def kernel(**inputs):
    consts = host_constants()
    consts.update(l3_host_constants())
    x = np.ascontiguousarray(inputs["L0_attn_x"], dtype=np.float32)
    nc = build_layer_program([0, 1, 2, 3])
    wnames = [k for lid in range(4) for k in LAYER_WEIGHTS[lid]]
    in_maps = []
    for b in range(8):
        m = {"x": x[b]}
        for k in wnames:
            m[k] = np.ascontiguousarray(inputs[k])
        m.update(consts)
        in_maps.append(m)
    res = run_bass_kernel_spmd(nc, in_maps, core_ids=list(range(8)))
    return np.stack([r["x_out"] for r in res.results], axis=0)


# version 37
# speedup vs baseline: 1.0148x; 1.0148x over previous
import numpy as np
import ml_dtypes
from contextlib import ExitStack

import concourse.bass as bass
import concourse.mybir as mybir
from concourse.bass_utils import run_bass_kernel_spmd

F32 = mybir.dt.float32
BF16 = mybir.dt.bfloat16
I32 = mybir.dt.int32
AF = mybir.ActivationFunctionType
ALU = mybir.AluOpType
AX = mybir.AxisListType

T = 2048
D = 1024
NT = T // 128
ND = D // 128
NEG = -1.0e30

BF = ml_dtypes.bfloat16


def cdiv(a, b):
    return (a + b - 1) // b


def host_constants():
    ident = np.eye(128)
    mask = np.where(np.arange(128)[None, :] > np.arange(128)[:, None], NEG, 0.0)
    return {
        "c_ident_bf": ident.astype(BF),
        "c_ident_f32": ident.astype(np.float32),
        "c_mask": mask.astype(np.float32),
        "c_ones_bf": np.ones((1, 512), dtype=BF),
    }


CONST_SHAPES = {"c_ident_bf": ([128, 128], BF16), "c_ident_f32": ([128, 128], F32),
                "c_mask": ([128, 128], F32), "c_ones_bf": ([1, 512], BF16)}


class C:

    def __init__(self, nc, stack):
        self.nc = nc
        self.stack = stack
        self.val = {}
        self.sems = {}
        for name in ["dma", "dma2", "dxa", "dxb", "doa", "dob",
                     "self_DVE", "self_ACT", "self_POOL",
                     "s0", "s1", "s2", "s3", "s4", "s5", "s6", "s7",
                     "s8", "s9", "s10", "s11", "s12", "s13",
                     "dxc", "dxd"]:
            self.sems[name] = stack.enter_context(nc.semaphore(name))
            self.val[name] = 0
        self.k = {}
        self.last_ins = {}
        self.inc_used = set()
        self._uid = 0

    def uid(self):
        self._uid += 1
        return self._uid

    def sb(self, scope, name, shape, dtype):
        return scope.enter_context(
            self.nc.sbuf_tensor(f"{name}_{self.uid()}", shape, dtype))

    def ps(self, scope, name, shape, dtype=F32):
        return scope.enter_context(
            self.nc.psum_tensor(f"{name}_{self.uid()}", shape, dtype))

    def sem(self, name):
        return self.sems[name]

    def dma_push(self, eng, dst, src, sem="dma"):
        self.val[sem] += 16
        eng.dma_start(dst, src).then_inc(self.sems[sem], 16)
        return self.val[sem]

    def base(self, *names):
        return [self.val[n] for n in names]

    def fin(self, **upd):
        for k, v in upd.items():
            self.val[k] = v

    def block(self):
        from contextlib import contextmanager

        @contextmanager
        def _mk():
            with self.nc.Block() as b:
                yield BlockProxy(self, b)
        return _mk()



SELF_SYNC = True
EMIT_OPS = {
    "tensor_copy", "tensor_tensor", "tensor_add", "tensor_sub", "tensor_mul",
    "tensor_max", "tensor_scalar", "tensor_single_scalar", "tensor_scalar_add",
    "tensor_scalar_sub", "tensor_scalar_mul", "tensor_scalar_max",
    "tensor_scalar_min", "tensor_relu", "tensor_reduce", "reduce_max",
    "reduce_sum", "scalar_tensor_tensor", "tensor_tensor_scan", "select",
    "reciprocal", "memset", "memzero", "activation", "copy", "square", "sqrt",
    "mul", "add", "sign", "iota", "partition_broadcast", "affine_select",
}


class EngWrap:
    def __init__(self, c, eng, key):
        self._c, self._e, self._k = c, eng, key

    def sig(self, sem_name, n=1):
        if SELF_SYNC:
            cnt = self._c.val[self._k]
            if cnt > 0:
                self._e.wait_ge(self._c.sems[self._k], cnt)
            self._e.sem_inc(self._c.sems[sem_name], n)
        else:
            ins = self._c.last_ins.get(self._k)
            if ins is not None and id(ins) not in self._c.inc_used:
                self._c.inc_used.add(id(ins))
                ins.then_inc(self._c.sems[sem_name], n)
            else:
                self._e.drain(fusable=True)
                self._e.sem_inc(self._c.sems[sem_name], n)

    def __getattr__(self, attr):
        f = getattr(self._e, attr)
        if attr in EMIT_OPS:
            c, k = self._c, self._k
            def wrap(*a, **kw):
                ins = f(*a, **kw)
                if SELF_SYNC:
                    n = c.val[k]
                    if n > 0:
                        ins._wait_ge(c.sems[k], n)
                    c.val[k] = n + 1
                    ins.then_inc(c.sems[k], 1)
                else:
                    c.last_ins[k] = ins
                return ins
            return wrap
        return f


class BlockProxy:
    def __init__(self, c, b):
        self._c, self._b = c, b

    def vector(self, f):
        return self._b.vector(lambda e: f(EngWrap(self._c, e, "self_DVE")))

    def scalar(self, f):
        return self._b.scalar(lambda e: f(EngWrap(self._c, e, "self_ACT")))

    def gpsimd(self, f):
        return self._b.gpsimd(lambda e: f(EngWrap(self._c, e, "self_POOL")))

    def tensor(self, f):
        return self._b.tensor(f)

    def sync(self, f):
        return self._b.sync(f)


DEFERRED_CONSTS = {"c_Abc"}


def declare_consts(c, extra=()):
    nc = c.nc
    items = [(n,) + CONST_SHAPES[n] for n in CONST_SHAPES] + list(extra)
    drams = {}
    c.kd = {}
    for n, sh, dt in items:
        drams[n] = nc.dram_tensor(n, sh, dt, kind="ExternalInput").ap()
        if n in DEFERRED_CONSTS:
            c.kd[n] = drams[n]
            continue
        c.k[n] = c.stack.enter_context(nc.sbuf_tensor(n + "_s", sh, dt))
    with c.block() as block:
        def body(sync):
            for n, sh, dt in items:
                if n in DEFERRED_CONSTS:
                    continue
                c.dma_push(sync, c.k[n][:], drams[n])
            sync.wait_ge(c.sem("dma"), c.val["dma"])
        block.sync(body)


def emit_load_natural(c, dram_ap, R, Cc, dst_bf, stage_f32):
    nc = c.nc
    nr = R // 128
    src = dram_ap.rearrange("(r p) c -> p r c", p=128)
    with c.block() as block:
        def sy(sync):
            c.dma_push(sync, stage_f32[:, 0:nr * Cc].rearrange(
                "p (r c) -> p r c", r=nr), src)
        block.sync(sy)
        dtot = c.val["dma"]

        def sc(scalar):
            scalar.wait_ge(c.sem("dma"), dtot)
            scalar.copy(dst_bf[:, 0:nr * Cc], stage_f32[:, 0:nr * Cc])
        block.scalar(sc)


def emit_load_natural_chunked(c, dram_ap, R, Cc, dst_bf, name="wnc"):
    nc = c.nc
    nr = R // 128
    src = dram_ap.rearrange("(r p) c -> p r c", p=128)
    with ExitStack() as S:
        st = [c.sb(S, f"{name}_st{i}", [128, Cc], F32) for i in range(2)]
        bA = c.val["s10"]
        dsems = ["dxa", "dxb"]
        bdx = [c.val["dxa"], c.val["dxb"]]
        dvals = [bdx[r % 2] + 16 * (r // 2 + 1) for r in range(nr)]
        with c.block() as block:
            def sy(sync):
                for r in range(nr):
                    if r >= 2:
                        sync.wait_ge(c.sem("s10"), bA + r - 1)
                    c.dma_push(sync, st[r % 2][:, :], src[:, r, :],
                               sem=dsems[r % 2])
            block.sync(sy)

            def sc(scalar):
                for r in range(nr):
                    scalar.wait_ge(c.sem(dsems[r % 2]), dvals[r])
                    scalar.copy(dst_bf[:, r * Cc:(r + 1) * Cc], st[r % 2][:, :])
                    scalar.sig("s10")
            block.scalar(sc)
        c.fin(s10=bA + nr)


def emit_transpose_natural_to_T(c, src_bf, R, Cc, dstT_bf, ps_tr):
    nc = c.nc
    nr, ncc = R // 128, Cc // 128
    bT, bV = c.base("s6", "s7")
    pairs = [(r, cc) for r in range(nr) for cc in range(ncc)]
    with c.block() as block:
        def tb(tensor):
            for k, (r, cc) in enumerate(pairs):
                if k > 0:
                    tensor.wait_ge(c.sem("s7"), bV + k)
                tensor.transpose(ps_tr[:, 0:128],
                                 src_bf[:, r * Cc + cc * 128: r * Cc + cc * 128 + 128],
                                 c.k["c_ident_bf"][:, :]).then_inc(c.sem("s6"), 1)
        block.tensor(tb)

        def vb(vector):
            for k, (r, cc) in enumerate(pairs):
                vector.wait_ge(c.sem("s6"), bT + k + 1)
                vector.tensor_copy(
                    dstT_bf[:, cc * R + r * 128: cc * R + r * 128 + 128],
                    ps_tr[:, 0:128])
                vector.sig("s7")
        block.vector(vb)
    c.fin(s6=bT + len(pairs), s7=bV + len(pairs))


def emit_row_load(c, dram_ap, n_elem, stage_row_f32, dst_row_bf=None):
    nc = c.nc
    with c.block() as block:
        def sy(sync):
            c.dma_push(sync, stage_row_f32[0:1, 0:n_elem],
                       dram_ap.rearrange("(o n) -> o n", o=1))
        block.sync(sy)
        dtot = c.val["dma"]
        if dst_row_bf is not None:
            def sc(scalar):
                scalar.wait_ge(c.sem("dma"), dtot)
                scalar.copy(dst_row_bf[0:1, 0:n_elem], stage_row_f32[0:1, 0:n_elem])
            block.scalar(sc)
        else:
            def sy2(sync):
                sync.wait_ge(c.sem("dma"), dtot)
            block.sync(sy2)


def emit_row_load_cast(c, dram_ap, n_elem, dst_bf):
    nc = c.nc
    with c.block() as block:
        def gy(gpsimd):
            c.dma_push(gpsimd, dst_bf[0:1, 0:n_elem],
                       dram_ap.rearrange("(o n) -> o n", o=1), sem="dma2")
            gpsimd.wait_ge(c.sem("dma2"), c.val["dma2"])
        block.gpsimd(gy)


def emit_col_from_row(c, row_f32, n, dst_col_f32, scale=1.0):
    nc = c.nc
    with ExitStack() as P:
        ps_col = c.ps(P, "pscol", [n, 1], F32)
        with c.block() as block:
            def tb(tensor):
                tensor.transpose(ps_col[0:n, 0:1], row_f32[0:1, 0:n],
                                 c.k["c_ident_f32"][0:1, 0:1])
            block.tensor(tb)
        with c.block() as block:
            def vb(vector):
                if scale == 1.0:
                    vector.tensor_copy(dst_col_f32[0:n, 0:1], ps_col[0:n, 0:1])
                else:
                    vector.tensor_scalar_mul(dst_col_f32[0:n, 0:1],
                                             ps_col[0:n, 0:1], scale)
            block.vector(vb)


def emit_load_T_chunked(c, dram_ap, R, Cc, dstT_bf, name="wtc", src_dtype=F32, CH=512):
    nc = c.nc
    nr, ncc = R // 128, Cc // 128
    nch = Cc // CH
    ntile = CH // 128
    src = dram_ap.rearrange("(r p) c -> p r c", p=128)
    jobs = [(r, ch) for r in range(nr) for ch in range(nch)]
    nj = len(jobs)
    with ExitStack() as S:
        st = [c.sb(S, f"{name}_st{i}", [128, CH], src_dtype) for i in range(4)]
        bf = [c.sb(S, f"{name}_bf{i}", [128, CH], BF16) for i in range(2)]
        ps_tr = [c.ps(S, f"{name}_ps{i}", [128, CH], BF16) for i in range(2)]
        bA, bT, bV = c.base("s0", "s1", "s2")
        dsems = ["dxa", "dxb", "dxc", "dxd"]
        bdx = [c.val[s] for s in dsems]
        dvals = [bdx[j % 4] + 16 * (j // 4 + 1) for j in range(nj)]
        with c.block() as block:
            def sy(sync):
                for j in range(nj):
                    r, ch = jobs[j]
                    if j >= 4:
                        sync.wait_ge(c.sem("s0"), bA + j - 3)
                    c.dma_push(sync, st[j % 4][:, :],
                               src[:, r, ch * CH:(ch + 1) * CH], sem=dsems[j % 4])
            block.sync(sy)

            def sc(scalar):
                for j in range(nj):
                    scalar.wait_ge(c.sem(dsems[j % 4]), dvals[j])
                    if j >= 2:
                        scalar.wait_ge(c.sem("s1"), bT + j - 1)
                    scalar.copy(bf[j % 2][:, :], st[j % 4][:, :])
                    scalar.sig("s0")
            block.scalar(sc)

            def tb(tensor):
                for j in range(nj):
                    tensor.wait_ge(c.sem("s0"), bA + j + 1)
                    if j >= 2:
                        tensor.wait_ge(c.sem("s2"), bV + j - 1)
                    for cc in range(ntile):
                        ins = tensor.transpose(
                            ps_tr[j % 2][:, cc * 128:cc * 128 + 128],
                            bf[j % 2][:, cc * 128:cc * 128 + 128],
                            c.k["c_ident_bf"][:, :])
                    ins.then_inc(c.sem("s1"), 1)
            block.tensor(tb)

            def vb(vector):
                for j in range(nj):
                    r, ch = jobs[j]
                    vector.wait_ge(c.sem("s1"), bT + j + 1)
                    dv = dstT_bf[:].rearrange("p (cc q) -> p cc q", cc=ncc)
                    pv = ps_tr[j % 2][:].rearrange("p (cc q) -> p cc q", cc=ntile)
                    vector.tensor_copy(
                        dv[:, ch * ntile:(ch + 1) * ntile, r * 128:r * 128 + 128],
                        pv)
                    vector.sig("s2")
            block.vector(vb)
        c.fin(s0=bA + nj, s1=bT + nj, s2=bV + nj)


def emit_bcast_row(c, row_bf, n_elem, dst_f32):
    nc = c.nc
    with ExitStack() as S:
        ps = c.ps(S, "bcrow", [128, n_elem], F32)
        with c.block() as block:
            def tb(tensor):
                for n0 in range(0, n_elem, 512):
                    nn = min(512, n_elem - n0)
                    tensor.matmul(ps[:, n0:n0 + nn],
                                  c.k["c_ones_bf"][0:1, 0:128],
                                  row_bf[0:1, n0:n0 + nn],
                                  start=True, stop=True)
            block.tensor(tb)
        with c.block() as block:
            def vb(vector):
                vector.tensor_copy(dst_f32[:, 0:n_elem], ps[:, 0:n_elem])
            block.vector(vb)


def emit_ln(c, x_dram, xnT, stats_out=None, eps=1e-5):
    nc = c.nc
    with ExitStack() as S:
        xstage = c.sb(S, "ln_xstage", [128, 2 * D], F32)
        sq = c.sb(S, "ln_sq", [128, D], F32)
        stats = c.sb(S, "ln_stats", [128, 8], F32)
        xn_bf = c.sb(S, "ln_xnbf", [128, D], BF16)
        ps_tr = c.ps(S, "ln_pstr", [128, D], BF16)
        xsrc = x_dram.rearrange("(t p) d -> p t d", p=128)

        a1, v1, a2, v2, a3, t1, v3 = c.base("s0", "s1", "s2", "s3", "s4", "s5", "s6")
        bdx = [c.val["dxa"], c.val["dxb"]]
        dsem = ["dxa", "dxb"]
        dval = [bdx[t % 2] + 16 * (t // 2 + 1) for t in range(NT)]
        with c.block() as block:
            def sy(sync):
                for t in range(NT):
                    if t >= 2:
                        sync.wait_ge(c.sem("s4"), a3 + t - 1)
                    c.dma_push(sync, xstage[:, (t % 2) * D:(t % 2) * D + D],
                               xsrc[:, t, :], sem=dsem[t % 2])
                sync.wait_ge(c.sem("dxa"), dval[NT - 2])
                sync.wait_ge(c.sem("dxb"), dval[NT - 1])
            block.sync(sy)

            def sc(scalar):
                for t in range(NT):
                    xt = xstage[:, (t % 2) * D:(t % 2) * D + D]
                    scalar.wait_ge(c.sem(dsem[t % 2]), dval[t])
                    scalar.square(sq[:], xt)
                    scalar.sig("s0")
                    scalar.wait_ge(c.sem("s1"), v1 + t + 1)
                    scalar.activation(stats[:, 2:3], stats[:, 1:2], AF.Sqrt)
                    scalar.sig("s2")
                    scalar.wait_ge(c.sem("s3"), v2 + t + 1)
                    if t > 0:
                        scalar.wait_ge(c.sem("s5"), t1 + t)
                    scalar.activation(xn_bf[:], xt, AF.Identity,
                                      bias=stats[:, 4:5], scale=stats[:, 3:4])
                    scalar.sig("s4")
            block.scalar(sc)

            def vb(vector):
                for t in range(NT):
                    xt = xstage[:, (t % 2) * D:(t % 2) * D + D]
                    vector.wait_ge(c.sem("s0"), a1 + t + 1)
                    vector.reduce_sum(stats[:, 0:1], xt, axis=AX.X)
                    vector.reduce_sum(stats[:, 5:6], sq[:], axis=AX.X)
                    vector.tensor_scalar_mul(stats[:, 6:7], stats[:, 0:1], -1.0 / D)
                    vector.tensor_mul(stats[:, 7:8], stats[:, 6:7], stats[:, 6:7])
                    vector.tensor_scalar(stats[:, 1:2], stats[:, 5:6], 1.0 / D,
                                         eps, ALU.mult, ALU.add)
                    vector.tensor_sub(stats[:, 1:2], stats[:, 1:2],
                                      stats[:, 7:8])
                    vector.sig("s1")
                    vector.wait_ge(c.sem("s2"), a2 + t + 1)
                    vector.reciprocal(stats[:, 3:4], stats[:, 2:3])
                    vector.tensor_mul(stats[:, 4:5], stats[:, 6:7],
                                      stats[:, 3:4])
                    if stats_out is not None:
                        vector.tensor_copy(stats_out[:, 2 * t:2 * t + 1],
                                           stats[:, 3:4])
                        vector.tensor_copy(stats_out[:, 2 * t + 1:2 * t + 2],
                                           stats[:, 4:5])
                    vector.sig("s3")
                    vector.wait_ge(c.sem("s5"), t1 + t + 1)
                    dst = xnT[:].rearrange("p (cc n) -> p cc n", cc=ND)
                    vector.tensor_copy(
                        dst[:, :, t * 128:t * 128 + 128],
                        ps_tr[:].rearrange("p (cc n) -> p cc n", cc=ND))
                    vector.sig("s6")
            block.vector(vb)

            def tb(tensor):
                for t in range(NT):
                    tensor.wait_ge(c.sem("s4"), a3 + t + 1)
                    if t > 0:
                        tensor.wait_ge(c.sem("s6"), v3 + t)
                    for cc in range(ND):
                        ins = tensor.transpose(ps_tr[:, cc * 128:cc * 128 + 128],
                                               xn_bf[:, cc * 128:cc * 128 + 128],
                                               c.k["c_ident_bf"][:, :])
                    ins.then_inc(c.sem("s5"), 1)
            block.tensor(tb)

        c.fin(s0=a1 + NT, s1=v1 + NT, s2=a2 + NT, s3=v2 + NT,
              s4=a3 + NT, s5=t1 + NT, s6=v3 + NT)


def emit_quant_tile(c, vector, src_f32_ap, dst_bf_ap, q):
    nblk = 32
    src3 = src_f32_ap.rearrange("p (b e) -> p b e", e=32)
    vector.tensor_reduce(q["scl"][:, 0:nblk], src3, axis=AX.X, op=ALU.max,
                         apply_absolute_value=True)
    vector.tensor_scalar(q["scl"][:, 0:nblk], q["scl"][:, 0:nblk], 1.0 / 7.5, 1e-30,
                         ALU.mult, ALU.max)
    vector.reciprocal(q["iscl"][:, 0:nblk], q["scl"][:, 0:nblk])
    iscl_bc = bass.AP(q["iscl"][:].tensor, 0, [[32, 128], [1, 32], [0, 32]])
    vector.tensor_tensor(q["qy"][:].rearrange("p (b e) -> p b e", e=32),
                         src3, iscl_bc, ALU.mult)
    vector.scalar_tensor_tensor(q["qa"][:], q["qy"][:], -1.0, q["qy"][:],
                                ALU.mult, ALU.max)
    vector.tensor_single_scalar(q["qb"][:], q["qa"][:], 2.0, ALU.is_ge)
    vector.tensor_single_scalar(q["qc"][:], q["qa"][:], 4.0, ALU.is_ge)
    vector.tensor_scalar(q["qa"][:], q["qb"][:], -4.0, 8.0, ALU.mult, ALU.add)
    vector.scalar_tensor_tensor(q["qa"][:], q["qc"][:], -2.0, q["qa"][:],
                                ALU.mult, ALU.add)
    vector.tensor_scalar(q["qb"][:], q["qb"][:], 0.125, 0.125, ALU.mult, ALU.add)
    vector.scalar_tensor_tensor(q["qb"][:], q["qc"][:], 0.25, q["qb"][:],
                                ALU.mult, ALU.add)
    vector.tensor_mul(q["qa"][:], q["qy"][:], q["qa"][:])
    vector.tensor_scalar_add(q["qa"][:], q["qa"][:], 12582912.0)
    vector.scalar_tensor_tensor(q["qb"][:], q["qa"][:], -12582912.0,
                                q["qb"][:], ALU.add, ALU.mult)
    scl_bc = bass.AP(q["scl"][:].tensor, 0, [[32, 128], [1, 32], [0, 32]])
    ins = vector.tensor_tensor(dst_bf_ap.rearrange("p (b e) -> p b e", e=32),
                               q["qb"][:].rearrange("p (b e) -> p b e", e=32),
                               scl_bc, ALU.mult)
    return ins


def make_quant_scratch(c, scope):
    q = {}
    for n in ["qa", "qb", "qc", "qy"]:
        q[n] = c.sb(scope, "q" + n, [128, D], F32)
    q["qi"] = c.sb(scope, "qqi", [128, D], I32)
    q["scl"] = c.sb(scope, "qscl", [128, 32], F32)
    q["iscl"] = c.sb(scope, "qiscl", [128, 32], F32)
    return q


def emit_attn_core(c, *, s_chunks, klen_fn, v_lhsT_fn, dv, sink_fn,
                   P, PT, ps_S, ps_PT, ps_PV, exp_scale=1.0, causal=True,
                   n_qt=NT):
    nc = c.nc
    b0, b1, b2, b3, b5, b6, b7 = c.base("s0", "s1", "s2", "s3",
                                        "s5", "s6", "s7")
    stat = c.attn_stat
    nbs = [cdiv(klen_fn(qt), 512) for qt in range(n_qt)]
    cnb = [0]
    cnb1 = [0]
    for qt in range(n_qt):
        cnb.append(cnb[-1] + nbs[qt])
        cnb1.append(cnb1[-1] + nbs[qt] + 1)

    with c.block() as block:
        def tb(tensor):
            for qt in range(n_qt):
                kl = klen_fn(qt)
                nb = nbs[qt]
                nkc = cdiv(kl, 128)
                if qt > 0:
                    tensor.wait_ge(c.sem("s2"), b2 + qt)
                for b in range(nb):
                    n0 = b * 512
                    nn = min(512, kl - n0)
                    for ci, (lhsT_fn, rhs_fn) in enumerate(s_chunks):
                        ins = tensor.matmul(ps_S[:, n0:n0 + nn], lhsT_fn(qt),
                                            rhs_fn(n0, nn),
                                            start=ci == 0,
                                            stop=ci == len(s_chunks) - 1)
                    ins.then_inc(c.sem("s0"), 1)
                tensor.wait_ge(c.sem("s3"), b3 + qt + 1)
                if qt > 0:
                    tensor.wait_ge(c.sem("s6"), b6 + qt)
                for kc in range(nkc):
                    ins = tensor.transpose(ps_PT[:, kc * 128:kc * 128 + 128],
                                           P[:, kc * 128:kc * 128 + 128],
                                           c.k["c_ident_bf"][:, :])
                ins.then_inc(c.sem("s5"), 1)
                tensor.wait_ge(c.sem("s6"), b6 + qt + 1)
                if qt > 0:
                    tensor.wait_ge(c.sem("s7"), b7 + qt)
                for kc in range(nkc):
                    ins = tensor.matmul(ps_PV[0:dv, 0:128], v_lhsT_fn(kc),
                                        PT[:, kc * 128:kc * 128 + 128],
                                        start=kc == 0, stop=kc == nkc - 1)
                ins.then_inc(c.sem("s0"), 1)
        block.tensor(tb)

        def vb(vector):
            for qt in range(n_qt):
                kl = klen_fn(qt)
                nb = nbs[qt]
                for b in range(nb):
                    vector.wait_ge(c.sem("s0"), b0 + cnb1[qt] + b + 1)
                    n0 = b * 512
                    nn = min(512, kl - n0)
                    if causal and b == nb - 1:
                        vector.tensor_tensor(ps_S[:, kl - 128:kl],
                                             ps_S[:, kl - 128:kl],
                                             c.k["c_mask"][:, :], ALU.add)
                    vector.reduce_max(stat[:, 4 + b:5 + b], ps_S[:, n0:n0 + nn],
                                      axis=AX.X)
                vector.reduce_max(stat[:, 0:1], stat[:, 4:4 + nb], axis=AX.X,
                                  negate=True)
                if exp_scale != 1.0:
                    vector.tensor_scalar_mul(stat[:, 3:4], stat[:, 0:1],
                                             exp_scale)
                vector.sig("s1")
                vector.wait_ge(c.sem("s2"), b2 + qt + 1)
                vector.reciprocal(stat[:, 2:3], stat[:, 1:2])
                vector.tensor_scalar_mul(P[:, 0:kl], P[:, 0:kl], stat[:, 2:3])
                vector.sig("s3")
        block.vector(vb)

        def ab(scalar):
            for qt in range(n_qt):
                kl = klen_fn(qt)
                nkc = cdiv(kl, 128)
                scalar.wait_ge(c.sem("s1"), b1 + qt + 1)
                if qt > 0:
                    scalar.wait_ge(c.sem("s5"), b5 + qt)
                bias = stat[:, 3:4] if exp_scale != 1.0 else stat[:, 0:1]
                scalar.activation(P[:, 0:kl], ps_S[:, 0:kl], AF.Exp,
                                  bias=bias, scale=exp_scale,
                                  accum_out=stat[:, 1:2])
                scalar.sig("s2")
                scalar.wait_ge(c.sem("s5"), b5 + qt + 1)
                scalar.copy(PT[:, 0:nkc * 128], ps_PT[:, 0:nkc * 128])
                scalar.sig("s6")
                scalar.wait_ge(c.sem("s0"), b0 + cnb1[qt + 1])
                sink_fn(scalar, qt, ps_PV)
                scalar.sig("s7")
        block.scalar(ab)

    c.fin(s0=b0 + cnb1[n_qt], s1=b1 + n_qt, s2=b2 + n_qt, s3=b3 + n_qt,
          s5=b5 + n_qt, s6=b6 + n_qt, s7=b7 + n_qt)




def attn_core_v2_streams(c, *, jobs, s_chunks, klen_fn, v_lhsT_fn, dv,
                         sink_fn, P2, S_sb2, ps_Sb, ps_PT, ps_PV, PT,
                         exp_scale=1.0, causal=True, wave_cols=2048,
                         pre_s_hook=None, t_hook=None, a_hook=None):
    nc = c.nc
    stat = c.attn_stat
    nj = len(jobs)
    nvp = len(ps_PV)
    nss = len(S_sb2)
    pd = len(P2)
    wkc = wave_cols // 128
    nbs = [cdiv(klen_fn(qt), 512) for (h, qt) in jobs]
    nwv = [cdiv(cdiv(klen_fn(qt), 128), wkc) for (h, qt) in jobs]
    cnb = [0]
    cwv = [0]
    for j in range(nj):
        cnb.append(cnb[-1] + nbs[j])
        cwv.append(cwv[-1] + nwv[j])
    bS, bE, bM, bX, bN, bPTT, bPTe, bPV, bPVe, bMask = c.base(
        "s0", "s1", "s2", "s3", "s4", "s5", "s6", "s7", "s8", "s9")
    ncaus = [0]
    for j in range(nj):
        ncaus.append(ncaus[-1] + (1 if causal else 0))

    def tb(tensor):
        def pt_pv(k):
            h, qt = jobs[k]
            kl = klen_fn(qt)
            nkc = cdiv(kl, 128)
            tensor.wait_ge(c.sem("s4"), bN + k + 1)
            for wv in range(nwv[k]):
                if bPTe + cwv[k] + wv > 0:
                    tensor.wait_ge(c.sem("s6"), bPTe + cwv[k] + wv)
                for kc in range(wv * wkc, min(nkc, (wv + 1) * wkc)):
                    ins = tensor.transpose(
                        ps_PT[:, (kc - wv * wkc) * 128:
                              (kc - wv * wkc) * 128 + 128],
                        P2[k % pd][:, kc * 128:kc * 128 + 128],
                        c.k["c_ident_bf"][:, :])
                ins.then_inc(c.sem("s5"), 1)
            tensor.wait_ge(c.sem("s6"), bPTe + cwv[k + 1])
            if bPVe + k - nvp + 1 > 0:
                tensor.wait_ge(c.sem("s8"), bPVe + k - nvp + 1)
            for kc in range(nkc):
                ins = tensor.matmul(ps_PV[k % nvp][0:dv, 0:128],
                                    v_lhsT_fn(h, kc),
                                    PT[:, kc * 128:kc * 128 + 128],
                                    start=kc == 0, stop=kc == nkc - 1)
            ins.then_inc(c.sem("s7"), 1)

        for j, (h, qt) in enumerate(jobs):
            kl = klen_fn(qt)
            chunks = s_chunks(h, qt)
            if pre_s_hook is not None:
                pre_s_hook(tensor, j)
            for b in range(nbs[j]):
                n0 = b * 512
                nn = min(512, kl - n0)
                if b == 0:
                    if nss == 0:
                        if bX + j > 0:
                            tensor.wait_ge(c.sem("s3"), bX + j)
                    elif j > 0:
                        tensor.wait_ge(c.sem("s1"), bE + j)
                for ci, (lhsT_ap, rhs_fn) in enumerate(chunks):
                    ins = tensor.matmul(ps_Sb[:, n0:n0 + nn], lhsT_ap,
                                        rhs_fn(n0, nn),
                                        start=ci == 0,
                                        stop=ci == len(chunks) - 1)
                ins.then_inc(c.sem("s0"), 1)
            if j >= 2:
                pt_pv(j - 2)
            if t_hook is not None:
                t_hook(tensor, j)
        if nj >= 2:
            pt_pv(nj - 2)
        pt_pv(nj - 1)

    def ab(scalar):
        def evac(k):
            kl = klen_fn(jobs[k][1])
            scalar.wait_ge(c.sem("s0"), bS + cnb[k + 1])
            if causal:
                scalar.wait_ge(c.sem("s9"), bMask + ncaus[k + 1])
            scalar.copy(S_sb2[k % nss][:, 0:kl], ps_Sb[:, 0:kl])
            scalar.sig("s1")

        def sink(k):
            hp, qtp = jobs[k]
            scalar.wait_ge(c.sem("s7"), bPV + k + 1)
            sink_fn(scalar, hp, qtp, ps_PV[k % nvp])
            scalar.sig("s8")

        for j, (h, qt) in enumerate(jobs):
            kl = klen_fn(qt)
            p = j % 2
            if j == 0 and nss > 0:
                evac(0)
            if nss > 1 and j + 1 < nj:
                evac(j + 1)
            scalar.wait_ge(c.sem("s2"), bM + j + 1)
            if j >= pd + 1:
                scalar.wait_ge(c.sem("s5"), bPTT + cwv[j - pd + 1])
            bias = stat[:, 2 + p:3 + p] if exp_scale != 1.0 \
                else stat[:, 0 + p:1 + p]
            esrc = ps_Sb[:, 0:kl] if nss == 0 else S_sb2[j % nss][:, 0:kl]
            scalar.activation(P2[j % pd][:, 0:kl], esrc, AF.Exp,
                              bias=bias, scale=exp_scale,
                              accum_out=stat[:, 6 + p:7 + p])
            scalar.sig("s3")
            if j >= 2:
                sink(j - 2)
            if a_hook is not None:
                a_hook(scalar, j)
            if nss == 1 and j + 1 < nj:
                evac(j + 1)
        if nj >= 2:
            sink(nj - 2)
        sink(nj - 1)

    def vb(vector):
        def norm(k):
            klp = klen_fn(jobs[k][1])
            pp = k % 2
            vector.wait_ge(c.sem("s3"), bX + k + 1)
            vector.reciprocal(stat[:, 8 + pp:9 + pp],
                              stat[:, 6 + pp:7 + pp])
            vector.tensor_scalar_mul(P2[k % pd][:, 0:klp],
                                     P2[k % pd][:, 0:klp],
                                     stat[:, 8 + pp:9 + pp])
            vector.sig("s4")

        def ptevac(k):
            nkcp = cdiv(klen_fn(jobs[k][1]), 128)
            for wv in range(nwv[k]):
                ncc = min(nkcp, (wv + 1) * wkc) - wv * wkc
                vector.wait_ge(c.sem("s5"), bPTT + cwv[k] + wv + 1)
                vector.tensor_copy(
                    PT[:, wv * wave_cols:wv * wave_cols + ncc * 128],
                    ps_PT[:, 0:ncc * 128])
                vector.sig("s6")

        for j, (h, qt) in enumerate(jobs):
            kl = klen_fn(qt)
            p = j % 2
            if causal:
                vector.wait_ge(c.sem("s0"), bS + cnb[j + 1])
                vector.tensor_tensor(ps_Sb[:, kl - 128:kl],
                                     ps_Sb[:, kl - 128:kl],
                                     c.k["c_mask"][:, :], ALU.add)
                vector.sig("s9")
            if j >= 2:
                ptevac(j - 2)
            if j > 0:
                norm(j - 1)
            if nss == 0:
                if not causal:
                    vector.wait_ge(c.sem("s0"), bS + cnb[j + 1])
                vector.reduce_max(stat[:, 0 + p:1 + p], ps_Sb[:, 0:kl],
                                  axis=AX.X, negate=True)
            else:
                vector.wait_ge(c.sem("s1"), bE + j + 1)
                vector.reduce_max(stat[:, 0 + p:1 + p],
                                  S_sb2[j % nss][:, 0:kl],
                                  axis=AX.X, negate=True)
            if exp_scale != 1.0:
                vector.tensor_scalar_mul(stat[:, 2 + p:3 + p],
                                         stat[:, 0 + p:1 + p], exp_scale)
            vector.sig("s2")
        norm(nj - 1)
        if nj >= 2:
            ptevac(nj - 2)
        ptevac(nj - 1)

    def fin():
        c.fin(s0=bS + cnb[nj], s1=bE + (nj if nss > 0 else 0), s2=bM + nj,
              s3=bX + nj,
              s4=bN + nj, s5=bPTT + cwv[nj], s6=bPTe + cwv[nj], s7=bPV + nj,
              s9=bMask + ncaus[nj])
        c.val["s8"] = bPVe + nj

    return tb, ab, vb, fin


def emit_attn_core_v2(c, **kw):
    tb, ab, vb, fin = attn_core_v2_streams(c, **kw)
    with c.block() as block:
        block.tensor(tb)
        block.scalar(ab)
        block.vector(vb)
    fin()


def emit_combine(c, xin, xout, emit_s_psum, g_all, s_sbuf=None):
    nc = c.nc
    with ExitStack() as S:
        xst = c.sb(S, "cmb_xst", [128, 2 * D], F32)
        ost = c.sb(S, "cmb_ost", [128, 2 * D], F32)
        pbf = c.sb(S, "cmb_pbf", [128, D], BF16)
        ps_o = c.ps(S, "cmb_pso", [128, 2 * D], F32)
        xsrc = xin.rearrange("(t p) d -> p t d", p=128)
        odst = xout.rearrange("(t p) d -> p t d", p=128)
        bT, bV = c.base("s0", "s1")
        bdx = [c.val["dxa"], c.val["dxb"]]
        bdo = [c.val["doa"], c.val["dob"]]
        xsem = ["dxa", "dxb"]
        osem = ["doa", "dob"]
        dma_x = [bdx[t % 2] + 16 * (t // 2 + 1) for t in range(NT)]
        dma_o = [bdo[t % 2] + 16 * (t // 2 + 1) for t in range(NT)]
        with c.block() as block:
            def sy(sync):
                for t in range(NT):
                    if t >= 2:
                        sync.wait_ge(c.sem("s1"), bV + t - 1)
                    c.dma_push(sync, xst[:, (t % 2) * D:(t % 2) * D + D],
                               xsrc[:, t, :], sem=xsem[t % 2])
                sync.wait_ge(c.sem("dxa"), dma_x[NT - 2])
                sync.wait_ge(c.sem("dxb"), dma_x[NT - 1])
            block.sync(sy)

            def gy(gpsimd):
                for t in range(NT):
                    gpsimd.wait_ge(c.sem("s1"), bV + t + 1)
                    c.dma_push(gpsimd, odst[:, t, :],
                               ost[:, (t % 2) * D:(t % 2) * D + D], sem=osem[t % 2])
                gpsimd.wait_ge(c.sem("doa"), dma_o[NT - 2])
                gpsimd.wait_ge(c.sem("dob"), dma_o[NT - 1])
            block.gpsimd(gy)

            def tb(tensor):
                if emit_s_psum is None:
                    return
                for t in range(NT):
                    buf = (t % 2) * D
                    if t >= 2:
                        tensor.wait_ge(c.sem("s1"), bV + t - 1)
                    emit_s_psum(tensor, t, ps_o[:, buf:buf + D]).then_inc(
                        c.sem("s0"), 1)
            block.tensor(tb)

            def vb(vector):
                for t in range(NT):
                    buf = (t % 2) * D
                    if emit_s_psum is not None:
                        vector.wait_ge(c.sem("s0"), bT + t + 1)
                        vector.tensor_mul(pbf[:], ps_o[:, buf:buf + D],
                                          g_all[:, t * D:t * D + D])
                    else:
                        vector.tensor_mul(pbf[:], s_sbuf[:, t * D:t * D + D],
                                          g_all[:, t * D:t * D + D])
                    vector.wait_ge(c.sem(xsem[t % 2]), dma_x[t])
                    if t >= 2:
                        vector.wait_ge(c.sem(osem[t % 2]), dma_o[t - 2])
                    vector.tensor_add(ost[:, buf:buf + D], xst[:, buf:buf + D],
                                      pbf[:])
                    vector.sig("s1")
            block.vector(vb)
        c.fin(s1=bV + NT)
        if emit_s_psum is not None:
            c.fin(s0=bT + NT)


def emit_combine_ln(c, xin, xout, emit_s_psum, g_all, xnT_next,
                    stats_next=None, s_sbuf=None, eps=1e-5):
    nc = c.nc
    with ExitStack() as S:
        xst = c.sb(S, "cmb_xst", [128, 2 * D], F32)
        ost = c.sb(S, "cmb_ost", [128, 2 * D], F32)
        pbf = c.sb(S, "cmb_pbf", [128, D], BF16)
        sq = c.sb(S, "cln_sq", [128, D], F32)
        stats = c.sb(S, "cln_stats", [128, 8], F32)
        xn_bf = c.sb(S, "cln_xnbf", [128, D], BF16)
        ps_o = c.ps(S, "cmb_pso", [128, 2 * D], F32)
        ps_tr = c.ps(S, "cln_pstr", [128, D], BF16)
        xsrc = xin.rearrange("(t p) d -> p t d", p=128)
        odst = xout.rearrange("(t p) d -> p t d", p=128)
        bT, bV, bA1, bV1, bA2, bV2, bA3, bT2, bV3 = c.base(
            "s0", "s1", "s2", "s3", "s4", "s5", "s6", "s7", "s9")
        bdx = [c.val["dxa"], c.val["dxb"]]
        bdo = [c.val["doa"], c.val["dob"]]
        xsem = ["dxa", "dxb"]
        osem = ["doa", "dob"]
        dma_x = [bdx[t % 2] + 16 * (t // 2 + 1) for t in range(NT)]
        dma_o = [bdo[t % 2] + 16 * (t // 2 + 1) for t in range(NT)]
        with c.block() as block:
            def sy(sync):
                for t in range(NT):
                    if t >= 2:
                        sync.wait_ge(c.sem("s1"), bV + t - 1)
                    c.dma_push(sync, xst[:, (t % 2) * D:(t % 2) * D + D],
                               xsrc[:, t, :], sem=xsem[t % 2])
                sync.wait_ge(c.sem("dxa"), dma_x[NT - 2])
                sync.wait_ge(c.sem("dxb"), dma_x[NT - 1])
            block.sync(sy)

            def gy(gpsimd):
                for t in range(NT):
                    gpsimd.wait_ge(c.sem("s1"), bV + t + 1)
                    c.dma_push(gpsimd, odst[:, t, :],
                               ost[:, (t % 2) * D:(t % 2) * D + D],
                               sem=osem[t % 2])
                gpsimd.wait_ge(c.sem("doa"), dma_o[NT - 2])
                gpsimd.wait_ge(c.sem("dob"), dma_o[NT - 1])
            block.gpsimd(gy)

            def tb(tensor):
                def transp(t):
                    tensor.wait_ge(c.sem("s6"), bA3 + t + 1)
                    if t >= 1:
                        tensor.wait_ge(c.sem("s9"), bV3 + t)
                    for cc in range(ND):
                        ins = tensor.transpose(
                            ps_tr[:, cc * 128:cc * 128 + 128],
                            xn_bf[:, cc * 128:cc * 128 + 128],
                            c.k["c_ident_bf"][:, :])
                    ins.then_inc(c.sem("s7"), 1)

                for t in range(NT):
                    if emit_s_psum is not None:
                        buf = (t % 2) * D
                        if t >= 2:
                            tensor.wait_ge(c.sem("s1"), bV + t - 1)
                        emit_s_psum(tensor, t, ps_o[:, buf:buf + D]).then_inc(
                            c.sem("s0"), 1)
                    if t >= 1:
                        transp(t - 1)
                transp(NT - 1)
            block.tensor(tb)

            def ab(scalar):
                for t in range(NT):
                    xt = ost[:, (t % 2) * D:(t % 2) * D + D]
                    scalar.wait_ge(c.sem("s1"), bV + t + 1)
                    scalar.square(sq[:], xt)
                    scalar.sig("s2")
                    scalar.wait_ge(c.sem("s3"), bV1 + t + 1)
                    scalar.activation(stats[:, 2:3], stats[:, 1:2], AF.Sqrt)
                    scalar.sig("s4")
                    scalar.wait_ge(c.sem("s5"), bV2 + t + 1)
                    if t > 0:
                        scalar.wait_ge(c.sem("s7"), bT2 + t)
                    scalar.activation(xn_bf[:], xt, AF.Identity,
                                      bias=stats[:, 4:5], scale=stats[:, 3:4])
                    scalar.sig("s6")
            block.scalar(ab)

            def vb(vector):
                def evac(t):
                    vector.wait_ge(c.sem("s7"), bT2 + t + 1)
                    dst = xnT_next[:].rearrange("p (cc n) -> p cc n", cc=ND)
                    vector.tensor_copy(
                        dst[:, :, t * 128:t * 128 + 128],
                        ps_tr[:].rearrange("p (cc n) -> p cc n", cc=ND))
                    vector.sig("s9")

                for t in range(NT):
                    buf = (t % 2) * D
                    if emit_s_psum is not None:
                        vector.wait_ge(c.sem("s0"), bT + t + 1)
                        vector.tensor_mul(pbf[:], ps_o[:, buf:buf + D],
                                          g_all[:, t * D:t * D + D])
                    else:
                        vector.tensor_mul(pbf[:], s_sbuf[:, t * D:t * D + D],
                                          g_all[:, t * D:t * D + D])
                    vector.wait_ge(c.sem(xsem[t % 2]), dma_x[t])
                    if t >= 2:
                        vector.wait_ge(c.sem(osem[t % 2]), dma_o[t - 2])
                        vector.wait_ge(c.sem("s6"), bA3 + t - 1)
                    vector.tensor_add(ost[:, buf:buf + D], xst[:, buf:buf + D],
                                      pbf[:])
                    vector.sig("s1")
                    vector.wait_ge(c.sem("s2"), bA1 + t + 1)
                    xt = ost[:, buf:buf + D]
                    vector.reduce_sum(stats[:, 0:1], xt, axis=AX.X)
                    vector.reduce_sum(stats[:, 5:6], sq[:], axis=AX.X)
                    vector.tensor_scalar_mul(stats[:, 6:7], stats[:, 0:1],
                                             -1.0 / D)
                    vector.tensor_mul(stats[:, 7:8], stats[:, 6:7],
                                      stats[:, 6:7])
                    vector.tensor_scalar(stats[:, 1:2], stats[:, 5:6], 1.0 / D,
                                         eps, ALU.mult, ALU.add)
                    vector.tensor_sub(stats[:, 1:2], stats[:, 1:2],
                                      stats[:, 7:8])
                    vector.sig("s3")
                    vector.wait_ge(c.sem("s4"), bA2 + t + 1)
                    vector.reciprocal(stats[:, 3:4], stats[:, 2:3])
                    vector.tensor_mul(stats[:, 4:5], stats[:, 6:7],
                                      stats[:, 3:4])
                    if stats_next is not None:
                        vector.tensor_copy(stats_next[:, 2 * t:2 * t + 1],
                                           stats[:, 3:4])
                        vector.tensor_copy(stats_next[:, 2 * t + 1:2 * t + 2],
                                           stats[:, 4:5])
                    vector.sig("s5")
                    if t >= 1:
                        evac(t - 1)
                evac(NT - 1)
            block.vector(vb)
        c.fin(s1=bV + NT, s2=bA1 + NT, s3=bV1 + NT, s4=bA2 + NT, s5=bV2 + NT,
              s6=bA3 + NT, s7=bT2 + NT, s9=bV3 + NT)
        if emit_s_psum is not None:
            c.fin(s0=bT + NT)


def emit_renorm_pipeline(c, S, xin, stats, consumer):
    raise NotImplementedError


def emit_layer0(c, xin, xout, w, fuse_next=None):
    nc = c.nc
    with ExitStack() as L:
        xnT = c.sb(L, "l0_xnT", [128, ND * T], BF16)
        lnstats = c.sb(L, "l0_lnst", [128, 2 * NT], F32)
        g0_dram = nc.dram_tensor(f"l0_gspill_{c.uid()}", [128, NT * D], BF16,
                                 kind="Internal").ap()
        rstage = c.sb(L, "l0_rstage", [1, D], F32)

        emit_ln(c, xin, xnT, stats_out=lnstats)

        with ExitStack() as Sf:
            g_all = c.sb(Sf, "l0_g", [128, NT * D], BF16)
            xdT = c.sb(Sf, "l0_xdT", [128, ND * T], BF16)
            wdT = c.sb(Sf, "l0_wdT", [128, ND * D], BF16)
            fb_row = c.sb(Sf, "l0_fbrow", [1, D], BF16)
            with ExitStack() as Sq:
                q = make_quant_scratch(c, Sq)
                xstage = c.sb(Sq, "l0_qxst", [128, 2 * D], F32)
                wstage = c.sb(Sq, "l0_qwst", [128, 2 * D], F32)
                xnf = c.sb(Sq, "l0_qxnf", [128, D], F32)
                xd_bf = c.sb(Sq, "l0_xdbf", [128, D], BF16)
                wd_bf = c.sb(Sq, "l0_wdbf", [128, D], BF16)
                ps_tr = c.ps(Sq, "l0_pstr", [128, D], BF16)
                ps_trw = c.ps(Sq, "l0_pstrw", [128, D], BF16)
                xsrc = xin.rearrange("(t p) d -> p t d", p=128)
                fsrc = w["L0_ffn_weight"].rearrange("(r p) c -> p r c", p=128)

                bA, bV, bT, bE, bWT, bWE, bWQ = c.base(
                    "s0", "s1", "s2", "s3", "s5", "s6", "s9")
                bdx = [c.val["dxa"], c.val["dxb"]]
                dsem = ["dxa", "dxb"]
                dval = [bdx[t % 2] + 16 * (t // 2 + 1) for t in range(NT)]
                bdw = [c.val["dxc"], c.val["dxd"]]
                dsemw = ["dxc", "dxd"]
                dvalw = [bdw[r % 2] + 16 * (r // 2 + 1) for r in range(ND)]
                with c.block() as block:
                    def sy(sync):
                        for r in range(2):
                            c.dma_push(sync, wstage[:, r * D:r * D + D],
                                       fsrc[:, r, :], sem=dsemw[r % 2])
                        for t in range(NT):
                            if t >= 2:
                                sync.wait_ge(c.sem("s0"), bA + t - 1)
                            c.dma_push(sync, xstage[:, (t % 2) * D:(t % 2) * D + D],
                                       xsrc[:, t, :], sem=dsem[t % 2])
                            if t >= 2 and t % 2 == 0 and t // 2 + 1 < ND:
                                r = t // 2 + 1
                                sync.wait_ge(c.sem("s9"), bWQ + r - 1)
                                c.dma_push(sync, wstage[:, (r % 2) * D:
                                                        (r % 2) * D + D],
                                           fsrc[:, r, :], sem=dsemw[r % 2])
                        sync.wait_ge(c.sem("dxa"), dval[NT - 2])
                        sync.wait_ge(c.sem("dxb"), dval[NT - 1])
                        sync.wait_ge(c.sem("dxc"), dvalw[ND - 2])
                        sync.wait_ge(c.sem("dxd"), dvalw[ND - 1])
                    block.sync(sy)

                    def ab(scalar):
                        for t in range(NT):
                            scalar.wait_ge(c.sem(dsem[t % 2]), dval[t])
                            if t > 0:
                                scalar.wait_ge(c.sem("s1"), bV + t)
                            scalar.activation(
                                xnf[:], xstage[:, (t % 2) * D:(t % 2) * D + D],
                                AF.Identity, bias=lnstats[:, 2 * t + 1:2 * t + 2],
                                scale=lnstats[:, 2 * t:2 * t + 1])
                            scalar.sig("s0")
                    block.scalar(ab)

                    def vb(vector):
                        for t in range(NT):
                            if t % 2 == 0 and t // 2 < ND:
                                r = t // 2
                                vector.wait_ge(c.sem(dsemw[r % 2]), dvalw[r])
                                if r > 0:
                                    vector.wait_ge(c.sem("s5"), bWT + r)
                                emit_quant_tile(
                                    c, vector,
                                    wstage[:, (r % 2) * D:(r % 2) * D + D],
                                    wd_bf[:], q)
                                vector.sig("s9")
                            vector.wait_ge(c.sem("s0"), bA + t + 1)
                            if t > 0:
                                vector.wait_ge(c.sem("s2"), bT + t)
                            emit_quant_tile(c, vector, xnf[:], xd_bf[:], q)
                            vector.sig("s1")
                            vector.wait_ge(c.sem("s2"), bT + t + 1)
                            dst = xdT[:].rearrange("p (cc n) -> p cc n", cc=ND)
                            vector.tensor_copy(
                                dst[:, :, t * 128:t * 128 + 128],
                                ps_tr[:].rearrange("p (cc n) -> p cc n", cc=ND))
                            vector.sig("s3")
                            if t % 2 == 1 and t // 2 < ND:
                                r = t // 2
                                vector.wait_ge(c.sem("s5"), bWT + r + 1)
                                dstw = wdT[:].rearrange("p (cc n) -> p cc n",
                                                        cc=ND)
                                vector.tensor_copy(
                                    dstw[:, :, r * 128:r * 128 + 128],
                                    ps_trw[:].rearrange("p (cc n) -> p cc n",
                                                        cc=ND))
                                vector.sig("s6")
                    block.vector(vb)

                    def tb(tensor):
                        for t in range(NT):
                            tensor.wait_ge(c.sem("s1"), bV + t + 1)
                            if t > 0:
                                tensor.wait_ge(c.sem("s3"), bE + t)
                            for cc in range(ND):
                                ins = tensor.transpose(
                                    ps_tr[:, cc * 128:cc * 128 + 128],
                                    xd_bf[:, cc * 128:cc * 128 + 128],
                                    c.k["c_ident_bf"][:, :])
                            ins.then_inc(c.sem("s2"), 1)
                            if t % 2 == 1 and t // 2 < ND:
                                r = t // 2
                                tensor.wait_ge(c.sem("s9"), bWQ + r + 1)
                                if r > 0:
                                    tensor.wait_ge(c.sem("s6"), bWE + r)
                                for cc in range(ND):
                                    ins = tensor.transpose(
                                        ps_trw[:, cc * 128:cc * 128 + 128],
                                        wd_bf[:, cc * 128:cc * 128 + 128],
                                        c.k["c_ident_bf"][:, :])
                                ins.then_inc(c.sem("s5"), 1)
                    block.tensor(tb)
                c.fin(s0=bA + NT, s1=bV + NT, s2=bT + NT, s3=bE + NT,
                      s5=bWT + ND, s6=bWE + ND, s9=bWQ + ND)

            emit_row_load(c, w["L0_ffn_bias"], D, rstage, fb_row)

            with ExitStack() as Sp:
                ps_f = c.ps(Sp, "l0_psf", [128, 2 * D], F32)
                bT, bA = c.base("s0", "s1")
                with c.block() as block:
                    def tb(tensor):
                        for t in range(NT):
                            buf = (t % 2) * D
                            if t >= 2:
                                tensor.wait_ge(c.sem("s1"), bA + t - 1)
                            for h in range(2):
                                for cc in range(ND):
                                    tensor.matmul(
                                        ps_f[:, buf + h * 512: buf + h * 512 + 512],
                                        xdT[:, cc * T + t * 128: cc * T + t * 128 + 128],
                                        wdT[:, cc * D + h * 512: cc * D + h * 512 + 512],
                                        start=cc == 0, stop=False)
                                ins = tensor.matmul(
                                    ps_f[:, buf + h * 512: buf + h * 512 + 512],
                                    c.k["c_ones_bf"][0:1, 0:128],
                                    fb_row[0:1, h * 512: h * 512 + 512],
                                    start=False, stop=True)
                            ins.then_inc(c.sem("s0"), 1)
                    block.tensor(tb)

                    def ab(scalar):
                        for t in range(NT):
                            buf = (t % 2) * D
                            scalar.wait_ge(c.sem("s0"), bT + t + 1)
                            scalar.activation(g_all[:, t * D:t * D + D],
                                              ps_f[:, buf:buf + D], AF.Silu)
                            scalar.sig("s1")
                    block.scalar(ab)
                c.fin(s0=bT + NT, s1=bA + NT)
            with c.block() as block:
                def sy(sync):
                    c.dma_push(sync, g0_dram, g_all[:])
                    sync.wait_ge(c.sem("dma"), c.val["dma"])
                block.sync(sy)

        ctxT = c.sb(L, "l0_ctxT", [128, 8 * T], BF16)
        with ExitStack() as Sa:
            p_tok = c.sb(Sa, "l0_ptok", [128, NT * 40], F32)
            pm_tok = c.sb(Sa, "l0_pmtok", [128, NT * 40], F32)
            bp_col = c.sb(Sa, "l0_bpcol", [40, 1], F32)
            with ExitStack() as Sw:
                wp_st = c.sb(Sw, "l0_wpst", [128, ND * 40], F32)
                wp_bf = c.sb(Sw, "l0_wpbf", [128, ND * 40], BF16)
                pT = c.sb(Sw, "l0_pT", [40, T], F32)
                emit_load_natural(c, w["L0_attn_Wp"], D, 40, wp_bf, wp_st)
                emit_row_load(c, w["L0_attn_bp"], 40, rstage)
                emit_col_from_row(c, rstage, 40, bp_col)
                with ExitStack() as Sp:
                    ps_p = c.ps(Sp, "l0_psp", [40, T], F32)
                    with c.block() as block:
                        def tb(tensor):
                            for n0 in range(0, T, 512):
                                for cc in range(ND):
                                    tensor.matmul(
                                        ps_p[:, n0:n0 + 512],
                                        wp_bf[:, cc * 40: cc * 40 + 40],
                                        xnT[:, cc * T + n0: cc * T + n0 + 512],
                                        start=cc == 0, stop=cc == ND - 1)
                        block.tensor(tb)
                    with c.block() as block:
                        def ab(scalar):
                            scalar.activation(pT[:, :], ps_p[:, :], AF.Sigmoid,
                                              bias=bp_col[:, 0:1])
                        block.scalar(ab)
                with ExitStack() as Sp:
                    ps_t = c.ps(Sp, "l0_pspt", [128, NT * 64], F32)
                    with c.block() as block:
                        def tb(tensor):
                            for t in range(NT):
                                tensor.transpose(ps_t[:, t * 64:t * 64 + 40],
                                                 pT[:, t * 128:t * 128 + 128],
                                                 c.k["c_ident_f32"][0:40, 0:40])
                        block.tensor(tb)
                    with c.block() as block:
                        def vb(vector):
                            pv = ps_t[:].rearrange("p (t e) -> p t e", e=64)
                            vector.tensor_copy(
                                p_tok[:].rearrange("p (t e) -> p t e", e=40),
                                pv[:, :, 0:40])
                            vector.tensor_scalar(
                                pm_tok[:].rearrange("p (t e) -> p t e", e=40),
                                pv[:, :, 0:40], -1.0, 1.0, ALU.mult, ALU.add)
                        block.vector(vb)

            with ExitStack() as Score:
                PhiT = c.sb(Score, "l0_PhiT", [62, 8 * T], BF16)
                phi = c.sb(Score, "l0_phi", [128, 8 * 62], F32)
                phi_bf = c.sb(Score, "l0_phibf", [128, 8 * 62], BF16)
                V_all = c.sb(Score, "l0_Vall", [128, 8 * T], BF16)
                FB = [0, 2, 6, 14, 30]
                with ExitStack() as Sp:
                    ps_phi = c.ps(Sp, "l0_psphi", [62, 8 * 128], BF16)
                    bV, bT2, bA2 = c.base("s0", "s1", "s2")
                    phi_h = phi[:].tensor
                    ptok_h = p_tok[:].tensor
                    pmtok_h = pm_tok[:].tensor
                    with c.block() as block:
                        def vb(vector):
                            for t in range(NT):
                                if t > 0:
                                    vector.wait_ge(c.sem("s1"), bT2 + t)
                                for j in range(5):
                                    nb = 1 << j
                                    pc = bass.AP(ptok_h, t * 40 + j,
                                                 [[NT * 40, 128], [5, 8], [0, nb]])
                                    pm = bass.AP(pmtok_h, t * 40 + j,
                                                 [[NT * 40, 128], [5, 8], [0, nb]])
                                    if j == 0:
                                        vector.tensor_copy(
                                            bass.AP(phi_h, 0,
                                                    [[496, 128], [62, 8], [1, 1]]),
                                            bass.AP(pmtok_h, t * 40,
                                                    [[NT * 40, 128], [5, 8], [1, 1]]))
                                        vector.tensor_copy(
                                            bass.AP(phi_h, 1,
                                                    [[496, 128], [62, 8], [1, 1]]),
                                            bass.AP(ptok_h, t * 40,
                                                    [[NT * 40, 128], [5, 8], [1, 1]]))
                                    else:
                                        par = bass.AP(phi_h, FB[j - 1],
                                                      [[496, 128], [62, 8], [1, nb]])
                                        out0 = bass.AP(phi_h, FB[j],
                                                       [[496, 128], [62, 8], [1, nb]])
                                        out1 = bass.AP(phi_h, FB[j] + nb,
                                                       [[496, 128], [62, 8], [1, nb]])
                                        vector.tensor_tensor(out0, par, pm, ALU.mult)
                                        vector.tensor_tensor(out1, par, pc, ALU.mult)
                                vector.tensor_copy(phi_bf[:], phi[:])
                                vector.sig("s0")
                        block.vector(vb)

                        def tb(tensor):
                            for t in range(NT):
                                tensor.wait_ge(c.sem("s0"), bV + t + 1)
                                if t > 0:
                                    tensor.wait_ge(c.sem("s2"), bA2 + t)
                                for hh in range(8):
                                    ins = tensor.transpose(
                                        ps_phi[:, hh * 128:hh * 128 + 128],
                                        phi_bf[:, hh * 62:hh * 62 + 62],
                                        c.k["c_ident_bf"][:, :])
                                ins.then_inc(c.sem("s1"), 1)
                        block.tensor(tb)

                        def ab(scalar):
                            for t in range(NT):
                                scalar.wait_ge(c.sem("s1"), bT2 + t + 1)
                                dst = PhiT[:].rearrange("p (hh n) -> p hh n", hh=8)
                                pv = ps_phi[:].rearrange("p (hh n) -> p hh n", hh=8)
                                scalar.copy(dst[:, :, t * 128:t * 128 + 128], pv)
                                scalar.sig("s2")
                        block.scalar(ab)
                    c.fin(s0=bV + NT, s1=bT2 + NT, s2=bA2 + NT)

                with ExitStack() as Sw:
                    wv_bf = c.sb(Sw, "l0_wvbf", [128, ND * D], BF16)
                    emit_load_natural_chunked(c, w["L0_attn_Wv"], D, D, wv_bf,
                                              "l0wv")
                    with ExitStack() as Sp:
                        ps_v = [c.ps(Sp, f"l0_psv{i}", [128, 128], F32)
                                for i in range(2)]
                        bT2, bA2 = c.base("s1", "s2")
                        jobsv = [(h, t) for h in range(8) for t in range(NT)]
                        with c.block() as block:
                            def tb(tensor):
                                for jv, (h, t) in enumerate(jobsv):
                                    if jv >= 2:
                                        tensor.wait_ge(c.sem("s2"), bA2 + jv - 1)
                                    for cc in range(ND):
                                        ins = tensor.matmul(
                                            ps_v[jv % 2][:, :],
                                            xnT[:, cc * T + t * 128: cc * T + t * 128 + 128],
                                            wv_bf[:, cc * D + h * 128: cc * D + h * 128 + 128],
                                            start=cc == 0, stop=cc == ND - 1)
                                    ins.then_inc(c.sem("s1"), 1)
                            block.tensor(tb)

                            def ab(scalar):
                                for jv, (h, t) in enumerate(jobsv):
                                    scalar.wait_ge(c.sem("s1"), bT2 + jv + 1)
                                    scalar.copy(
                                        V_all[:, h * T + t * 128: h * T + t * 128 + 128],
                                        ps_v[jv % 2][:, :])
                                    scalar.sig("s2")
                            block.scalar(ab)
                        c.fin(s1=bT2 + len(jobsv), s2=bA2 + len(jobsv))

                P2 = [c.sb(Score, f"l0_P{i}", [128, T], BF16) for i in range(3)]
                S_sb2 = [c.sb(Score, f"l0_Ssb{i}", [128, T], F32) for i in range(2)]
                PTb = c.sb(Score, "l0_PTb", [128, T], BF16)
                c.attn_stat = c.sb(Score, "l0_astat", [128, 24], F32)
                with ExitStack() as Sp:
                    ps_Sb = c.ps(Sp, "l0_psS", [128, 2048], F32)
                    ps_PT = c.ps(Sp, "l0_psPT", [128, T], BF16)
                    ps_PV = [c.ps(Sp, f"l0_psPV{i}", [128, 128], F32)
                             for i in range(2)]

                    def sink(scalar, h, qt, ps):
                        return scalar.copy(
                            ctxT[:, h * T + qt * 128: h * T + qt * 128 + 128],
                            ps[0:128, 0:128])

                    emit_attn_core_v2(
                        c,
                        jobs=[(h, qt) for h in range(8) for qt in range(NT)],
                        s_chunks=lambda h, qt: [
                            (PhiT[0:62, h * T + qt * 128: h * T + qt * 128 + 128],
                             lambda k0, nn, h=h: PhiT[0:62, h * T + k0:
                                                      h * T + k0 + nn])],
                        klen_fn=lambda qt: (qt + 1) * 128,
                        v_lhsT_fn=lambda h, kc: V_all[:, h * T + kc * 128:
                                                      h * T + kc * 128 + 128],
                        dv=128, sink_fn=sink, P2=P2, S_sb2=S_sb2, PT=PTb,
                        ps_Sb=ps_Sb, ps_PT=ps_PT, ps_PV=ps_PV,
                        exp_scale=0.2, causal=True)

            wo_bf = c.sb(Sa, "l0_wobf", [128, ND * D], BF16)
            g_all = c.sb(Sa, "l0_g2", [128, NT * D], BF16)
            with c.block() as block:
                def syg(sync):
                    c.dma_push(sync, g_all[:], g0_dram)
                    sync.wait_ge(c.sem("dma"), c.val["dma"])
                block.sync(syg)
            emit_load_natural_chunked(c, w["L0_attn_Wo"], D, D, wo_bf, "l0wo")

            def s_psum(tensor, t, psum_ap):
                for nf in range(2):
                    for hh in range(8):
                        ins = tensor.matmul(
                            psum_ap[:, nf * 512: nf * 512 + 512],
                            ctxT[:, hh * T + t * 128: hh * T + t * 128 + 128],
                            wo_bf[:, hh * D + nf * 512: hh * D + nf * 512 + 512],
                            start=hh == 0, stop=hh == 7)
                return ins

            if fuse_next is None:
                emit_combine(c, xin, xout, s_psum, g_all)
            else:
                emit_combine_ln(c, xin, xout, s_psum, g_all, fuse_next[0],
                                stats_next=fuse_next[1])


def emit_layer1(c, xin, xout, w, xnT_in=None, fuse_next=None):
    nc = c.nc
    H1 = 16
    with ExitStack() as L:
        if xnT_in is None:
            xnT = c.sb(L, "l1_xnT", [128, ND * T], BF16)
        g_dram = nc.dram_tensor(f"l1_gspill_{c.uid()}", [128, NT * D], BF16,
                                kind="Internal").ap()
        if xnT_in is None:
            emit_ln(c, xin, xnT)
        else:
            xnT = xnT_in

        with ExitStack() as Sf:
            g_all = c.sb(Sf, "l1_g", [128, NT * D], BF16)
            wT = c.sb(Sf, "l1_wT", [128, ND * D], BF16)
            emit_load_T_chunked(c, w["L1_ffn_weight"], D, D, wT, "l1wt")

            gate_tok = c.sb(Sf, "l1_gate", [128, NT * 8], F32)
            gstat = c.sb(Sf, "l1_gstat", [128, 4], F32)
            gex = c.sb(Sf, "l1_gex", [128, 8], F32)
            gw_st = c.sb(Sf, "l1_gwst", [8, D], F32)
            gw_bf = c.sb(Sf, "l1_gwbf", [8, D], BF16)
            gwT = c.sb(Sf, "l1_gwT", [128, ND * 8], BF16)
            with c.block() as block:
                def sy(sync):
                    c.dma_push(sync, gw_st[:, :], w["L1_ffn_gate_w"])
                block.sync(sy)
                dtot = c.val["dma"]

                def sc(scalar):
                    scalar.wait_ge(c.sem("dma"), dtot)
                    scalar.copy(gw_bf[:, :], gw_st[:, :])
                block.scalar(sc)
            with ExitStack() as Sp:
                ps_gt = c.ps(Sp, "l1_psgt", [128, ND * 8], BF16)
                with c.block() as block:
                    def tb(tensor):
                        for cc in range(ND):
                            tensor.transpose(ps_gt[:, cc * 8:cc * 8 + 8],
                                             gw_bf[:, cc * 128:cc * 128 + 128],
                                             c.k["c_ident_bf"][0:8, 0:8])
                    block.tensor(tb)
                with c.block() as block:
                    def vb(vector):
                        vector.tensor_copy(gwT[:, :], ps_gt[:, :])
                    block.vector(vb)
            with ExitStack() as Sp:
                ps_g = [c.ps(Sp, "l1_psg0", [128, 8], F32),
                        c.ps(Sp, "l1_psg1", [128, 8], F32)]
                bT, bM, bE, bN = c.base("s0", "s1", "s2", "s3")
                with c.block() as block:
                    def tb(tensor):
                        for t in range(NT):
                            if t >= 2:
                                tensor.wait_ge(c.sem("s2"), bE + t - 1)
                            for cc in range(ND):
                                ins = tensor.matmul(
                                    ps_g[t % 2][:, 0:8],
                                    xnT[:, cc * T + t * 128: cc * T + t * 128 + 128],
                                    gwT[:, cc * 8: cc * 8 + 8],
                                    start=cc == 0, stop=cc == ND - 1)
                            ins.then_inc(c.sem("s0"), 1)
                    block.tensor(tb)

                    def vb(vector):
                        for t in range(NT):
                            vector.wait_ge(c.sem("s0"), bT + t + 1)
                            vector.reduce_max(gstat[:, 0:1], ps_g[t % 2][:, 0:8],
                                              axis=AX.X, negate=True)
                            vector.sig("s1")
                            vector.wait_ge(c.sem("s2"), bE + t + 1)
                            vector.reciprocal(gstat[:, 2:3], gstat[:, 1:2])
                            vector.tensor_scalar(gate_tok[:, t * 8:t * 8 + 8],
                                                 gex[:], gstat[:, 2:3], 1.0 / 16.0,
                                                 ALU.mult, ALU.mult)
                            vector.sig("s3")
                    block.vector(vb)

                    def ab(scalar):
                        for t in range(NT):
                            scalar.wait_ge(c.sem("s1"), bM + t + 1)
                            if t > 0:
                                scalar.wait_ge(c.sem("s3"), bN + t)
                            scalar.activation(gex[:], ps_g[t % 2][:, 0:8], AF.Exp,
                                              bias=gstat[:, 0:1],
                                              accum_out=gstat[:, 1:2])
                            scalar.sig("s2")
                    block.scalar(ab)
                c.fin(s0=bT + NT, s1=bM + NT, s2=bE + NT, s3=bN + NT)

            UallT = c.sb(Sf, "l1_UallT", [128, ND * 128], BF16)
            u_st = c.sb(Sf, "l1_ust", [128, D], F32)
            u_bf = c.sb(Sf, "l1_ubf", [128, D], BF16)
            emit_load_natural(c, w["L1_ffn_lora_U"].rearrange("e r c -> (e r) c"),
                              128, D, u_bf, u_st)
            with ExitStack() as Sp:
                ps_u = c.ps(Sp, "l1_psu", [128, D], BF16)
                with c.block() as block:
                    def tb(tensor):
                        for cc in range(ND):
                            tensor.transpose(ps_u[:, cc * 128:cc * 128 + 128],
                                             u_bf[:, cc * 128:cc * 128 + 128],
                                             c.k["c_ident_bf"][:, :])
                    block.tensor(tb)
                with c.block() as block:
                    def vb(vector):
                        vector.tensor_copy(UallT[:, :], ps_u[:, :])
                    block.vector(vb)

            teT = c.sb(Sf, "l1_teT", [16, T], BF16)
            tg_tok = c.sb(Sf, "l1_tgtok", [128, 16], BF16)
            tgT = [c.sb(Sf, f"l1_tgT{e}", [16, T], BF16) for e in range(8)]
            VeT = [c.sb(Sf, f"l1_VeT{e}", [16, D], BF16) for e in range(8)]
            v_st = c.sb(Sf, "l1_vst", [128, ND * 16], F32)
            v_bf = c.sb(Sf, "l1_vbf", [128, ND * 16], BF16)
            for e in range(8):
                emit_load_natural(c, w["L1_ffn_lora_V"][e, :, :], D, 16, v_bf, v_st)
                with ExitStack() as Sp:
                    ps_v = c.ps(Sp, "l1_psvT", [16, D], BF16)
                    with c.block() as block:
                        def tb(tensor, e=e):
                            for r in range(ND):
                                tensor.transpose(ps_v[:, r * 128:r * 128 + 128],
                                                 v_bf[:, r * 16:r * 16 + 16],
                                                 c.k["c_ident_bf"][:, :])
                        block.tensor(tb)
                    with c.block() as block:
                        def vb(vector, e=e):
                            vector.tensor_copy(VeT[e][:, :], ps_v[:, :])
                        block.vector(vb)
                with ExitStack() as Sp:
                    ps_t = c.ps(Sp, "l1_pste", [16, T], F32)
                    with c.block() as block:
                        def tb(tensor, e=e):
                            for n0 in range(0, T, 512):
                                for cc in range(ND):
                                    tensor.matmul(
                                        ps_t[:, n0:n0 + 512],
                                        UallT[:, cc * 128 + e * 16:
                                              cc * 128 + e * 16 + 16],
                                        xnT[:, cc * T + n0: cc * T + n0 + 512],
                                        start=cc == 0, stop=cc == ND - 1)
                        block.tensor(tb)
                    with c.block() as block:
                        def ab(scalar, e=e):
                            scalar.copy(teT[:, :], ps_t[:, :])
                        block.scalar(ab)
                with ExitStack() as Sp:
                    ps_tt = c.ps(Sp, "l1_pstt", [128, 16], BF16)
                    ps_tb = c.ps(Sp, "l1_pstb", [16, 128], BF16)
                    bT2, bV2, bT3, bV3 = c.base("s0", "s1", "s2", "s3")
                    with c.block() as block:
                        def tb(tensor, e=e):
                            for t in range(NT):
                                if t > 0:
                                    tensor.wait_ge(c.sem("s1"), bV2 + t)
                                tensor.transpose(
                                    ps_tt[:, 0:16], teT[:, t * 128:t * 128 + 128],
                                    c.k["c_ident_bf"][0:16, 0:16]
                                ).then_inc(c.sem("s0"), 1)
                                tensor.wait_ge(c.sem("s1"), bV2 + t + 1)
                                if t > 0:
                                    tensor.wait_ge(c.sem("s3"), bV3 + t)
                                tensor.transpose(
                                    ps_tb[:, 0:128], tg_tok[:, 0:16],
                                    c.k["c_ident_bf"][:, :]
                                ).then_inc(c.sem("s2"), 1)
                        block.tensor(tb)

                        def vb(vector, e=e):
                            for t in range(NT):
                                vector.wait_ge(c.sem("s0"), bT2 + t + 1)
                                vector.tensor_scalar_mul(
                                    tg_tok[:, 0:16], ps_tt[:, 0:16],
                                    gate_tok[:, t * 8 + e: t * 8 + e + 1])
                                vector.sig("s1")
                                vector.wait_ge(c.sem("s2"), bT3 + t + 1)
                                vector.tensor_copy(
                                    tgT[e][:, t * 128:t * 128 + 128],
                                    ps_tb[:, 0:128])
                                vector.sig("s3")
                        block.vector(vb)
                    c.fin(s0=bT2 + NT, s1=bV2 + NT, s2=bT3 + NT, s3=bV3 + NT)

            with ExitStack() as Sp:
                ps_f = c.ps(Sp, "l1_psf", [128, 2 * D], F32)
                bT, bA = c.base("s0", "s1")
                with c.block() as block:
                    def tb(tensor):
                        for t in range(NT):
                            buf = (t % 2) * D
                            if t >= 2:
                                tensor.wait_ge(c.sem("s1"), bA + t - 1)
                            for hf in range(2):
                                for cc in range(ND):
                                    tensor.matmul(
                                        ps_f[:, buf + hf * 512: buf + hf * 512 + 512],
                                        xnT[:, cc * T + t * 128: cc * T + t * 128 + 128],
                                        wT[:, cc * D + hf * 512: cc * D + hf * 512 + 512],
                                        start=cc == 0, stop=False)
                                for e in range(8):
                                    ins = tensor.matmul(
                                        ps_f[:, buf + hf * 512: buf + hf * 512 + 512],
                                        tgT[e][:, t * 128:t * 128 + 128],
                                        VeT[e][:, hf * 512: hf * 512 + 512],
                                        start=False, stop=e == 7)
                            ins.then_inc(c.sem("s0"), 1)
                    block.tensor(tb)

                    def ab(scalar):
                        for t in range(NT):
                            buf = (t % 2) * D
                            scalar.wait_ge(c.sem("s0"), bT + t + 1)
                            scalar.activation(g_all[:, t * D:t * D + D],
                                              ps_f[:, buf:buf + D], AF.Silu)
                            scalar.sig("s1")
                    block.scalar(ab)
                c.fin(s0=bT + NT, s1=bA + NT)
            with c.block() as block:
                def sy(sync):
                    c.dma_push(sync, g_dram, g_all[:])
                    sync.wait_ge(c.sem("dma"), c.val["dma"])
                block.sync(sy)

        saT = [c.sb(L, f"l1_saT{rt}", [128, T], BF16) for rt in range(8)]
        with ExitStack() as Sa:
            KT2 = [c.sb(Sa, f"l1_KT{i}", [128, ND * T], BF16) for i in range(2)]
            QTr = [c.sb(Sa, f"l1_QTr{i}", [128, ND * 512], BF16)
                   for i in range(2)]
            Qst = c.sb(Sa, "l1_Qst", [128, ND * D], BF16)
            V_all2 = [c.sb(Sa, f"l1_V{i}", [128, NT * 64], BF16)
                      for i in range(2)]
            wst = [c.sb(Sa, f"l1_wst{i}", [128, 512], F32) for i in range(4)]
            wkbf = [c.sb(Sa, f"l1_wkbf{i}", [128, D], BF16) for i in range(2)]
            vbf = c.sb(Sa, "l1_vbf", [128, 512], BF16)
            bkrow = [c.sb(Sa, f"l1_bkrow{i}", [1, 2112], BF16)
                     for i in range(2)]
            bcol_bf = [c.sb(Sa, f"l1_bcolb{i}", [128, 16], BF16)
                       for i in range(2)]
            bcolf = [c.sb(Sa, f"l1_bcolf{i}", [128, 16], F32)
                     for i in range(2)]
            P2 = [c.sb(Sa, f"l1_P{i}", [128, T], BF16) for i in range(2)]
            PTb = c.sb(Sa, "l1_PTb", [128, T], BF16)
            c.attn_stat = c.sb(Sa, "l1_astat", [128, 24], F32)
            wsrc = w["L1_attn_Wkqv"].rearrange("h (r p) c -> h p r c", p=128)
            bsrc = w["L1_attn_bkqv"]

            def chain_of(g):
                ch = []
                for et in range(3):
                    ch += [("K", et, 0), ("K", et, 1)]
                ch.append(("V", 0, 0))
                ch += [("K", 3, 0), ("K", 3, 1), ("K", 4, 0), ("K", 4, 1)]
                ch += [("Q", 0, 0), ("Q", 0, 1), ("Q", 1, 0), ("Q", 1, 1)]
                ch += [("K", 5, 0), ("K", 5, 1)]
                ch += [("Q", 2, 0), ("Q", 2, 1), ("Q", 3, 0), ("Q", 3, 1)]
                ch += [("K", 6, 0), ("K", 6, 1)]
                ch += [("Q", 4, 0), ("Q", 4, 1), ("Q", 5, 0), ("Q", 5, 1)]
                ch += [("K", 7, 0), ("K", 7, 1)]
                ch += [("Q", 6, 0), ("Q", 6, 1), ("Q", 7, 0), ("Q", 7, 1)]
                return ch
            CH_N = 33
            U_RING = 2

            cvt_idx = {}
            unit_idx = {}
            cnt = {"cvt": 0, "unit": 0}

            def reg_cvt(g, kind, et, hf):
                cvt_idx[(g, kind, et, hf)] = cnt["cvt"]
                cnt["cvt"] += 1

            def reg_unit(g, kind, et, n0):
                unit_idx[(g, kind, et, n0)] = cnt["unit"]
                cnt["unit"] += 1

            b10, b11, b12, b13 = c.base("s10", "s11", "s12", "s13")
            QUEUES = ["dxa", "dxb", "dxc", "dxd"]

            def emit_dma_chunk(sync, g, kind, et, hf, ci):
                q = QUEUES[ci % 4]
                if ci >= 4:
                    sync.wait_ge(c.sem("s10"), b10 + ci - 3)
                if kind == "K":
                    src = wsrc[g, :, hf * 4:hf * 4 + 4, et * 128:et * 128 + 128]
                elif kind == "Q":
                    src = wsrc[g, :, hf * 4:hf * 4 + 4,
                               D + et * 128:D + et * 128 + 128]
                else:
                    src = wsrc[g, :, :, 2048:2112]
                r = 4 if kind != "V" else ND
                c.dma_push(sync,
                           wst[ci % 4][:].rearrange("p (r cw) -> p r cw", r=r),
                           src, sem=q)

            def emit_cvt(scalar, g, kind, et, hf, ci, dval):
                scalar.wait_ge(c.sem(QUEUES[ci % 4]), dval)
                if kind == "K":
                    key = (g, "K", et - 2, 3) if et >= 2 else \
                        (g - 1, "K", et + 6, 3)
                    dst = wkbf[et % 2][:, hf * 512:hf * 512 + 512]
                elif kind == "Q":
                    key = (g - 1, "Q", et, 3)
                    dst = Qst[:, et * D + hf * 512:et * D + hf * 512 + 512]
                else:
                    key = (g - 1, "V", 3, 0)
                    dst = vbf[:, :]
                if key in unit_idx:
                    scalar.wait_ge(c.sem("s11"), b11 + unit_idx[key] + 1)
                scalar.copy(dst, wst[ci % 4][:, :])
                scalar.sig("s10")

            def emit_unit_mm(tensor, g, kind, et, n0):
                u = unit_idx[(g, kind, et, n0)]
                tensor.wait_ge(c.sem("s13"), b13 + g + 1)
                ckey = (g, kind, et, 1) if kind != "V" else (g, "V", 0, 0)
                tensor.wait_ge(c.sem("s10"), b10 + cvt_idx[ckey] + 1)
                if u >= U_RING:
                    tensor.wait_ge(c.sem("s12"), b12 + u - U_RING + 1)
                ps = ps_u[u % U_RING]
                if kind != "V":
                    wsb = wkbf[et % 2] if kind == "K" else \
                        Qst[:, et * D:et * D + D]
                    for dc in range(ND):
                        ins = tensor.matmul(ps[:, 0:512],
                                            wsb[:, dc * 128:dc * 128 + 128],
                                            xnT[:, dc * T + n0 * 512:
                                                dc * T + n0 * 512 + 512],
                                            start=dc == 0, stop=dc == ND - 1)
                else:
                    for tt in range(4):
                        t = et * 4 + tt
                        for dc in range(ND):
                            tensor.matmul(
                                ps[:, tt * 64:tt * 64 + 64],
                                xnT[:, dc * T + t * 128:dc * T + t * 128 + 128],
                                vbf[:, dc * 64:dc * 64 + 64],
                                start=dc == 0, stop=False)
                        ins = tensor.matmul(ps[:, tt * 64:tt * 64 + 64],
                                            c.k["c_ones_bf"][0:1, 0:128],
                                            bkrow[g % 2][0:1, 2048:2112],
                                            start=False, stop=True)
                ins.then_inc(c.sem("s11"), 1)

            def emit_unit_evac(scalar, g, kind, et, n0):
                u = unit_idx[(g, kind, et, n0)]
                scalar.wait_ge(c.sem("s11"), b11 + u + 1)
                ps = ps_u[u % U_RING]
                if kind == "K":
                    scalar.activation(KT2[g % 2][:, et * T + n0 * 512:
                                               et * T + n0 * 512 + 512],
                                      ps[:, 0:512], AF.Identity,
                                      bias=bcolf[g % 2][:, et:et + 1],
                                      scale=1.0)
                elif kind == "Q":
                    scalar.activation(QTr[n0 % 2][:, et * 512:et * 512 + 512],
                                      ps[:, 0:512], AF.Identity,
                                      bias=bcolf[g % 2][:, 8 + et:9 + et],
                                      scale=1.0 / 32.0)
                else:
                    scalar.copy(V_all2[g % 2][:, et * 256:et * 256 + 256],
                                ps[:, 0:256])
                scalar.sig("s12")

            def emit_bias_prep(gpsimd, g):
                c.dma_push(gpsimd, bkrow[g % 2][0:1, :], bsrc[g:g + 1, :],
                           sem="dma2")
                gpsimd.wait_ge(c.sem("dma2"), c.val["dma2"])
                for et in range(ND):
                    c.dma_push(gpsimd, bcol_bf[g % 2][0:128, et:et + 1],
                               bkrow[g % 2][0:1, et * 128:et * 128 + 128],
                               sem="dma2")
                    c.dma_push(gpsimd, bcol_bf[g % 2][0:128, 8 + et:9 + et],
                               bkrow[g % 2][0:1, D + et * 128:
                                            D + et * 128 + 128],
                               sem="dma2")
                gpsimd.wait_ge(c.sem("dma2"), c.val["dma2"])
                gpsimd.tensor_copy(bcolf[g % 2][:, 0:8],
                                   bcol_bf[g % 2][:, 0:8])
                gpsimd.tensor_scalar_mul(bcolf[g % 2][:, 8:16],
                                         bcol_bf[g % 2][:, 8:16], 1.0 / 32.0)
                gpsimd.sig("s13")

            def sched_units(h):
                g = h + 1
                s = {0: [(h, "Q", et, 1) for et in range(4)],
                     1: [(h, "Q", et, 1) for et in range(4, 8)],
                     3: [(h, "Q", et, 2) for et in range(4)],
                     4: [(h, "Q", et, 2) for et in range(4, 8)],
                     7: [(h, "Q", et, 3) for et in range(4)],
                     8: [(h, "Q", et, 3) for et in range(4, 8)]}
                if h < H1 - 1:
                    for i, sl in enumerate([2, 5, 6, 9, 10, 11, 12, 13]):
                        s[sl] = [(g, "K", i, n0) for n0 in range(4)]
                    s[14] = [(g, "V", tg, 0) for tg in range(4)]
                    s[15] = [(g, "Q", et, 0) for et in range(ND)]
                return s

            def sched_cvts(h):
                if h >= H1 - 1:
                    return {}
                return {0: [0, 1, 2, 3], 2: [4, 5], 4: [6], 5: [7, 8],
                        6: [9, 10], 8: [11, 12, 13, 14],
                        9: [15, 16, 17, 18, 19, 20],
                        10: [21, 22, 23, 24, 25, 26],
                        11: [27, 28, 29, 30, 31, 32]}

            ch0 = chain_of(0)
            for kind, et, hf in ch0:
                reg_cvt(0, kind, et, hf)
            _u0 = {2: [("K", 0, n0) for n0 in range(4)],
                   5: [("K", 1, n0) for n0 in range(4)],
                   6: [("K", 2, n0) for n0 in range(4)],
                   9: [("K", 3, n0) for n0 in range(4)],
                   10: [("K", 4, n0) for n0 in range(4)],
                   11: [("K", 5, n0) for n0 in range(4)],
                   12: [("K", 6, n0) for n0 in range(4)],
                   13: [("K", 7, n0) for n0 in range(4)],
                   14: [("V", tg, 0) for tg in range(4)],
                   15: [("Q", et, 0) for et in range(ND)]}
            prolog_units = [u for s in range(NT) for u in _u0.get(s, [])]
            for kind, et, n0 in prolog_units:
                reg_unit(0, kind, et, n0)

            with ExitStack() as Sp:
                ps_u = [c.ps(Sp, f"l1_psu{i}", [128, 512], F32)
                        for i in range(2)]
                dv_snap = []
                with c.block() as block:
                    def sy(sync):
                        for ci, (kind, et, hf) in enumerate(ch0):
                            emit_dma_chunk(sync, 0, kind, et, hf, ci)
                            dv_snap.append(c.val[QUEUES[ci % 4]])
                    block.sync(sy)

                    def gp(gpsimd):
                        for g in range(2):
                            emit_bias_prep(gpsimd, g)
                    block.gpsimd(gp)

                    units0 = {2: [("K", 0, n0) for n0 in range(4)],
                              5: [("K", 1, n0) for n0 in range(4)],
                              6: [("K", 2, n0) for n0 in range(4)],
                              9: [("K", 3, n0) for n0 in range(4)],
                              10: [("K", 4, n0) for n0 in range(4)],
                              11: [("K", 5, n0) for n0 in range(4)],
                              12: [("K", 6, n0) for n0 in range(4)],
                              13: [("K", 7, n0) for n0 in range(4)],
                              14: [("V", tg, 0) for tg in range(4)],
                              15: [("Q", et, 0) for et in range(ND)]}
                    cvts0 = {0: [0, 1, 2, 3], 2: [4, 5], 4: [6], 5: [7, 8],
                             6: [9, 10], 8: [11, 12, 13, 14],
                             9: [15, 16, 17, 18, 19, 20],
                             10: [21, 22, 23, 24, 25, 26],
                             11: [27, 28, 29, 30, 31, 32]}

                    def sc(scalar):
                        for s in range(NT):
                            for kind, et, n0 in units0.get(s, []):
                                emit_unit_evac(scalar, 0, kind, et, n0)
                            for ci in cvts0.get(s, []):
                                kind, et, hf = ch0[ci]
                                emit_cvt(scalar, 0, kind, et, hf, ci,
                                         dv_snap[ci])
                    block.scalar(sc)

                    def tb0(tensor):
                        for kind, et, n0 in prolog_units:
                            emit_unit_mm(tensor, 0, kind, et, n0)
                    block.tensor(tb0)
                c.val["s10"] = b10 + cnt["cvt"]
                c.val["s11"] = b11 + cnt["unit"]
                c.val["s12"] = b12 + cnt["unit"]
                c.val["s13"] = b13 + 2

            with ExitStack() as Sp:
                ps_u = [c.ps(Sp, f"l1_psu{i}b", [128, 512], F32)
                        for i in range(2)]
                ps_Sb = c.ps(Sp, "l1_psS", [128, 2048], F32)
                ps_PT = c.ps(Sp, "l1_psPT", [128, 1024], BF16)
                ps_PV = [c.ps(Sp, "l1_psPV", [64, 128], F32)]

                def sink(scalar, hh, qt, ps):
                    rt, sub = hh // 2, (hh % 2) * 64
                    return scalar.copy(
                        saT[rt][sub:sub + 64, qt * 128:qt * 128 + 128],
                        ps[0:64, 0:128])

                heads = []
                for h in range(H1):
                    g = h + 1
                    su = sched_units(h)
                    scv = sched_cvts(h)
                    chain = chain_of(g) if h < H1 - 1 else []
                    cb = cnt["cvt"]
                    for kind, et, hf in chain:
                        reg_cvt(g, kind, et, hf)
                    for s in range(NT):
                        for (gg, kind, et, n0) in su.get(s, []):
                            reg_unit(gg, kind, et, n0)
                    thr = {0: b12 + unit_idx[(h, "Q", ND - 1, 0)] + 1}
                    for n0 in range(1, 4):
                        thr[4 * n0] = b12 + unit_idx[(h, "Q", ND - 1, n0)] + 1
                    dvals_h = []

                    def pre_s(tensor, j, thr=thr):
                        if j in thr:
                            tensor.wait_ge(c.sem("s12"), thr[j])

                    def t_hook(tensor, j, su=su):
                        for (gg, kind, et, n0) in su.get(j, []):
                            emit_unit_mm(tensor, gg, kind, et, n0)

                    def a_hook(scalar, j, su=su, scv=scv, chain=chain, cb=cb,
                               g=g, dvals_h=dvals_h):
                        for (gg, kind, et, n0) in su.get(j, []):
                            emit_unit_evac(scalar, gg, kind, et, n0)
                        for pos in scv.get(j, []):
                            kind, et, hf = chain[pos]
                            emit_cvt(scalar, g, kind, et, hf, cb + pos,
                                     dvals_h[pos])

                    tb, ab, vb, fin = attn_core_v2_streams(
                        c,
                        jobs=[(h, qt) for qt in range(NT)],
                        s_chunks=lambda hh, qt: [
                            (QTr[(qt // 4) % 2][:, et * 512 + (qt % 4) * 128:
                                 et * 512 + (qt % 4) * 128 + 128],
                             lambda k0, nn, et=et, hh=hh:
                             KT2[hh % 2][:, et * T + k0:et * T + k0 + nn])
                            for et in range(ND)],
                        klen_fn=lambda qt: (qt + 1) * 128,
                        v_lhsT_fn=lambda hh, kc:
                        V_all2[hh % 2][:, kc * 64:kc * 64 + 64],
                        dv=64, sink_fn=sink, P2=P2, S_sb2=[], PT=PTb,
                        ps_Sb=ps_Sb, ps_PT=ps_PT, ps_PV=ps_PV,
                        exp_scale=1.0, causal=True, wave_cols=1024,
                        pre_s_hook=pre_s, t_hook=t_hook, a_hook=a_hook)
                    fin()
                    c.val["s10"] = b10 + cnt["cvt"]
                    c.val["s11"] = b11 + cnt["unit"]
                    c.val["s12"] = b12 + cnt["unit"]
                    if h + 2 < H1:
                        c.val["s13"] = b13 + h + 3
                    heads.append((h, g, chain, cb, tb, ab, vb, dvals_h))

                with c.block() as block:
                    def sy(sync):
                        for (h, g, chain, cb, tb, ab, vb, dvals_h) in heads:
                            for pos, (kind, et, hf) in enumerate(chain):
                                emit_dma_chunk(sync, g, kind, et, hf,
                                               cb + pos)
                                dvals_h.append(
                                    c.val[QUEUES[(cb + pos) % 4]])
                    block.sync(sy)

                    def gp(gpsimd):
                        for h in range(H1 - 2):
                            gpsimd.wait_ge(
                                c.sem("s12"),
                                b12 + unit_idx[(h, "Q", ND - 1, 3)] + 1)
                            emit_bias_prep(gpsimd, h + 2)
                    block.gpsimd(gp)

                    def tall(tensor):
                        for rec in heads:
                            rec[4](tensor)
                    block.tensor(tall)

                    def aall(scalar):
                        for rec in heads:
                            rec[5](scalar)
                    block.scalar(aall)

                    def vall(vector):
                        for rec in heads:
                            rec[6](vector)
                    block.vector(vall)

        g_all = c.sb(L, "l1_g2", [128, NT * D], BF16)
        with c.block() as block:
            def sy(sync):
                c.dma_push(sync, g_all[:], g_dram)
                sync.wait_ge(c.sem("dma"), c.val["dma"])
            block.sync(sy)
        wp_bf = c.sb(L, "l1_wpbf", [128, ND * D], BF16)
        bp_bf = c.sb(L, "l1_bpbf", [1, D], BF16)
        with ExitStack() as Sw:
            wp_st = c.sb(Sw, "l1_wpst", [128, ND * D], F32)
            rstage = c.sb(Sw, "l1_rstage", [1, D], F32)
            emit_load_natural(c, w["L1_attn_Wp"], D, D, wp_bf, wp_st)
            emit_row_load(c, w["L1_attn_bp"], D, rstage, bp_bf)

        def s_psum(tensor, t, psum_ap):
            for nf in range(2):
                for rt in range(8):
                    tensor.matmul(
                        psum_ap[:, nf * 512: nf * 512 + 512],
                        saT[rt][:, t * 128: t * 128 + 128],
                        wp_bf[:, rt * D + nf * 512: rt * D + nf * 512 + 512],
                        start=rt == 0, stop=False)
                ins = tensor.matmul(
                    psum_ap[:, nf * 512: nf * 512 + 512],
                    c.k["c_ones_bf"][0:1, 0:128],
                    bp_bf[0:1, nf * 512: nf * 512 + 512],
                    start=False, stop=True)
            return ins

        if fuse_next is None:
            emit_combine(c, xin, xout, s_psum, g_all)
        else:
            emit_combine_ln(c, xin, xout, s_psum, g_all, fuse_next[0],
                            stats_next=fuse_next[1])


def emit_ln_stats_chain(c, vector, scalar_pre, stats, eps=1e-5):
    x_ap, sq_ap = scalar_pre
    vector.reduce_sum(stats[:, 0:1], x_ap, axis=AX.X)
    vector.reduce_sum(stats[:, 5:6], sq_ap, axis=AX.X)
    vector.tensor_scalar_mul(stats[:, 6:7], stats[:, 0:1], -1.0 / D)
    vector.tensor_mul(stats[:, 7:8], stats[:, 6:7], stats[:, 6:7])
    vector.tensor_scalar(stats[:, 1:2], stats[:, 5:6], 1.0 / D, eps,
                         ALU.mult, ALU.add)
    vector.tensor_sub(stats[:, 1:2], stats[:, 1:2], stats[:, 7:8])


def emit_layer2(c, xin, xout, w, xnT_in=None, lnstats_in=None,
                fuse_next=None):
    nc = c.nc
    with ExitStack() as L:
        if xnT_in is None:
            xnT = c.sb(L, "l2_xnT", [128, ND * T], BF16)
            lnstats = c.sb(L, "l2_lnst", [128, 2 * NT], F32)
        s_all = c.sb(L, "l2_s", [128, NT * D], BF16)
        if xnT_in is None:
            emit_ln(c, xin, xnT, stats_out=lnstats)
        else:
            xnT, lnstats = xnT_in, lnstats_in

        with ExitStack() as Sa:
            rstage = c.sb(Sa, "l2_rstage", [1, D], F32)
            rbf = c.sb(Sa, "l2_rbf", [1, D], BF16)
            saT = [c.sb(Sa, f"l2_saT{rt}", [128, T], BF16) for rt in range(4)]
            QT = [c.sb(Sa, f"l2_QT{h}", [64, T], BF16) for h in range(8)]
            KTt = c.sb(Sa, "l2_KT", [64, T], BF16)
            VTt = c.sb(Sa, "l2_VT", [64, T], BF16)
            V_tok = c.sb(Sa, "l2_Vtok", [128, NT * 64], BF16)
            with ExitStack() as S1:
                ln2T = c.sb(S1, "l2_ln2T", [128, ND * T], BF16)
                nw_bc = c.sb(S1, "l2_nw", [128, D], F32)
                nb_bc = c.sb(S1, "l2_nb", [128, D], F32)
                for nm, dst in [("L2_attn_norm_w", nw_bc),
                                ("L2_attn_norm_b", nb_bc)]:
                    emit_row_load(c, w[nm], D, rstage, rbf)
                    emit_bcast_row(c, rbf, D, dst)

                with ExitStack() as S2:
                    xstage = c.sb(S2, "l2_xst", [128, 2 * D], F32)
                    xnf = c.sb(S2, "l2_xnf", [128, D], F32)
                    sq = c.sb(S2, "l2_sq", [128, D], F32)
                    t1f = c.sb(S2, "l2_t1f", [128, D], F32)
                    ln2_bf = c.sb(S2, "l2_ln2bf", [128, D], BF16)
                    stats = c.sb(S2, "l2_st2", [128, 8], F32)
                    ps_tr = c.ps(S2, "l2_pstr", [128, D], BF16)
                    xsrc = xin.rearrange("(t p) d -> p t d", p=128)
                    bA, bV1, bA2, bV2, bA3, bV3, bT1, bV4 = c.base(
                        "s0", "s1", "s2", "s3", "s4", "s5", "s6", "s7")
                    bdx = [c.val["dxa"], c.val["dxb"]]
                    dsem = ["dxa", "dxb"]
                    dval = [bdx[t % 2] + 16 * (t // 2 + 1) for t in range(NT)]
                    with c.block() as block:
                        def sy(sync):
                            for t in range(NT):
                                if t >= 2:
                                    sync.wait_ge(c.sem("s0"), bA + t - 1)
                                c.dma_push(sync,
                                           xstage[:, (t % 2) * D:(t % 2) * D + D],
                                           xsrc[:, t, :], sem=dsem[t % 2])
                            sync.wait_ge(c.sem("dxa"), dval[NT - 2])
                            sync.wait_ge(c.sem("dxb"), dval[NT - 1])
                        block.sync(sy)

                        def ab(scalar):
                            for t in range(NT):
                                xt = xstage[:, (t % 2) * D:(t % 2) * D + D]
                                scalar.wait_ge(c.sem(dsem[t % 2]), dval[t])
                                if t > 0:
                                    scalar.wait_ge(c.sem("s1"), bV1 + t)
                                scalar.activation(
                                    xnf[:], xt, AF.Identity,
                                    bias=lnstats[:, 2 * t + 1:2 * t + 2],
                                    scale=lnstats[:, 2 * t:2 * t + 1])
                                scalar.square(sq[:], xnf[:])
                                scalar.sig("s0")
                                scalar.wait_ge(c.sem("s1"), bV1 + t + 1)
                                scalar.activation(stats[:, 2:3], stats[:, 1:2],
                                                  AF.Sqrt)
                                scalar.sig("s2")
                                scalar.wait_ge(c.sem("s3"), bV2 + t + 1)
                                scalar.activation(t1f[:], xnf[:], AF.Identity,
                                                  bias=stats[:, 4:5],
                                                  scale=stats[:, 3:4])
                                scalar.sig("s4")
                        block.scalar(ab)

                        def vb(vector):
                            for t in range(NT):
                                vector.wait_ge(c.sem("s0"), bA + t + 1)
                                emit_ln_stats_chain(c, vector, (xnf[:], sq[:]),
                                                    stats)
                                vector.sig("s1")
                                vector.wait_ge(c.sem("s2"), bA2 + t + 1)
                                vector.reciprocal(stats[:, 3:4], stats[:, 2:3])
                                vector.tensor_mul(stats[:, 4:5], stats[:, 6:7],
                                                  stats[:, 3:4])
                                vector.sig("s3")
                                vector.wait_ge(c.sem("s4"), bA3 + t + 1)
                                vector.tensor_mul(t1f[:], t1f[:], nw_bc[:])
                                if t > 0:
                                    vector.wait_ge(c.sem("s6"), bT1 + t)
                                vector.tensor_add(ln2_bf[:], t1f[:], nb_bc[:])
                                vector.sig("s5")
                                vector.wait_ge(c.sem("s6"), bT1 + t + 1)
                                dst = ln2T[:].rearrange("p (cc n) -> p cc n", cc=ND)
                                vector.tensor_copy(
                                    dst[:, :, t * 128:t * 128 + 128],
                                    ps_tr[:].rearrange("p (cc n) -> p cc n", cc=ND))
                                vector.sig("s7")
                        block.vector(vb)

                        def tb(tensor):
                            for t in range(NT):
                                tensor.wait_ge(c.sem("s5"), bV3 + t + 1)
                                if t > 0:
                                    tensor.wait_ge(c.sem("s7"), bV4 + t)
                                for cc in range(ND):
                                    ins = tensor.transpose(
                                        ps_tr[:, cc * 128:cc * 128 + 128],
                                        ln2_bf[:, cc * 128:cc * 128 + 128],
                                        c.k["c_ident_bf"][:, :])
                                ins.then_inc(c.sem("s6"), 1)
                        block.tensor(tb)
                    c.fin(s0=bA + NT, s1=bV1 + NT, s2=bA2 + NT, s3=bV2 + NT,
                          s4=bA3 + NT, s5=bV3 + NT, s6=bT1 + NT, s7=bV4 + NT)

                with ExitStack() as Sw:
                    wq_st = c.sb(Sw, "l2_wqst", [128, ND * 512], F32)
                    wq_bf = c.sb(Sw, "l2_wqbf", [128, ND * 512], BF16)
                    wkv_bf = c.sb(Sw, "l2_wkvbf", [128, ND * 128], BF16)
                    emit_load_natural(c, w["L2_attn_Wq"], D, 512, wq_bf, wq_st)
                    emit_load_natural(c, w["L2_attn_Wkv"], D, 128, wkv_bf, wq_st)
                    for h in range(8):
                        with ExitStack() as Sp:
                            ps_q = c.ps(Sp, "l2_psq", [64, T], F32)
                            with c.block() as block:
                                def tb(tensor, h=h):
                                    for n0 in range(0, T, 512):
                                        for cc in range(ND):
                                            tensor.matmul(
                                                ps_q[:, n0:n0 + 512],
                                                wq_bf[:, cc * 512 + h * 64:
                                                      cc * 512 + h * 64 + 64],
                                                ln2T[:, cc * T + n0:
                                                     cc * T + n0 + 512],
                                                start=cc == 0, stop=cc == ND - 1)
                                block.tensor(tb)
                            with c.block() as block:
                                def ab(scalar, h=h):
                                    scalar.activation(QT[h][:, :], ps_q[:, :],
                                                      AF.Copy, scale=0.125)
                                block.scalar(ab)
                    with ExitStack() as Sp:
                        ps_kv = c.ps(Sp, "l2_pskv", [128, T], F32)
                        with c.block() as block:
                            def tb(tensor):
                                for n0 in range(0, T, 512):
                                    for cc in range(ND):
                                        tensor.matmul(
                                            ps_kv[:, n0:n0 + 512],
                                            wkv_bf[:, cc * 128:cc * 128 + 128],
                                            ln2T[:, cc * T + n0: cc * T + n0 + 512],
                                            start=cc == 0, stop=cc == ND - 1)
                            block.tensor(tb)
                        with c.block() as block:
                            def ab(scalar):
                                scalar.copy(KTt[:, :], ps_kv[0:64, :])
                            block.scalar(ab)

                            def vb(vector):
                                vector.tensor_copy(VTt[:, :], ps_kv[64:128, :])
                            block.vector(vb)
                    with ExitStack() as Sp:
                        ps_vt = c.ps(Sp, "l2_psvt", [128, NT * 64], BF16)
                        with c.block() as block:
                            def tb(tensor):
                                for t in range(NT):
                                    tensor.transpose(
                                        ps_vt[:, t * 64:t * 64 + 64],
                                        VTt[:, t * 128:t * 128 + 128],
                                        c.k["c_ident_bf"][0:64, 0:64])
                            block.tensor(tb)
                        with c.block() as block:
                            def vb(vector):
                                vector.tensor_copy(V_tok[:, :], ps_vt[:, :])
                            block.vector(vb)

            wo_bf = c.sb(Sa, "l2_wobf", [128, 4 * D], BF16)
            onw_bc = c.sb(Sa, "l2_onw", [128, D], F32)
            onb_bc = c.sb(Sa, "l2_onb", [128, D], F32)
            for nm, dst in [("L2_attn_out_norm_w", onw_bc),
                            ("L2_attn_out_norm_b", onb_bc)]:
                emit_row_load(c, w[nm], D, rstage, rbf)
                emit_bcast_row(c, rbf, D, dst)
            with ExitStack() as Sw2:
                wo_st = c.sb(Sw2, "l2_wost", [128, 4 * D], F32)
                emit_load_natural(c, w["L2_attn_Wo"], 512, D, wo_bf, wo_st)

            P2 = [c.sb(Sa, f"l2_P{i}", [128, T], BF16) for i in range(3)]
            S_sb2 = [c.sb(Sa, f"l2_Ssb{i}", [128, T], F32) for i in range(2)]
            PTb = c.sb(Sa, "l2_PTb", [128, T], BF16)
            c.attn_stat = c.sb(Sa, "l2_astat", [128, 24], F32)
            with ExitStack() as Sp:
                ps_Sb = c.ps(Sp, "l2_psS", [128, 2048], F32)
                ps_PT = c.ps(Sp, "l2_psPT", [128, T], BF16)
                ps_PV = [c.ps(Sp, f"l2_psPV{i}", [64, 128], F32)
                         for i in range(2)]

                def sink(scalar, h, qt, ps):
                    rt, sub = h // 2, (h % 2) * 64
                    return scalar.copy(
                        saT[rt][sub:sub + 64, qt * 128:qt * 128 + 128],
                        ps[0:64, 0:128])

                emit_attn_core_v2(
                    c,
                    jobs=[(h, qt) for h in range(8) for qt in range(NT)],
                    s_chunks=lambda h, qt: [
                        (QT[h][0:64, qt * 128:qt * 128 + 128],
                         lambda k0, nn: KTt[0:64, k0:k0 + nn])],
                    klen_fn=lambda qt: T,
                    v_lhsT_fn=lambda h, kc: V_tok[:, kc * 64:kc * 64 + 64],
                    dv=64, sink_fn=sink, P2=P2, S_sb2=S_sb2, PT=PTb,
                    ps_Sb=ps_Sb, ps_PT=ps_PT, ps_PV=ps_PV,
                    exp_scale=1.0, causal=False)

            with ExitStack() as Sp:
                ps_y = c.ps(Sp, "l2_psy", [128, 2 * D], F32)
                y_tok = c.sb(Sp, "l2_ytok", [128, D], F32)
                sq = c.sb(Sp, "l2_sqy", [128, D], F32)
                t1f = c.sb(Sp, "l2_t1fy", [128, D], F32)
                stats = c.sb(Sp, "l2_sty", [128, 8], F32)
                bT, bA1, bV1, bA2, bV2, bA3, bV3 = c.base(
                    "s0", "s1", "s2", "s3", "s4", "s5", "s6")
                with c.block() as block:
                    def tb(tensor):
                        for t in range(NT):
                            buf = (t % 2) * D
                            if t >= 2:
                                tensor.wait_ge(c.sem("s1"), bA1 + t - 1)
                            for nf in range(2):
                                for ic in range(4):
                                    ins = tensor.matmul(
                                        ps_y[:, buf + nf * 512: buf + nf * 512 + 512],
                                        saT[ic][:, t * 128: t * 128 + 128],
                                        wo_bf[:, ic * D + nf * 512:
                                              ic * D + nf * 512 + 512],
                                        start=ic == 0, stop=ic == 3)
                            ins.then_inc(c.sem("s0"), 1)
                    block.tensor(tb)

                    def ab(scalar):
                        for t in range(NT):
                            buf = (t % 2) * D
                            scalar.wait_ge(c.sem("s0"), bT + t + 1)
                            if t > 0:
                                scalar.wait_ge(c.sem("s2"), bV1 + t)
                            scalar.copy(y_tok[:], ps_y[:, buf:buf + D])
                            scalar.square(sq[:], y_tok[:])
                            scalar.sig("s1")
                            scalar.wait_ge(c.sem("s2"), bV1 + t + 1)
                            scalar.activation(stats[:, 2:3], stats[:, 1:2],
                                              AF.Sqrt)
                            scalar.sig("s3")
                            scalar.wait_ge(c.sem("s4"), bV2 + t + 1)
                            scalar.activation(t1f[:], y_tok[:], AF.Identity,
                                              bias=stats[:, 4:5],
                                              scale=stats[:, 3:4])
                            scalar.sig("s5")
                    block.scalar(ab)

                    def vb(vector):
                        for t in range(NT):
                            vector.wait_ge(c.sem("s1"), bA1 + t + 1)
                            emit_ln_stats_chain(c, vector, (y_tok[:], sq[:]),
                                                stats)
                            vector.sig("s2")
                            vector.wait_ge(c.sem("s3"), bA2 + t + 1)
                            vector.reciprocal(stats[:, 3:4], stats[:, 2:3])
                            vector.tensor_mul(stats[:, 4:5], stats[:, 6:7],
                                              stats[:, 3:4])
                            vector.sig("s4")
                            vector.wait_ge(c.sem("s5"), bA3 + t + 1)
                            vector.tensor_mul(t1f[:], t1f[:], onw_bc[:])
                            vector.tensor_add(s_all[:, t * D:t * D + D], t1f[:],
                                              onb_bc[:])
                            vector.sig("s6")
                    block.vector(vb)
                c.fin(s0=bT + NT, s1=bA1 + NT, s2=bV1 + NT, s3=bA2 + NT,
                      s4=bV2 + NT, s5=bA3 + NT, s6=bV3 + NT)

        g_all = c.sb(L, "l2_g", [128, NT * D], BF16)
        with ExitStack() as Sf:
            murT = c.sb(Sf, "l2_murT", [128, ND], F32)
            mukT = c.sb(Sf, "l2_mukT", [128, ND], F32)
            with ExitStack() as Smu:
                rstage = c.sb(Smu, "l2_rstage2", [1, D], F32)
                for nm, dst in [("L2_ffn_mu_r", murT), ("L2_ffn_mu_k", mukT)]:
                    emit_row_load(c, w[nm], D, rstage)
                    with ExitStack() as Sp:
                        ps_mu = c.ps(Sp, "l2_psmu", [128, ND], F32)
                        with c.block() as block:
                            def tb(tensor, rstage=rstage):
                                for cc in range(ND):
                                    tensor.transpose(
                                        ps_mu[:, cc:cc + 1],
                                        rstage[0:1, cc * 128:cc * 128 + 128],
                                        c.k["c_ident_f32"][0:1, 0:1])
                            block.tensor(tb)
                        with c.block() as block:
                            def vb(vector, dst=dst):
                                vector.tensor_copy(dst[:, :], ps_mu[:, :])
                            block.vector(vb)

            def emit_mix(dstT, muT):
                with c.block() as block:
                    def vb(vector):
                        for cc in range(ND):
                            xc = xnT[:, cc * T: cc * T + T]
                            dc = dstT[:, cc * T: cc * T + T]
                            vector.tensor_sub(dc[:, 1:T], xc[:, 1:T],
                                              xc[:, 0:T - 1])
                            vector.scalar_tensor_tensor(
                                dc[:, 1:T], dc[:, 1:T], muT[:, cc:cc + 1],
                                xc[:, 0:T - 1], ALU.mult, ALU.add)
                            vector.tensor_scalar_mul(dc[:, 0:1], xc[:, 0:1],
                                                     muT[:, cc:cc + 1])
                    block.vector(vb)

            kaT = c.sb(Sf, "l2_kaT", [128, ND * T], BF16)
            with ExitStack() as Sk:
                xkT = c.sb(Sk, "l2_xkT", [128, ND * T], BF16)
                WkT = c.sb(Sk, "l2_WkT", [128, ND * D], BF16)
                emit_load_T_chunked(c, w["L2_ffn_W_k"], D, D, WkT, "l2wk", CH=256)
                emit_mix(xkT, mukT)
                for ot in range(ND):
                    with ExitStack() as Sp:
                        ps_k = c.ps(Sp, "l2_psk", [128, T], F32)
                        with c.block() as block:
                            def tb(tensor, ot=ot):
                                for n0 in range(0, T, 512):
                                    for cc in range(ND):
                                        tensor.matmul(
                                            ps_k[:, n0:n0 + 512],
                                            WkT[:, cc * D + ot * 128:
                                                cc * D + ot * 128 + 128],
                                            xkT[:, cc * T + n0: cc * T + n0 + 512],
                                            start=cc == 0, stop=cc == ND - 1)
                            block.tensor(tb)
                        with c.block() as block:
                            def ab(scalar, ot=ot):
                                scalar.activation(kaT[:, ot * T: ot * T + T],
                                                  ps_k[:, :], AF.Relu)
                            block.scalar(ab)
                        with c.block() as block:
                            def vb(vector, ot=ot):
                                kc = kaT[:, ot * T: ot * T + T]
                                vector.tensor_mul(kc, kc, kc)
                            block.vector(vb)

            with ExitStack() as Sr:
                xrT = c.sb(Sr, "l2_xrT", [128, ND * T], BF16)
                WrT = c.sb(Sr, "l2_WrT", [128, ND * D], BF16)
                WvT = c.sb(Sr, "l2_WvT", [128, ND * D], BF16)
                r_tile = c.sb(Sr, "l2_rtile", [128, D], BF16)
                tmp_bf = c.sb(Sr, "l2_tmpbf", [128, D], BF16)
                sgm_bf = c.sb(Sr, "l2_sgmbf", [128, D], BF16)
                emit_load_T_chunked(c, w["L2_ffn_W_r"], D, D, WrT, "l2wr", CH=256)
                emit_load_T_chunked(c, w["L2_ffn_W_v"], D, D, WvT, "l2wv", CH=256)
                emit_mix(xrT, murT)
                with ExitStack() as Sp:
                    ps_r = c.ps(Sp, "l2_psr", [128, D], F32)
                    ps_v = c.ps(Sp, "l2_psv", [128, D], F32)
                    bT, bA1, bV1, bA2 = c.base("s0", "s1", "s2", "s3")
                    with c.block() as block:
                        def tb(tensor):
                            for t in range(NT):
                                if t > 0:
                                    tensor.wait_ge(c.sem("s1"), bA1 + t)
                                for nf in range(2):
                                    for cc in range(ND):
                                        ins = tensor.matmul(
                                            ps_r[:, nf * 512: nf * 512 + 512],
                                            xrT[:, cc * T + t * 128:
                                                cc * T + t * 128 + 128],
                                            WrT[:, cc * D + nf * 512:
                                                cc * D + nf * 512 + 512],
                                            start=cc == 0, stop=cc == ND - 1)
                                ins.then_inc(c.sem("s0"), 1)
                                if t > 0:
                                    tensor.wait_ge(c.sem("s2"), bV1 + t)
                                for nf in range(2):
                                    for cc in range(ND):
                                        ins = tensor.matmul(
                                            ps_v[:, nf * 512: nf * 512 + 512],
                                            kaT[:, cc * T + t * 128:
                                                cc * T + t * 128 + 128],
                                            WvT[:, cc * D + nf * 512:
                                                cc * D + nf * 512 + 512],
                                            start=cc == 0, stop=cc == ND - 1)
                                ins.then_inc(c.sem("s0"), 1)
                        block.tensor(tb)

                        def ab(scalar):
                            for t in range(NT):
                                scalar.wait_ge(c.sem("s0"), bT + 2 * t + 1)
                                scalar.activation(r_tile[:], ps_r[:, :],
                                                  AF.Sigmoid)
                                scalar.sig("s1")
                                scalar.wait_ge(c.sem("s2"), bV1 + t + 1)
                                scalar.activation(sgm_bf[:], tmp_bf[:],
                                                  AF.Sigmoid)
                                scalar.sig("s3")
                        block.scalar(ab)

                        def vb(vector):
                            for t in range(NT):
                                vector.wait_ge(c.sem("s0"), bT + 2 * t + 2)
                                vector.wait_ge(c.sem("s1"), bA1 + t + 1)
                                if t > 0:
                                    vector.wait_ge(c.sem("s3"), bA2 + t)
                                vector.tensor_mul(tmp_bf[:], ps_v[:, :],
                                                  r_tile[:])
                                vector.sig("s2")
                                vector.wait_ge(c.sem("s3"), bA2 + t + 1)
                                vector.tensor_mul(g_all[:, t * D:t * D + D],
                                                  tmp_bf[:], sgm_bf[:])
                                vector.sig("s4")
                        block.vector(vb)
                    c.fin(s0=bT + 2 * NT, s1=bA1 + NT, s2=bV1 + NT, s3=bA2 + NT,
                          s4=c.val["s4"] + NT)

        if fuse_next is None:
            emit_combine(c, xin, xout, None, g_all, s_sbuf=s_all)
        else:
            emit_combine_ln(c, xin, xout, None, g_all, fuse_next[0],
                            stats_next=fuse_next[1], s_sbuf=s_all)


D4 = [0.4829629131445341, 0.8365163037378079, 0.2241438680420134,
      -0.1294095225512604]
N_SCALES = 11
OFFSETS = sorted({(1 << j) * tau for j in range(N_SCALES) for tau in range(4)
                  if (1 << j) * tau < T})
HCO = np.zeros((N_SCALES, len(OFFSETS)), dtype=np.float64)
for _j in range(N_SCALES):
    for _tau, _hc in enumerate(D4):
        _o = (1 << _j) * _tau
        if _o < T:
            HCO[_j, OFFSETS.index(_o)] += _hc


def l3_host_constants():
    no = len(OFFSETS)
    c_A = np.zeros((128, no * 4), dtype=np.float64)
    for oi in range(no):
        for hl in range(4):
            for s in range(N_SCALES):
                c_A[hl * 32 + s, oi * 4 + hl] = HCO[s, oi]
    c_Z = np.zeros((128, 4), dtype=np.float64)
    for hl in range(4):
        for s in range(N_SCALES):
            c_Z[hl * 32 + s, hl] = 1.0
    c_Zbc = np.zeros((4, 128), dtype=np.float64)
    for p in range(128):
        c_Zbc[p // 32, p] = 1.0
    c_sel = np.zeros((128, 2 * 4), dtype=np.float64)
    for rtl in range(2):
        for p in range(128):
            c_sel[p, rtl * 4 + 2 * rtl + p // 64] = 1.0
    c_bc = np.zeros((4, 2 * 128), dtype=np.float64)
    for rtl in range(2):
        for p in range(128):
            c_bc[2 * rtl + p // 64, rtl * 128 + p] = 1.0
    c_Abc = np.zeros((128, no * 2 * 128), dtype=np.float64)
    for oi in range(no):
        for i in range(2):
            for p in range(128):
                c_Abc[:, (oi * 2 + i) * 128 + p] = \
                    c_A[:, oi * 4 + 2 * i + p // 64]
    c_selbc = np.zeros((128, 256), dtype=np.float64)
    for i in range(2):
        for p in range(128):
            c_selbc[:, i * 128 + p] = c_sel[:, 6 * i + p // 64]
    return {"c_A": c_A.astype(BF), "c_Z": c_Z.astype(BF),
            "c_Zbc": c_Zbc.astype(BF), "c_sel": c_sel.astype(BF),
            "c_bc": c_bc.astype(BF), "c_Abc": c_Abc.astype(BF),
            "c_selbc": c_selbc.astype(BF)}


L3_CONSTS = [("c_Z", [128, 4], BF16),
             ("c_Zbc", [4, 128], BF16),
             ("c_bc", [4, 256], BF16),
             ("c_Abc", [128, len(OFFSETS) * 2 * 128], BF16),
             ("c_selbc", [128, 256], BF16)]


def emit_layer3(c, xin, xout, w, xnT_in=None):
    nc = c.nc
    NO = len(OFFSETS)
    with ExitStack() as L:
        if xnT_in is None:
            xnT = c.sb(L, "l3_xnT", [128, ND * T], BF16)
            emit_ln(c, xin, xnT)
        else:
            xnT = xnT_in
        g_all = c.sb(L, "l3_g", [128, NT * D], BF16)

        with ExitStack() as Sf:
            wdT = c.sb(Sf, "l3_wdT", [128, 24 * D], BF16)
            emit_load_T_chunked(c, w["L3_ffn_w_down"], D, 3072, wdT, "l3wd",
                                src_dtype=I32)
            hT = c.sb(Sf, "l3_hT", [128, 24 * (T // 2)], BF16)
            wg_st2 = [c.sb(Sf, f"l3_wgst{i}", [128, D], I32) for i in range(2)]
            wu_st2 = [c.sb(Sf, f"l3_wust{i}", [128, D], I32) for i in range(2)]
            wg_bf2 = [c.sb(Sf, f"l3_wgbf{i}", [128, D], BF16) for i in range(2)]
            wu_bf2 = [c.sb(Sf, f"l3_wubf{i}", [128, D], BF16) for i in range(2)]
            wgT2 = [c.sb(Sf, f"l3_wgT{i}", [128, D], BF16) for i in range(2)]
            wuT2 = [c.sb(Sf, f"l3_wuT{i}", [128, D], BF16) for i in range(2)]
            tmp_si = [c.sb(Sf, f"l3_tmpsi{i}", [128, T // 2], BF16)
                      for i in range(2)]
            NH = T // 2
            gsrc = w["L3_ffn_w_gate"].rearrange("(r p) c -> p r c", p=128)
            usrc = w["L3_ffn_w_up"].rearrange("(r p) c -> p r c", p=128)
            for nh in range(2):
                with ExitStack() as Sp:
                    ps_t = [c.ps(Sp, f"l3_pswt{i}", [128, 2 * D], BF16)
                            for i in range(2)]
                    ps_gate = c.ps(Sp, "l3_psga", [128, D], F32)
                    ps_up = c.ps(Sp, "l3_psup", [128, D], F32)
                    bA, bT1, bV1, bT2, bT3, bSi, bH = c.base(
                        "s0", "s1", "s2", "s3", "s4", "s5", "s6")
                    dsems = ["dxa", "dxb"]
                    bdx = [c.val["dxa"], c.val["dxb"]]
                    dvals = [bdx[cc % 2] + 32 * (cc // 2 + 1) for cc in range(24)]
                    with c.block() as block:
                        def sy(sync, nh=nh):
                            for cc in range(24):
                                if cc >= 2:
                                    sync.wait_ge(c.sem("s0"), bA + cc - 1)
                                c.dma_push(sync, wg_st2[cc % 2][:],
                                           gsrc[:, cc, :], sem=dsems[cc % 2])
                                c.dma_push(sync, wu_st2[cc % 2][:],
                                           usrc[:, cc, :], sem=dsems[cc % 2])
                        block.sync(sy)

                        def sc(scalar, nh=nh):
                            for cc in range(24):
                                scalar.wait_ge(c.sem(dsems[cc % 2]), dvals[cc])
                                if cc >= 2:
                                    scalar.wait_ge(c.sem("s1"), bT1 + cc - 1)
                                scalar.copy(wg_bf2[cc % 2][:], wg_st2[cc % 2][:])
                                scalar.copy(wu_bf2[cc % 2][:], wu_st2[cc % 2][:])
                                scalar.sig("s0")
                                if cc >= 1:
                                    scalar.wait_ge(c.sem("s3"), bT2 + cc)
                                    if cc >= 3:
                                        scalar.wait_ge(c.sem("s6"), bH + cc - 2)
                                    scalar.activation(tmp_si[(cc - 1) % 2][:],
                                                      ps_gate[:, :], AF.Silu,
                                                      scale=1.0 / 127.0)
                                    scalar.sig("s5")
                            scalar.wait_ge(c.sem("s3"), bT2 + 24)
                            scalar.wait_ge(c.sem("s6"), bH + 22)
                            scalar.activation(tmp_si[23 % 2][:], ps_gate[:, :],
                                              AF.Silu, scale=1.0 / 127.0)
                            scalar.sig("s5")
                        block.scalar(sc)

                        def tb(tensor, nh=nh):
                            for cc in range(24):
                                tensor.wait_ge(c.sem("s0"), bA + cc + 1)
                                if cc >= 2:
                                    tensor.wait_ge(c.sem("s2"), bV1 + cc - 1)
                                for dc in range(ND):
                                    tensor.transpose(
                                        ps_t[cc % 2][:, dc * 128:dc * 128 + 128],
                                        wg_bf2[cc % 2][:, dc * 128:dc * 128 + 128],
                                        c.k["c_ident_bf"][:, :])
                                    ins = tensor.transpose(
                                        ps_t[cc % 2][:, D + dc * 128:
                                                      D + dc * 128 + 128],
                                        wu_bf2[cc % 2][:, dc * 128:dc * 128 + 128],
                                        c.k["c_ident_bf"][:, :])
                                ins.then_inc(c.sem("s1"), 1)
                                tensor.wait_ge(c.sem("s2"), bV1 + cc + 1)
                                if cc >= 1:
                                    tensor.wait_ge(c.sem("s5"), bSi + cc)
                                for n0 in range(0, NH, 512):
                                    for dc in range(ND):
                                        ins = tensor.matmul(
                                            ps_gate[:, n0:n0 + 512],
                                            wgT2[cc % 2][:, dc * 128:dc * 128 + 128],
                                            xnT[:, dc * T + nh * NH + n0:
                                                dc * T + nh * NH + n0 + 512],
                                            start=dc == 0, stop=dc == ND - 1)
                                ins.then_inc(c.sem("s3"), 1)
                                if cc >= 1:
                                    tensor.wait_ge(c.sem("s6"), bH + cc)
                                for n0 in range(0, NH, 512):
                                    for dc in range(ND):
                                        ins = tensor.matmul(
                                            ps_up[:, n0:n0 + 512],
                                            wuT2[cc % 2][:, dc * 128:dc * 128 + 128],
                                            xnT[:, dc * T + nh * NH + n0:
                                                dc * T + nh * NH + n0 + 512],
                                            start=dc == 0, stop=dc == ND - 1)
                                ins.then_inc(c.sem("s4"), 1)
                        block.tensor(tb)

                        def vb(vector, nh=nh):
                            for cc in range(24):
                                vector.wait_ge(c.sem("s1"), bT1 + cc + 1)
                                if cc >= 2:
                                    vector.wait_ge(c.sem("s4"), bT3 + cc - 1)
                                vector.tensor_copy(wgT2[cc % 2][:],
                                                   ps_t[cc % 2][:, 0:D])
                                vector.tensor_copy(wuT2[cc % 2][:],
                                                   ps_t[cc % 2][:, D:2 * D])
                                vector.sig("s2")
                                vector.wait_ge(c.sem("s4"), bT3 + cc + 1)
                                vector.wait_ge(c.sem("s5"), bSi + cc + 1)
                                vector.scalar_tensor_tensor(
                                    hT[:, cc * NH: cc * NH + NH], ps_up[:, :],
                                    1.0 / 127.0, tmp_si[cc % 2][:],
                                    ALU.mult, ALU.mult)
                                vector.sig("s6")
                        block.vector(vb)
                    c.fin(s0=bA + 24, s1=bT1 + 24, s2=bV1 + 24, s3=bT2 + 24,
                          s4=bT3 + 24, s5=bSi + 24, s6=bH + 24)
                with ExitStack() as Sp:
                    ps_d = c.ps(Sp, "l3_psd", [128, 2 * D], F32)
                    bT, bA = c.base("s0", "s1")
                    with c.block() as block:
                        def tb(tensor, nh=nh):
                            for tt in range(8):
                                buf = (tt % 2) * D
                                if tt >= 2:
                                    tensor.wait_ge(c.sem("s1"), bA + tt - 1)
                                for nf in range(2):
                                    for cc in range(24):
                                        ins = tensor.matmul(
                                            ps_d[:, buf + nf * 512:
                                                 buf + nf * 512 + 512],
                                            hT[:, cc * NH + tt * 128:
                                               cc * NH + tt * 128 + 128],
                                            wdT[:, cc * D + nf * 512:
                                                cc * D + nf * 512 + 512],
                                            start=cc == 0, stop=cc == 23)
                                ins.then_inc(c.sem("s0"), 1)
                        block.tensor(tb)

                        def ab(scalar, nh=nh):
                            for tt in range(8):
                                buf = (tt % 2) * D
                                scalar.wait_ge(c.sem("s0"), bT + tt + 1)
                                t = nh * 8 + tt
                                scalar.activation(g_all[:, t * D:t * D + D],
                                                  ps_d[:, buf:buf + D],
                                                  AF.Silu, scale=1.0 / 127.0)
                                scalar.sig("s1")
                        block.scalar(ab)
                    c.fin(s0=bT + 8, s1=bA + 8)

        ggT = [c.sb(L, f"l3_ggT{rt}", [128, T], BF16) for rt in range(8)]
        with ExitStack() as Sa:
            sg_sb = c.sb(Sa, "l3_sg", [11, 16], F32)
            wqsT = c.sb(Sa, "l3_wqsT", [128, 11], BF16)
            sp_cols = c.sb(Sa, "l3_spc", [4, 4], F32)
            with ExitStack() as Ss:
                row_st = c.sb(Ss, "l3_rowst", [1, 3072], F32)
                row_bf = c.sb(Ss, "l3_rowbf", [1, 3072], BF16)
                with c.block() as block:
                    def sy(sync):
                        c.dma_push(sync, sg_sb[:, :], w["L3_attn_scale_gain"])
                    block.sync(sy)
                    dtot = c.val["dma"]

                    def sc(scalar):
                        scalar.wait_ge(c.sem("dma"), dtot)
                    block.scalar(sc)
                wqs_st = c.sb(Ss, "l3_wqsst", [11, 64], F32)
                wqs_bf = c.sb(Ss, "l3_wqsbf", [11, 64], BF16)
                with c.block() as block:
                    def sy(sync):
                        c.dma_push(sync, wqs_st[:, :], w["L3_attn_W_qscale"])
                    block.sync(sy)
                    dtot = c.val["dma"]

                    def sc(scalar):
                        scalar.wait_ge(c.sem("dma"), dtot)
                        scalar.copy(wqs_bf[:, :], wqs_st[:, :])
                    block.scalar(sc)
                with ExitStack() as Sp:
                    ps_w = c.ps(Sp, "l3_psqs", [64, 11], BF16)
                    with c.block() as block:
                        def tb(tensor):
                            tensor.transpose(ps_w[:, :], wqs_bf[:, :],
                                             c.k["c_ident_bf"][0:11, 0:11])
                        block.tensor(tb)
                    with c.block() as block:
                        def vb(vector):
                            vector.tensor_copy(wqsT[0:64, :], ps_w[:, :])
                        block.vector(vb)
                emit_row_load(c, w["L3_attn_identity_bypass"], 16, row_st)
                sp_row = c.sb(Ss, "l3_sprow", [1, 16], F32)
                with c.block() as block:
                    def vb(vector):
                        vector.scalar_tensor_tensor(sp_row[0:1, :],
                                                    row_st[0:1, 0:16], -1.0,
                                                    row_st[0:1, 0:16],
                                                    ALU.mult, ALU.max)
                    block.vector(vb)
                with c.block() as block:
                    def ab(scalar):
                        scalar.activation(sp_row[0:1, :], sp_row[0:1, :],
                                          AF.Exp, scale=-1.0)
                        scalar.activation(sp_row[0:1, :], sp_row[0:1, :],
                                          AF.Ln, bias=1.0)
                    block.scalar(ab)
                with c.block() as block:
                    def vb(vector):
                        vector.tensor_scalar_max(row_st[0:1, 16:32],
                                                 row_st[0:1, 0:16], 0.0)
                        vector.tensor_add(sp_row[0:1, :], sp_row[0:1, :],
                                          row_st[0:1, 16:32])
                    block.vector(vb)
                with ExitStack() as Sp:
                    ps_sp = c.ps(Sp, "l3_pssp", [4, 4], F32)
                    with c.block() as block:
                        def tb(tensor):
                            for g in range(4):
                                tensor.transpose(
                                    ps_sp[:, g:g + 1],
                                    sp_row[0:1, 4 * g:4 * g + 4],
                                    c.k["c_ident_f32"][0:1, 0:1])
                        block.tensor(tb)
                    with c.block() as block:
                        def vb(vector):
                            vector.tensor_copy(sp_cols[:, :], ps_sp[:, :])
                        block.vector(vb)

            wst4 = [c.sb(Sa, f"l3_wst{i}", [128, D], F32) for i in range(3)]
            wrow_bf2 = [c.sb(Sa, f"l3_wrbf{i}", [128, D], BF16)
                        for i in range(2)]
            wrowT2 = [c.sb(Sa, f"l3_wrT{i}", [128, D], BF16) for i in range(2)]
            bqkv_col = c.sb(Sa, "l3_bqc", [128, 24], F32)
            bg_col = c.sb(Sa, "l3_bgc", [128, 8], F32)
            sp_bc_sb = c.sb(Sa, "l3_spbc", [128, 8], F32)
            c.k["c_Abc"] = c.sb(Sa, "l3_cAbc", [128, len(OFFSETS) * 2 * 128],
                                BF16)
            with c.block() as block:
                def sy(sync):
                    c.dma_push(sync, c.k["c_Abc"][:], c.kd["c_Abc"])
                    sync.wait_ge(c.sem("dma"), c.val["dma"])
                block.sync(sy)
            wqkv4 = w["L3_attn_W_qkv"].rearrange("(o p) c -> p o c", p=128)
            wgate4 = w["L3_attn_W_gate"].rearrange("(o p) c -> p o c", p=128)

            TILES = []
            for g in range(4):
                for i in range(2):
                    rt = 2 * g + i
                    TILES += [(wqkv4, rt, "q", i), (wqkv4, 8 + rt, "k", i),
                              (wqkv4, 16 + rt, "v", i)]
                TILES += [(wgate4, 2 * g, "gate", 0),
                          (wgate4, 2 * g + 1, "gate", 1)]
            QUEUES3 = ["dxa", "dxb", "dxc", "dxd"]
            b10g, b11g, b12g, b13g, b8g = c.base("s10", "s11", "s12", "s13",
                                                 "s8")
            dsnap = {}

            def front_dma(sync, k):
                wsrc4, ot, kind, i = TILES[k]
                if k >= 3:
                    sync.wait_ge(c.sem("s10"), b10g + k - 2)
                c.dma_push(sync, wst4[k % 3][:].rearrange(
                    "p (o cw) -> p o cw", o=1), wsrc4[:, ot:ot + 1, :],
                    sem=QUEUES3[k % 4])
                dsnap[k] = c.val[QUEUES3[k % 4]]

            with c.block() as block:
                def sy(sync):
                    for k in range(3):
                        front_dma(sync, k)
                block.sync(sy)

                def gp(gpsimd):
                    for ot in range(24):
                        c.dma_push(gpsimd, bqkv_col[0:128, ot:ot + 1],
                                   w["L3_attn_b_qkv"][ot * 128:ot * 128 + 128]
                                   .rearrange("(p n) -> p n", n=1), sem="dma2")
                    for ot in range(8):
                        c.dma_push(gpsimd, bg_col[0:128, ot:ot + 1],
                                   w["L3_attn_b_gate"][ot * 128:ot * 128 + 128]
                                   .rearrange("(p n) -> p n", n=1), sem="dma2")
                    c.dma_push(gpsimd, wqsT[64:128, :], wqsT[0:64, :],
                               sem="dma2")
                    sph = sp_cols[:].tensor
                    for g4 in range(4):
                        for i in range(2):
                            src_ap = bass.AP(sph, 2 * i * 4 + g4,
                                             [[4, 2], [0, 64]])
                            c.dma_push(gpsimd,
                                       sp_bc_sb[0:128, 2 * g4 + i:
                                                2 * g4 + i + 1],
                                       src_ap, sem="dma2")
                    gpsimd.wait_ge(c.sem("dma2"), c.val["dma2"])
                block.gpsimd(gp)

            def emit_front_group(g, dsts):
                k0 = g * 8
                with ExitStack() as Sp:
                    ps_t = c.ps(Sp, "l3_pst", [128, D], BF16)
                    ps_p = [c.ps(Sp, f"l3_psp{i}", [128, 512], F32)
                            for i in range(2)]
                    with c.block() as block:
                        def sy(sync):
                            for k in range(k0 + 3, min(32, k0 + 11)):
                                front_dma(sync, k)
                        block.sync(sy)

                        def sc(scalar):
                            for k in range(k0, k0 + 8):
                                wsrc4, ot, kind, i = TILES[k]
                                scalar.wait_ge(c.sem(QUEUES3[k % 4]), dsnap[k])
                                if k >= 2:
                                    scalar.wait_ge(c.sem("s13"), b13g + k - 1)
                                scalar.copy(wrow_bf2[k % 2][:], wst4[k % 3][:])
                                scalar.sig("s10")
                                bias = bqkv_col[:, ot:ot + 1] if kind != "gate" \
                                    else bg_col[:, ot:ot + 1]
                                func = AF.Sigmoid if kind == "gate" \
                                    else AF.Identity
                                for n0 in range(4):
                                    scalar.wait_ge(c.sem("s11"),
                                                   b11g + 4 * k + n0 + 1)
                                    scalar.activation(
                                        dsts[k - k0][:, n0 * 512:n0 * 512 + 512],
                                        ps_p[(4 * k + n0) % 2][:, :], func,
                                        bias=bias)
                                    scalar.sig("s12")
                        block.scalar(sc)

                        def tb(tensor):
                            for k in range(k0, k0 + 8):
                                tensor.wait_ge(c.sem("s10"), b10g + k + 1)
                                if k > 0:
                                    tensor.wait_ge(c.sem("s8"), b8g + k)
                                for dc in range(ND):
                                    ins = tensor.transpose(
                                        ps_t[:, dc * 128:dc * 128 + 128],
                                        wrow_bf2[k % 2][:, dc * 128:dc * 128 + 128],
                                        c.k["c_ident_bf"][:, :])
                                ins.then_inc(c.sem("s13"), 1)
                                tensor.wait_ge(c.sem("s8"), b8g + k + 1)
                                for n0 in range(4):
                                    if 4 * k + n0 >= 2:
                                        tensor.wait_ge(c.sem("s12"),
                                                       b12g + 4 * k + n0 - 1)
                                    for dc in range(ND):
                                        ins = tensor.matmul(
                                            ps_p[(4 * k + n0) % 2][:, :],
                                            wrowT2[k % 2][:, dc * 128:dc * 128 + 128],
                                            xnT[:, dc * T + n0 * 512:
                                                dc * T + n0 * 512 + 512],
                                            start=dc == 0, stop=dc == ND - 1)
                                    ins.then_inc(c.sem("s11"), 1)
                        block.tensor(tb)

                        def vb(vector):
                            for k in range(k0, k0 + 8):
                                vector.wait_ge(c.sem("s13"), b13g + k + 1)
                                if k >= 2:
                                    vector.wait_ge(c.sem("s11"),
                                                   b11g + 4 * (k - 1))
                                vector.tensor_copy(wrowT2[k % 2][:], ps_t[:])
                                vector.sig("s8")
                        block.vector(vb)
                c.fin(s10=b10g + k0 + 8, s11=b11g + 4 * (k0 + 8),
                      s12=b12g + 4 * (k0 + 8), s13=b13g + k0 + 8,
                      s8=b8g + k0 + 8)


            qG = [c.sb(Sa, f"l3_q{i}", [128, T], BF16) for i in range(2)]
            kG = [c.sb(Sa, f"l3_k{i}", [128, T], BF16) for i in range(2)]
            vG = [c.sb(Sa, f"l3_v{i}", [128, T], BF16) for i in range(2)]
            gateG = [c.sb(Sa, f"l3_gate{i}", [128, T], BF16) for i in range(2)]
            eG = c.sb(Sa, "l3_eG", [128, T], BF16)
            invZ = c.sb(Sa, "l3_invZ", [4, T], BF16)
            prod = [c.sb(Sa, f"l3_prod{i}", [128, T], BF16) for i in range(2)]
            QKbf = c.sb(Sa, "l3_QKbf", [128, T], BF16)
            tmpm = [c.sb(Sa, f"l3_tmpm{i}", [128, T], BF16) for i in range(2)]
            accG = [c.sb(Sa, f"l3_acc{i}", [128, T], F32) for i in range(2)]

            for g in range(4):
                emit_front_group(g, [qG[0], kG[0], vG[0], qG[1], kG[1], vG[1],
                                     gateG[0], gateG[1]])
                with ExitStack() as Sp:
                    ps_g = c.ps(Sp, "l3_psg", [128, T], F32)
                    bQO, bEX, bLZ, bMS = c.base("s0", "s1", "s2", "s3")
                    with c.block() as block:
                        def tbg(tensor, g=g):
                            for hl in range(4):
                                i, sub = hl // 2, (hl % 2) * 64
                                if hl > 0:
                                    tensor.wait_ge(c.sem("s1"), bEX + hl)
                                for n0 in range(0, T, 512):
                                    ins = tensor.matmul(
                                        ps_g[0:11, n0:n0 + 512],
                                        wqsT[sub:sub + 64, :],
                                        qG[i][sub:sub + 64, n0:n0 + 512],
                                        start=True, stop=True)
                                ins.then_inc(c.sem("s0"), 1)
                            tensor.wait_ge(c.sem("s1"), bEX + 4)
                            for n0 in range(0, T, 512):
                                ins = tensor.matmul(ps_g[0:4, n0:n0 + 512],
                                                    c.k["c_Z"][:, :],
                                                    eG[:, n0:n0 + 512],
                                                    start=True, stop=True)
                            ins.then_inc(c.sem("s0"), 1)
                            tensor.wait_ge(c.sem("s2"), bLZ + 1)
                            for n0 in range(0, T, 512):
                                ins = tensor.matmul(ps_g[:, n0:n0 + 512],
                                                    c.k["c_Zbc"][:, :],
                                                    invZ[0:4, n0:n0 + 512],
                                                    start=True, stop=True)
                            ins.then_inc(c.sem("s0"), 1)
                        block.tensor(tbg)

                        def abg(scalar, g=g):
                            for hl in range(4):
                                h = 4 * g + hl
                                scalar.wait_ge(c.sem("s0"), bQO + hl + 1)
                                if hl == 0:
                                    scalar.wait_ge(c.sem("s3"), bMS + 1)
                                scalar.activation(
                                    eG[hl * 32: hl * 32 + 11, :],
                                    ps_g[0:11, :], AF.Exp,
                                    bias=sg_sb[:, h:h + 1])
                                scalar.sig("s1")
                            scalar.wait_ge(c.sem("s0"), bQO + 5)
                            scalar.activation(invZ[:, :], ps_g[0:4, :], AF.Ln)
                            scalar.activation(invZ[:, :], invZ[:, :], AF.Exp,
                                              scale=-1.0)
                            scalar.sig("s2")
                        block.scalar(abg)

                        def vbg(vector, g=g):
                            vector.memset(eG[:, :], 0.0)
                            vector.sig("s3")
                            vector.wait_ge(c.sem("s0"), bQO + 6)
                            vector.tensor_mul(eG[:, :], eG[:, :], ps_g[:, :])
                        block.vector(vbg)
                    c.fin(s0=bQO + 6, s1=bEX + 4, s2=bLZ + 1, s3=bMS + 1)

                with ExitStack() as Sp:
                    ps_QK = c.ps(Sp, "l3_psQK", [128, T], F32)
                    ps_A = c.ps(Sp, "l3_psA", [128, T], F32)
                    NOo = len(OFFSETS)
                    bP0, bQK, bE3, bC, bM, bP, bAm, bQE = c.base(
                        "s0", "s1", "s2", "s4", "s6", "s7", "s3", "s5")
                    with c.block() as block:
                        def emit_prods(vector, oi):
                            o = OFFSETS[oi]
                            if oi > 0:
                                vector.wait_ge(c.sem("s1"), bQK + 2 * oi)
                            for i in range(2):
                                if o > 0:
                                    vector.memset(prod[i][:, 0:o], 0.0)
                                vector.tensor_mul(prod[i][:, o:T],
                                                  qG[i][:, o:T],
                                                  kG[i][:, 0:T - o])
                            vector.sig("s0")

                        def vb(vector, g=g):
                            emit_prods(vector, 0)
                            for oi, o in enumerate(OFFSETS):
                                for i in range(2):
                                    k = 2 * oi + i
                                    if i == 1 and oi + 1 < len(OFFSETS):
                                        emit_prods(vector, oi + 1)
                                    vector.wait_ge(c.sem("s2"), bE3 + k + 1)
                                    vector.wait_ge(c.sem("s5"), bQE + k + 1)
                                    vector.tensor_mul(tmpm[i][:, :],
                                                      tmpm[i][:, :],
                                                      QKbf[:, :])
                                    vector.sig("s4")
                                    if o == 0:
                                        vector.tensor_mul(tmpm[i][:, :],
                                                          tmpm[i][:, :],
                                                          vG[i][:, :])
                                    else:
                                        vector.tensor_mul(tmpm[i][:, o:T],
                                                          tmpm[i][:, o:T],
                                                          vG[i][:, 0:T - o])
                                    vector.sig("s6")
                        block.vector(vb)

                        def ab(scalar, g=g):
                            for oi, o in enumerate(OFFSETS):
                                for i in range(2):
                                    k = 2 * oi + i
                                    scalar.wait_ge(c.sem("s1"), bQK + k + 1)
                                    if k > 0:
                                        scalar.wait_ge(c.sem("s4"), bC + k)
                                    scalar.copy(QKbf[:, :], ps_QK[:, :])
                                    scalar.sig("s5")
                                    scalar.wait_ge(c.sem("s3"), bAm + k + 1)
                                    if k >= 2:
                                        scalar.wait_ge(c.sem("s7"),
                                                       bP + k - 1)
                                    if o == 0:
                                        scalar.activation(
                                            tmpm[i][:, :], ps_A[:, :],
                                            AF.Identity,
                                            bias=sp_bc_sb[:, 2 * g + i:
                                                          2 * g + i + 1])
                                    else:
                                        scalar.copy(tmpm[i][:, :],
                                                    ps_A[:, :])
                                    scalar.sig("s2")
                        block.scalar(ab)

                        def tb(tensor, g=g):
                            for oi, o in enumerate(OFFSETS):
                                for i in range(2):
                                    k = 2 * oi + i
                                    if k > 0:
                                        tensor.wait_ge(c.sem("s2"), bE3 + k)
                                    for n0 in range(0, T, 512):
                                        ins = tensor.matmul(
                                            ps_A[:, n0:n0 + 512],
                                            c.k["c_Abc"][:, (oi * 2 + i) * 128:
                                                         (oi * 2 + i) * 128 + 128],
                                            eG[:, n0:n0 + 512],
                                            start=True, stop=True)
                                    ins.then_inc(c.sem("s3"), 1)
                                    if i == 0:
                                        tensor.wait_ge(c.sem("s0"),
                                                       bP0 + oi + 1)
                                    if k > 0:
                                        tensor.wait_ge(c.sem("s5"), bQE + k)
                                    for n0 in range(0, T, 512):
                                        ins = tensor.matmul(
                                            ps_QK[:, n0:n0 + 512],
                                            c.k["c_selbc"][:, i * 128:
                                                           i * 128 + 128],
                                            prod[i][:, n0:n0 + 512],
                                            start=True, stop=True)
                                    ins.then_inc(c.sem("s1"), 1)
                        block.tensor(tb)

                        def gy(gpsimd, g=g):
                            for oi, o in enumerate(OFFSETS):
                                for i in range(2):
                                    k = 2 * oi + i
                                    gpsimd.wait_ge(c.sem("s6"), bM + k + 1)
                                    if oi == 0:
                                        gpsimd.tensor_copy(accG[i][:, :],
                                                           tmpm[i][:, :])
                                    else:
                                        gpsimd.tensor_add(accG[i][:, o:T],
                                                          accG[i][:, o:T],
                                                          tmpm[i][:, o:T])
                                    gpsimd.sig("s7")
                        block.gpsimd(gy)
                    c.fin(s0=bP0 + NOo, s1=bQK + 2 * NOo, s2=bE3 + 2 * NOo,
                          s4=bC + 2 * NOo, s6=bM + 2 * NOo, s7=bP + 2 * NOo,
                          s3=bAm + 2 * NOo, s5=bQE + 2 * NOo)

                with c.block() as block:
                    def vb(vector, g=g):
                        for i in range(2):
                            vector.tensor_mul(ggT[2 * g + i][:, :], accG[i][:, :],
                                              gateG[i][:, :])
                    block.vector(vb)

        woutT = c.sb(L, "l3_woutT", [128, ND * D], BF16)
        bout_bf = c.sb(L, "l3_bout", [1, D], BF16)
        with ExitStack() as Sw:
            row_st = c.sb(Sw, "l3_rowst2", [1, D], F32)
            emit_row_load(c, w["L3_attn_b_out"], D, row_st, bout_bf)
            emit_load_T_chunked(c, w["L3_attn_W_out"], D, D, woutT, "l3wo")

        def s_psum(tensor, t, psum_ap):
            for nf in range(2):
                for rt in range(8):
                    tensor.matmul(
                        psum_ap[:, nf * 512: nf * 512 + 512],
                        ggT[rt][:, t * 128: t * 128 + 128],
                        woutT[:, rt * D + nf * 512: rt * D + nf * 512 + 512],
                        start=rt == 0, stop=False)
                ins = tensor.matmul(
                    psum_ap[:, nf * 512: nf * 512 + 512],
                    c.k["c_ones_bf"][0:1, 0:128],
                    bout_bf[0:1, nf * 512: nf * 512 + 512],
                    start=False, stop=True)
            return ins

        emit_combine(c, xin, xout, s_psum, g_all)


L0_WEIGHTS = {
    "L0_attn_Wp": [D, 40], "L0_attn_bp": [40], "L0_attn_Wv": [D, D],
    "L0_attn_Wo": [D, D], "L0_ffn_weight": [D, D], "L0_ffn_bias": [D],
}
L1_WEIGHTS = {
    "L1_attn_Wkqv": [16, D, 2112], "L1_attn_bkqv": [16, 2112],
    "L1_attn_Wp": [D, D], "L1_attn_bp": [D],
    "L1_ffn_weight": [D, D], "L1_ffn_gate_w": [8, D],
    "L1_ffn_lora_U": [8, 16, D], "L1_ffn_lora_V": [8, D, 16],
}


L2_WEIGHTS = {
    "L2_attn_Wq": [D, 512], "L2_attn_Wkv": [D, 128], "L2_attn_Wo": [512, D],
    "L2_attn_norm_w": [D], "L2_attn_norm_b": [D],
    "L2_attn_out_norm_w": [D], "L2_attn_out_norm_b": [D],
    "L2_ffn_W_r": [D, D], "L2_ffn_W_k": [D, D], "L2_ffn_W_v": [D, D],
    "L2_ffn_mu_r": [D], "L2_ffn_mu_k": [D],
}
L3_WEIGHTS = {
    "L3_attn_W_qkv": [3072, D], "L3_attn_b_qkv": [3072],
    "L3_attn_W_out": [D, D], "L3_attn_b_out": [D],
    "L3_attn_W_gate": [D, D], "L3_attn_b_gate": [D],
    "L3_attn_scale_gain": [11, 16], "L3_attn_W_qscale": [11, 64],
    "L3_attn_identity_bypass": [16],
    "L3_ffn_w_gate": ([3072, D], I32), "L3_ffn_w_up": ([3072, D], I32),
    "L3_ffn_w_down": ([D, 3072], I32),
}

LAYER_WEIGHTS = {0: list(L0_WEIGHTS), 1: list(L1_WEIGHTS), 2: list(L2_WEIGHTS),
                 3: list(L3_WEIGHTS)}


def extra_host_inputs(lid):
    if lid != 3:
        return {}
    allc = l3_host_constants()
    names = {n for n, _, _ in L3_CONSTS}
    return {k: v for k, v in allc.items() if k in names}


def build_layer_program(layer_ids):
    nc = bass.Bass("TRN2", target_bir_lowering=False, debug=False)
    with ExitStack() as stack:
        c = C(nc, stack)
        xin = nc.dram_tensor("x", [T, D], F32, kind="ExternalInput").ap()
        xout = nc.dram_tensor("x_out", [T, D], F32, kind="ExternalOutput").ap()
        declare_consts(c, extra=L3_CONSTS if 3 in layer_ids else ())
        emitters = {0: (emit_layer0, L0_WEIGHTS), 1: (emit_layer1, L1_WEIGHTS),
                    2: (emit_layer2, L2_WEIGHTS), 3: (emit_layer3, L3_WEIGHTS)}
        fuse_layers = len(layer_ids) > 1
        if fuse_layers:
            xnT_hand = c.sb(stack, "xnT_hand", [128, ND * T], BF16)
            lnst_hand = c.sb(stack, "lnst_hand", [128, 2 * NT], F32)
        cur_in = xin
        fused_prev = False
        for i, lid in enumerate(layer_ids):
            emit, wshapes = emitters[lid]
            w = {}
            for k, sh in wshapes.items():
                sh, dt = sh if isinstance(sh, tuple) else (sh, F32)
                w[k] = nc.dram_tensor(k, sh, dt, kind="ExternalInput").ap()
            out_ap = xout if i == len(layer_ids) - 1 else \
                nc.dram_tensor(f"x_mid{i}", [T, D], F32, kind="Internal").ap()
            nxt = layer_ids[i + 1] if i + 1 < len(layer_ids) else None
            kw = {}
            if lid in (1, 2, 3) and fused_prev:
                kw["xnT_in"] = xnT_hand
                if lid == 2:
                    kw["lnstats_in"] = lnst_hand
            if lid in (0, 1, 2) and nxt in (1, 2, 3):
                kw["fuse_next"] = (xnT_hand,
                                   lnst_hand if nxt == 2 else None)
                fused_prev = True
            else:
                fused_prev = False
            emit(c, cur_in, out_ap, w, **kw)
            cur_in = out_ap
    return nc


def kernel(**inputs):
    consts = host_constants()
    consts.update(l3_host_constants())
    x = np.ascontiguousarray(inputs["L0_attn_x"], dtype=np.float32)
    nc = build_layer_program([0, 1, 2, 3])
    wnames = [k for lid in range(4) for k in LAYER_WEIGHTS[lid]]
    in_maps = []
    for b in range(8):
        m = {"x": x[b]}
        for k in wnames:
            m[k] = np.ascontiguousarray(inputs[k])
        m.update(consts)
        in_maps.append(m)
    res = run_bass_kernel_spmd(nc, in_maps, core_ids=list(range(8)))
    return np.stack([r["x_out"] for r in res.results], axis=0)
